# version 13
# speedup vs baseline: 1.3160x; 1.0561x over previous
import numpy as np
import ml_dtypes
from contextlib import ExitStack

import concourse.bass as bass
import concourse.mybir as mybir
from concourse.bass_utils import run_bass_kernel_spmd

F32 = mybir.dt.float32
BF16 = mybir.dt.bfloat16
AF = mybir.ActivationFunctionType
ALU = mybir.AluOpType
AX = mybir.AxisListType

T, D, FT = 4096, 1024, 8
COLS = 1 + T
C = 64
SLA = 512
NSA = T // SLA
SLB = 256
NSB = T // SLB
CPS = SLB // C
NPAIR = CPS // 2
CHW = NPAIR * 64
H2, DH = 8, 64
WIN = 512
NW = T // WIN
HID2 = 2816
NS1 = HID2 // 128
TB = 512
NTB = T // TB
LEPS = 1e-5
NEG = -1e30

DEV_STOP = None
DEV_SKIP_HEADS = False
DEV_SKIP_V = False
DEV_DNB = False
DEV_SKIP_K = False
DEV_NW = None


def _host_consts():
    c = {}
    c["ident"] = np.eye(128, dtype=np.float32)
    c["ones1"] = np.ones((1, 128), dtype=np.float32)
    c["onesh"] = np.ones((128, 128), dtype=np.float32)
    c["sumw1024"] = np.full((128, 128), 1.0 / 1024.0, dtype=np.float32)
    c["sumw256"] = np.full((128, 128), 1.0 / 256.0, dtype=np.float32)
    blk = np.zeros((128, 128), dtype=np.float32)
    blk[:64, :64] = 1.0
    blk[64:, 64:] = 1.0
    c["blk64"] = blk
    tri = np.tril(np.ones((C, C), dtype=np.float32), -1)
    c["maskL"] = np.concatenate([tri, tri], 0)
    c["maskU"] = np.concatenate([tri.T, tri.T], 0)
    ey = np.eye(C, dtype=np.float32)
    c["eye1"] = np.concatenate([ey, ey], 0)
    c["eye2"] = 2.0 * np.concatenate([ey, ey], 0)
    prot = np.zeros((128, 128), dtype=np.float32)
    for b in range(2):
        b0 = b * 64
        for m in range(32):
            prot[b0 + m + 32, b0 + m] = -1.0
        for m in range(32, 64):
            prot[b0 + m - 32, b0 + m] = 1.0
    c["prot"] = prot
    c["epsv"] = np.full((128, 8), LEPS, dtype=np.float32)
    mqk = np.full((128, 4, 1024), NEG, dtype=np.float32)
    p_ = np.arange(128)[:, None]
    col = np.arange(1024)[None, :]
    for qb in range(4):
        qi = qb * 128 + p_
        mqk[:, qb, :] = np.where((col >= qi) & (col <= qi + WIN), 0.0, NEG)
    c["mqk"] = np.ascontiguousarray(mqk.reshape(128, 4096)).astype(
        ml_dtypes.bfloat16)
    mt = np.full((128, 8, 512), NEG, dtype=np.float32)
    q_ = np.arange(512)[None, :]
    for kb in range(8):
        cv = kb * 128 + p_
        mt[:, kb, :] = np.where((q_ <= cv) & (cv <= q_ + WIN), 0.0, NEG)
    c["mt"] = np.ascontiguousarray(mt.reshape(128, 4096)).astype(
        ml_dtypes.bfloat16)
    sel4 = np.zeros((4, 512), np.float32)
    for h in range(4):
        sel4[h, h * 128:(h + 1) * 128] = 1.0
    c["sel4"] = sel4
    half = np.arange(0, DH, 2, dtype=np.float64)
    inv_freq = 1.0 / 10000.0 ** (half / DH)
    tt = np.arange(T, dtype=np.float64)
    fr = tt[:, None] * inv_freq[None, :]
    fr = np.concatenate([fr, fr], -1)
    base = (half + 0.4 * DH) / (1.4 * DH)
    power = (tt - T // 2) / float(WIN // 2)
    sc = base[None, :] ** power[:, None]
    sc = np.concatenate([sc, sc], -1)
    cs_q = (np.cos(fr) * sc).T.astype(np.float32)
    ss_q = (np.sin(fr) * sc).T.astype(np.float32)
    cs_k = (np.cos(fr) / sc).T.astype(np.float32)
    ss_k = (np.sin(fr) / sc).T.astype(np.float32)
    c["csq"] = np.ascontiguousarray(8.0 * np.concatenate([cs_q, cs_q], 0))
    c["ssq"] = np.ascontiguousarray(8.0 * np.concatenate([ss_q, ss_q], 0))
    c["csk"] = np.ascontiguousarray(np.concatenate([cs_k, cs_k], 0))
    c["ssk"] = np.ascontiguousarray(np.concatenate([ss_k, ss_k], 0))
    return c


def _prep_weights(inp):
    w = {}
    w["wq"] = inp["L0_attn_Wq"]
    w["wk"] = inp["L0_attn_Wk"]
    w["wv"] = inp["L0_attn_Wv"]
    w["wod"] = inp["L0_attn_Wo"]
    wbe = inp["L0_attn_Wb"].reshape(FT, 128, 4).transpose(1, 0, 2)
    w["wb"] = np.ascontiguousarray(np.repeat(wbe, 32, axis=2))
    for nm, key in (("cvq", "L0_attn_conv_q"), ("cvk", "L0_attn_conv_k"),
                    ("cvv", "L0_attn_conv_v")):
        cw = inp[key][:, 0, :]
        w[nm] = np.ascontiguousarray(cw.reshape(FT, 128, 4).transpose(1, 0, 2))
    w["g"] = np.ascontiguousarray(inp["L0_attn_g"].reshape(2, 128).T)
    w["fwqkv"] = inp["L0_ffn_wqkv"]
    w["fwo"] = inp["L0_ffn_wo"]
    qs = inp["L0_ffn_q_scale"]
    ks = inp["L0_ffn_k_scale"]
    rot = np.concatenate([np.arange(32, 64), np.arange(0, 32)])
    w["fqs"] = np.ascontiguousarray(np.concatenate([qs, qs], 1).T)
    w["fqsr"] = np.ascontiguousarray(np.concatenate([qs[:, rot], qs[:, rot]], 1).T)
    w["fks"] = np.ascontiguousarray(np.concatenate([ks, ks], 1).T)
    w["fksr"] = np.ascontiguousarray(np.concatenate([ks[:, rot], ks[:, rot]], 1).T)
    w["lng"] = np.ascontiguousarray(
        inp["L0_ffn_ln_g"].reshape(2, FT, 128).transpose(2, 0, 1).reshape(128, 16))
    w["lnb"] = np.ascontiguousarray(
        inp["L0_ffn_ln_b"].reshape(2, FT, 128).transpose(2, 0, 1).reshape(128, 16))
    w["b2"] = np.ascontiguousarray(
        inp["L0_ffn_b2"].reshape(2, FT, 128).transpose(2, 0, 1).reshape(128, 16))
    w1 = inp["L0_ffn_w1"]
    b1 = inp["L0_ffn_b1"]
    w1a = np.zeros((2, 1024, HID2), np.float32)
    w1g = np.zeros((2, 1024, HID2), np.float32)
    w1a[:, :, :2730] = w1[:, :, :2730]
    w1g[:, :, :2730] = w1[:, :, 2730:]
    w["w1a"] = w1a
    w["w1g"] = w1g
    b1a = np.zeros((2, HID2), np.float32)
    b1g = np.zeros((2, HID2), np.float32)
    b1a[:, :2730] = b1[:, :2730]
    b1g[:, :2730] = b1[:, 2730:]
    w["b1a"] = np.ascontiguousarray(
        b1a.reshape(2, NS1, 128).transpose(2, 0, 1).reshape(128, 2 * NS1))
    w["b1g"] = np.ascontiguousarray(
        b1g.reshape(2, NS1, 128).transpose(2, 0, 1).reshape(128, 2 * NS1))
    w2 = inp["L0_ffn_w2"]
    w2p = np.zeros((2, HID2, 1024), np.float32)
    w2p[:, :2730, :] = w2
    w["w2p"] = w2p
    return w


class P:
    def __init__(self):
        self.nc = bass.Bass("TRN2", target_bir_lowering=False,
                            detect_race_conditions=False)
        self.es = ExitStack()
        self.sems = {}
        self.cnt = {}

    def sem(self, name):
        h = self.es.enter_context(self.nc.semaphore(name))
        self.sems[name] = h
        self.cnt[name] = 0
        return h

    def bump(self, name, n=1):
        self.cnt[name] += n
        return self.cnt[name]

    def w(self, eng, name, val=None):
        eng.wait_ge(self.sems[name], self.cnt[name] if val is None else val)

    def sb(self, name, shape, dtype):
        t = self.es.enter_context(self.nc.sbuf_tensor(name, shape, dtype))
        setattr(self, name, t)
        return t

    def ps(self, name, shape, dtype):
        t = self.es.enter_context(self.nc.psum_tensor(name, shape, dtype))
        setattr(self, name, t)
        return t


_NOTRACK = {"wait_ge", "nop", "engine_nop", "sem_inc", "wait_op", "drain"}


class TRK:

    def __init__(self, eng):
        self._e = eng
        self.last = None

    def __getattr__(self, nm):
        f = getattr(self._e, nm)
        if not callable(f):
            return f

        def wrap(*a, **k):
            r = f(*a, **k)
            if nm not in _NOTRACK and hasattr(r, "then_inc"):
                self.last = r
            return r
        return wrap


class BlockProxy:
    def __init__(self, block):
        self._b = block

    def _mk(self, sec):
        def deco(fn):
            return sec(lambda eng: fn(TRK(eng)))
        return deco

    @property
    def tensor(self):
        return self._mk(self._b.tensor)

    @property
    def vector(self):
        return self._mk(self._b.vector)

    @property
    def scalar(self):
        return self._mk(self._b.scalar)

    @property
    def gpsimd(self):
        return self._mk(self._b.gpsimd)

    @property
    def sync(self):
        return self._mk(self._b.sync)


def fence(p, block):
    c = {nm: p.cnt[nm] for nm in ("sPE", "sVE", "sAE")}

    @block.tensor
    def _(t):
        t.wait_ge(p.sems["sVE"], c["sVE"])
        t.wait_ge(p.sems["sAE"], c["sAE"])

    @block.vector
    def _(v):
        v.wait_ge(p.sems["sPE"], c["sPE"])
        v.wait_ge(p.sems["sAE"], c["sAE"])

    @block.scalar
    def _(s):
        s.wait_ge(p.sems["sPE"], c["sPE"])
        s.wait_ge(p.sems["sVE"], c["sVE"])


def pipe(p, block, n, mm_emit, evac_emit, evac_engine="scalar", depth=2,
         pre_wait=(), pre_counts=()):
    pe0 = p.cnt["sPE"]
    ce_name = {"scalar": "sAE", "vector": "sVE"}[evac_engine]
    ce0 = p.cnt[ce_name]
    pw = [(nm, p.cnt[nm]) for nm in pre_wait] + list(pre_counts)

    @block.tensor
    def _(t):
        for nm, val in pw:
            t.wait_ge(p.sems[nm], val)
        for i in range(n):
            if i >= depth:
                t.wait_ge(p.sems[ce_name], ce0 + i - depth + 1)
            mm_emit(t, i)
            t.last.then_inc(p.sems["sPE"], 1)
    p.cnt["sPE"] += n

    sec = block.scalar if evac_engine == "scalar" else block.vector

    def body(e):
        for i in range(n):
            e.wait_ge(p.sems["sPE"], pe0 + i + 1)
            last = evac_emit(e, i)
            last.then_inc(p.sems[ce_name], 1)
    sec(body)
    p.cnt[ce_name] += n


def build_program():
    p = P()
    nc = p.nc
    din = {}

    def DI(name, shape):
        din[name] = nc.dram_tensor(name, list(shape), F32,
                                   kind="ExternalInput").ap()

    DI("x_in", (D, T))
    DI("wq", (D, D)); DI("wk", (D, D)); DI("wv", (D, D)); DI("wod", (D, D))
    DI("wb", (128, FT, 128))
    DI("cvq", (128, FT, 4)); DI("cvk", (128, FT, 4)); DI("cvv", (128, FT, 4))
    DI("g", (128, 2))
    DI("fwqkv", (2, D, 1536)); DI("fwo", (2, 512, D))
    DI("fqs", (128, 2)); DI("fqsr", (128, 2))
    DI("fks", (128, 2)); DI("fksr", (128, 2))
    DI("lng", (128, 2 * FT)); DI("lnb", (128, 2 * FT)); DI("b2", (128, 2 * FT))
    DI("w1a", (2, D, HID2)); DI("w1g", (2, D, HID2))
    DI("b1a", (128, 2 * NS1)); DI("b1g", (128, 2 * NS1))
    DI("w2p", (2, HID2, D))
    for nm, shp in (("ident", (128, 128)), ("ones1", (1, 128)),
                    ("onesh", (128, 128)), ("sumw1024", (128, 128)),
                    ("sumw256", (128, 128)), ("blk64", (128, 128)),
                    ("maskL", (128, 64)), ("maskU", (128, 64)),
                    ("eye1", (128, 64)), ("eye2", (128, 64)),
                    ("prot", (128, 128)), ("epsv", (128, 8)),
                    ("csq", (128, T)), ("ssq", (128, T)),
                    ("csk", (128, T)), ("ssk", (128, T)), ("dmask", (128, 64))):
        DI(nm, shp)
    din["sel4"] = nc.dram_tensor("sel4", [4, 512], F32,
                                 kind="ExternalInput").ap()
    for nm in ("mqk", "mt"):
        din[nm] = nc.dram_tensor(nm, [128, 4096], BF16,
                                 kind="ExternalInput").ap()

    scr = {}
    for nm, shape, dt in (("qn_s", [128, FT, T], BF16),
                          ("kn_s", [128, FT, T], BF16),
                          ("v_s", [128, FT, T], BF16),
                          ("x1_s", [128, FT, T], BF16),
                          ("w1a_bf", [2, D, HID2], BF16),
                          ("w1g_bf", [2, D, HID2], BF16),
                          ("betar_s", [4, T], F32),
                          ("dbg", [128, 16384], BF16)):
        scr[nm] = nc.dram_tensor(nm, shape, dt, kind="Internal").ap()
    out_d = nc.dram_tensor("out_d", [D, T], F32, kind="ExternalOutput").ap()

    p.sb("stream", [128, FT * COLS], BF16)
    for nm, shp, dt in (("c_id_bf", [128, 128], BF16),
                        ("c_id_f", [128, 128], F32),
                        ("c_ones1", [1, 128], BF16),
                        ("c_ones", [128, 128], BF16),
                        ("c_sw1024", [128, 128], BF16),
                        ("c_sw256", [128, 128], BF16),
                        ("c_blk64", [128, 128], BF16),
                        ("c_maskL", [128, 64], F32),
                        ("c_maskU", [128, 64], F32),
                        ("c_eye1", [128, 64], F32),
                        ("c_eye2", [128, 64], F32),
                        ("c_prot", [128, 128], BF16),
                        ("c_epsv", [128, 8], F32),
                        ("c_sel4", [4, 512], F32),
                        ("c_dmask", [128, 64], F32),
                        ("sq", [128, 4096], BF16),
                        ("gs1", [128, 512], F32), ("gs2", [128, 512], F32),
                        ("gs3", [128, 512], F32), ("gs4", [128, 512], F32),

                        ("sdt", [128, FT * 5], F32),
                        ("halo", [128, 3 * FT * 3], F32),
                        ("g_pp", [128, 2], F32),
                        ("cv_pp", [128, 3 * FT * 4], F32),
                        ("wb_bf", [128, FT * 128], BF16),
                        ("lng_pp", [128, 2 * FT], F32),
                        ("lnb_pp", [128, 2 * FT], F32),
                        ("lnb_bf", [128, 2 * FT], BF16),
                        ("b2_pp", [128, 2 * FT], F32),
                        ("qs_pp", [128, 2], F32), ("qsr_pp", [128, 2], F32),
                        ("ks_pp", [128, 2], F32), ("ksr_pp", [128, 2], F32),
                        ("b1ae", [128, NS1], F32), ("b1ge", [128, NS1], F32),
                        ("b1ac", [128, 2 * NS1], F32),
                        ("b1gc", [128, 2 * NS1], F32),
                        ("rowtmp", [1, 128], BF16),
                        ("dumm", [1, 8], F32)):
        p.sb(nm, shp, dt)
    p.ps("pw0", [128, 1024], F32)
    p.ps("pw1", [128, 1024], F32)
    p.ps("pp0", [128, 512], F32)
    p.ps("pp1", [128, 512], F32)
    p.ps("pp2", [128, 512], F32)
    p.ps("pp3", [128, 512], F32)
    for s in ("sLD", "sLG", "sWT", "sSP", "sST", "sPE", "sVE", "sAE", "sPO", "sDB"):
        p.sem(s)
    nc.const_aps.register(F32, LEPS, p.c_epsv[:, 0:1])

    def stile(ft, t0, n):
        base = ft * COLS + 1 + t0
        return p.stream[:, base:base + n]

    with nc.Block() as block:
        emit_all(p, BlockProxy(block), din, scr, out_d, stile)
    p.es.close()
    return p


def ln_stats_sb(p, block, rhs_tiles, n, mu_sb, rstd_sb, psA, psB,
               sq_off=0, eps=LEPS):
    L = p.sems

    @block.gpsimd
    def _(g):
        p.w(g, "sPE")
        p.w(g, "sVE")
        for ft in range(FT):
            g.tensor_mul(p.sq[:, sq_off + ft * n:sq_off + (ft + 1) * n],
                         rhs_tiles(ft), rhs_tiles(ft))
        g.last.then_inc(L["sPO"], 1)
    p.bump("sPO")

    @block.tensor
    def _(t):
        p.w(t, "sPO")
        p.w(t, "sVE")
        p.w(t, "sAE")
        for ft in range(FT):
            t.matmul(psA[:, 0:n], p.c_sw1024[:], rhs_tiles(ft),
                     start=(ft == 0), stop=(ft == FT - 1))
        for ft in range(FT):
            t.matmul(psB[:, 0:n], p.c_sw1024[:],
                     p.sq[:, sq_off + ft * n:sq_off + (ft + 1) * n],
                     start=(ft == 0), stop=(ft == FT - 1))
        t.last.then_inc(L["sPE"], 1)
    p.bump("sPE")

    @block.scalar
    def _(s):
        p.w(s, "sPE")
        s.copy(mu_sb, psA[:, 0:n])
        s.last.then_inc(L["sAE"], 1)
    p.bump("sAE")

    @block.gpsimd
    def _(g):
        p.w(g, "sAE")
        g.tensor_mul(rstd_sb, mu_sb, mu_sb)
        g.last.then_inc(L["sPO"], 1)
    p.bump("sPO")

    @block.vector
    def _(v):
        p.w(v, "sPO")
        v.tensor_sub(rstd_sb, psB[:, 0:n], rstd_sb)
        v.last.then_inc(L["sVE"], 1)
    p.bump("sVE")

    @block.scalar
    def _(s):
        p.w(s, "sVE")
        s.activation(rstd_sb, rstd_sb, AF.Ln, bias=eps)
        s.activation(rstd_sb, rstd_sb, AF.Exp, scale=-0.5)
        s.last.then_inc(L["sAE"], 1)
    p.bump("sAE")


def ln_stats(p, block, rhs_tiles, n, out_rstd, eps=LEPS):
    L = p.sems

    @block.gpsimd
    def _(g):
        p.w(g, "sPE")
        p.w(g, "sVE")
        for ft in range(FT):
            g.tensor_mul(p.sq[:, ft * n:(ft + 1) * n], rhs_tiles(ft),
                         rhs_tiles(ft))
        g.last.then_inc(L["sPO"], 1)
    p.bump("sPO")

    @block.tensor
    def _(t):
        p.w(t, "sPO")
        p.w(t, "sVE")
        p.w(t, "sAE")
        for ft in range(FT):
            t.matmul(p.pp0[:, 0:n], p.c_sw1024[:], rhs_tiles(ft),
                     start=(ft == 0), stop=(ft == FT - 1))
        for ft in range(FT):
            t.matmul(p.pp1[:, 0:n], p.c_sw1024[:], p.sq[:, ft * n:(ft + 1) * n],
                     start=(ft == 0), stop=(ft == FT - 1))
        t.last.then_inc(L["sPE"], 1)
    p.bump("sPE")

    @block.scalar
    def _(s):
        p.w(s, "sPE")
        s.copy(p.gs2[:, 0:n], p.pp0[:, 0:n])
        s.last.then_inc(L["sAE"], 1)
    p.bump("sAE")

    @block.gpsimd
    def _(g):
        p.w(g, "sAE")
        g.tensor_mul(p.gs1[:, 0:n], p.gs2[:, 0:n], p.gs2[:, 0:n])
        g.last.then_inc(L["sPO"], 1)
    p.bump("sPO")

    @block.vector
    def _(v):
        p.w(v, "sPO")
        v.tensor_sub(p.gs1[:, 0:n], p.pp1[:, 0:n], p.gs1[:, 0:n])
        v.last.then_inc(L["sVE"], 1)
    p.bump("sVE")

    @block.scalar
    def _(s):
        p.w(s, "sVE")
        s.activation(p.gs1[:, 0:n], p.gs1[:, 0:n], AF.Ln, bias=eps)
        s.activation(out_rstd, p.gs1[:, 0:n], AF.Exp, scale=-0.5)
        s.last.then_inc(L["sAE"], 1)
    p.bump("sAE")


def emit_all(p, block, din, scr, out_d, stile):
    L = p.sems

    @block.gpsimd
    def _(g):
        for dst, nm in ((p.c_id_bf, "ident"), (p.c_ones1, "ones1"),
                        (p.c_ones, "onesh"), (p.c_sw1024, "sumw1024"),
                        (p.c_sw256, "sumw256"), (p.c_blk64, "blk64"),
                        (p.c_prot, "prot")):
            g.dma_start(out=dst[:], in_=din[nm]).then_inc(L["sLG"], 16)
            p.cnt["sLG"] += 16
        g.dma_start(out=p.wb_bf[:].rearrange("p (ft m) -> p ft m", ft=FT),
                    in_=din["wb"]).then_inc(L["sLG"], 16)
        g.dma_start(out=p.lnb_bf[:], in_=din["lnb"]).then_inc(L["sLG"], 16)
        g.dma_start(out=p.stream[:].rearrange("p (ft c) -> p ft c", ft=FT)
                    [:, :, 1:1 + T],
                    in_=din["x_in"].rearrange("(ft p) t -> p ft t", p=128)
                    ).then_inc(L["sLG"], 16)
        p.cnt["sLG"] += 48

    @block.sync
    def _(sy):
        for dst, nm in ((p.c_epsv, "epsv"), (p.c_id_f, "ident"),
                        (p.c_sel4, "sel4"), (p.c_maskL, "maskL"),
                        (p.c_maskU, "maskU"), (p.c_eye1, "eye1"),
                        (p.c_eye2, "eye2"), (p.c_dmask, "dmask"),
                        (p.g_pp, "g")):
            sy.dma_start(out=dst[:], in_=din[nm]).then_inc(L["sLD"], 16)
            p.cnt["sLD"] += 16
        cv = p.cv_pp[:].rearrange("p (pr ft j) -> p pr ft j", pr=3, ft=FT)
        for i, nm in enumerate(("cvq", "cvk", "cvv")):
            sy.dma_start(out=cv[:, i, :, :], in_=din[nm]).then_inc(L["sLD"], 16)
            p.cnt["sLD"] += 16
        for dst, nm in ((p.lng_pp, "lng"), (p.lnb_pp, "lnb"), (p.b2_pp, "b2"),
                        (p.qs_pp, "fqs"), (p.qsr_pp, "fqsr"),
                        (p.ks_pp, "fks"), (p.ksr_pp, "fksr"),
                        (p.b1ac, "b1a"), (p.b1gc, "b1g")):
            sy.dma_start(out=dst[:], in_=din[nm]).then_inc(L["sLD"], 16)
            p.cnt["sLD"] += 16

    ld0 = p.cnt["sLD"]
    lg0 = p.cnt["sLG"]

    @block.vector
    def _(v):
        v.wait_ge(L["sLG"], lg0)
        v.wait_ge(L["sLD"], ld0)
        v.memset(p.sdt[:], 0.0)
        v.memset(p.dumm[:], 0.0)
        v.memset(p.halo[:], 0.0)
        v.memset(p.stream[:].rearrange("p (ft c) -> p ft c", ft=FT)[:, :, 0:1],
                 0.0)
        v.last.then_inc(L["sVE"], 1)
    p.bump("sVE")
    ve0 = p.cnt["sVE"]

    @block.tensor
    def _(t):
        t.wait_ge(L["sVE"], ve0)

    @block.scalar
    def _(s):
        s.wait_ge(L["sVE"], ve0)

    emit_dn_A(p, block, din, scr, stile)
    if DEV_STOP == "A":
        emit_dev_out(p, block, out_d, stile)
        return
    emit_dn_B(p, block, din, scr, stile)
    if DEV_STOP == "B":
        emit_dev_out(p, block, out_d, stile)
        return
    for layer in range(2):
        emit_ffn_attn(p, block, din, scr, layer, stile)
        if DEV_STOP == f"L{layer}a":
            emit_dev_out(p, block, out_d, stile)
            return
        emit_ffn_mlp(p, block, din, scr, layer, stile)
        if DEV_STOP == f"L{layer}":
            emit_dev_out(p, block, out_d, stile)
            return
    emit_final(p, block, scr, out_d, stile)


def emit_dev_out(p, block, out_d, stile):
    L = p.sems
    fence(p, block)

    @block.gpsimd
    def _(g):
        p.w(g, "sVE")
        g.dma_start(out=out_d.rearrange("(ft p) t -> p ft t", p=128),
                    in_=p.stream[:].rearrange("p (ft c) -> p ft c", ft=FT)
                    [:, :, 1:1 + T]).then_inc(L["sST"], 16)
    p.bump("sST", 16)

    @block.sync
    def _(sy):
        p.w(sy, "sST")
        for nm in ("sSP", "sLD", "sLG", "sWT"):
            p.w(sy, nm)


def emit_dn_A(p, block, din, scr, stile):
    nc = p.nc
    L = p.sems
    es = ExitStack()
    wq_sb = es.enter_context(nc.sbuf_tensor("wq_sb", [128, 8192], BF16))
    wk_sb = es.enter_context(nc.sbuf_tensor("wk_sb", [128, 8192], BF16))
    wv_sb = es.enter_context(nc.sbuf_tensor("wv_sb", [128, 8192], BF16))
    xps = [es.enter_context(nc.sbuf_tensor(f"xpd{i}", [128, FT * 515], F32))
           for i in range(2)]
    lnxs = [es.enter_context(nc.sbuf_tensor("lnxd0", [128, FT * 512], BF16))]
    sts = [es.enter_context(nc.sbuf_tensor(f"std{i}", [128, FT * 512], BF16))
           for i in range(2)]
    stf = es.enter_context(nc.sbuf_tensor("stf", [128, 4 * 512], F32))
    lnmu = es.enter_context(nc.sbuf_tensor("lnmu", [128, 512], F32))
    lnt = es.enter_context(nc.sbuf_tensor("lnt", [128, 512], F32))
    lnrs = es.enter_context(nc.sbuf_tensor("lnrs", [128, 512], F32))
    bet_a = lnt

    @block.gpsimd
    def _(g):
        for wsb, nm in ((wq_sb, "wq"), (wk_sb, "wk"), (wv_sb, "wv")):
            g.dma_start(out=wsb[:].rearrange("p (a b q) -> p a b q", a=FT,
                                             b=FT),
                        in_=din[nm].rearrange("(a p) (b q) -> p a b q",
                                              p=128, q=128)
                        ).then_inc(L["sWT"], 16)
    p.bump("sWT", 48)
    wt_w = p.cnt["sWT"]

    def wtile(wsb, fi, fo):
        return wsb[:].rearrange("p (a b q) -> p a b q", a=FT, b=FT)[:, fi,
                                                                    fo, :]

    ve_gate = p.cnt["sVE"]

    PLN = {}

    def lnx_t(sn, ft):
        return lnxs[0][:, ft * 512:(ft + 1) * 512]

    def st_t(j, ft):
        return sts[j % 2][:, ft * 512:(ft + 1) * 512]

    def stf_t(ft):
        return stf[:, (ft % 4) * 512:(ft % 4 + 1) * 512]

    def xp_t(j, ft):
        return xps[j % 2][:, ft * 515:(ft + 1) * 515]

    g0 = p.cnt["sPO"]
    pe0 = p.cnt["sPE"]
    a0 = p.cnt["sAE"]
    v0 = p.cnt["sVE"]
    s0 = p.cnt["sSP"]
    pro = dict(lnsq=g0 + 1, mu2=g0 + 2, lnmm=pe0 + 1, bmm=pe0 + 2,
               mu=a0 + 1, lnexp=a0 + 2, sig=a0 + 3, vesub=v0 + 1,
               lnap=v0 + 2, bst=s0 + 64)

    @block.gpsimd
    def _(g):
        g.wait_ge(L["sVE"], ve_gate)
        for ft in range(FT):
            g.tensor_mul(p.sq[:, ft * 512:(ft + 1) * 512],
                         stile(ft, 0, SLA), stile(ft, 0, SLA))
        g.last.then_inc(L["sPO"], 1)
        g.wait_ge(L["sAE"], pro["mu"])
        g.tensor_mul(lnrs[:], lnmu[:], lnmu[:])
        g.last.then_inc(L["sPO"], 1)
    p.cnt["sPO"] += 2

    @block.tensor
    def _(t):
        t.wait_ge(L["sPO"], pro["lnsq"])
        for ft in range(FT):
            t.matmul(p.pp2[:, 0:512], p.c_sw1024[:], stile(ft, 0, SLA),
                     start=(ft == 0), stop=(ft == FT - 1))
        for ft in range(FT):
            t.matmul(p.pp3[:, 0:512], p.c_sw1024[:],
                     p.sq[:, ft * 512:(ft + 1) * 512],
                     start=(ft == 0), stop=(ft == FT - 1))
        t.last.then_inc(L["sPE"], 1)
        t.wait_ge(L["sVE"], pro["lnap"])
        t.wait_ge(L["sWT"], wt_w)
        for fi in range(FT):
            t.matmul(p.pw0[:, 0:512],
                     p.wb_bf[:].rearrange("p (ft m) -> p ft m", ft=FT)[:, fi, :],
                     lnx_t(0, fi), start=(fi == 0), stop=(fi == FT - 1))
        t.last.then_inc(L["sPE"], 1)
    p.cnt["sPE"] += 2

    @block.scalar
    def _(sc):
        sc.wait_ge(L["sPE"], pro["lnmm"])
        sc.copy(lnmu[:], p.pp2[:, 0:512])
        sc.last.then_inc(L["sAE"], 1)
        sc.wait_ge(L["sVE"], pro["vesub"])
        sc.activation(lnrs[:], lnrs[:], AF.Ln, bias=LEPS)
        sc.activation(lnrs[:], lnrs[:], AF.Exp, scale=-0.5)
        sc.last.then_inc(L["sAE"], 1)
        sc.wait_ge(L["sPE"], pro["bmm"])
        for h in range(4):
            sc.activation(bet_a[h * 32:h * 32 + 1, :],
                          p.pw0[h * 32:h * 32 + 1, 0:512], AF.Sigmoid)
        sc.last.then_inc(L["sAE"], 1)
    p.cnt["sAE"] += 3

    @block.vector
    def _(v):
        v.wait_ge(L["sPO"], pro["mu2"])
        v.tensor_sub(lnrs[:], p.pp3[:, 0:512], lnrs[:])
        v.last.then_inc(L["sVE"], 1)
        v.wait_ge(L["sAE"], pro["lnexp"])
        for ft in range(FT):
            v.tensor_sub(lnt[:], stile(ft, 0, SLA), lnmu[:])
            v.tensor_mul(lnx_t(0, ft), lnt[:], lnrs[:])
        v.last.then_inc(L["sVE"], 1)
    p.cnt["sVE"] += 2

    @block.sync
    def _(sy):
        sy.wait_ge(L["sAE"], pro["sig"])
        for h in range(4):
            sy.dma_start(out=scr["betar_s"][h:h + 1, 0:SLA],
                         in_=bet_a[h * 32:h * 32 + 1, :]
                         ).then_inc(L["sSP"], 16)
    p.cnt["sSP"] += 64

    prev = dict(lnap=pro["lnap"], zipB_end=0, rsqB=0, rskB=0,
                vesub=pro["vesub"], convk_end=0, convv_end=0, l2kmul=0,
                sig=pro["sig"], sto_k=0, sto_v=0, bst=pro["bst"],
                siv_l=None, l2kB=pro["lnmm"], sie_v=0, a_end=p.cnt["sAE"])
    outs_nm = ("qn_s", "kn_s", "v_s")

    for s in range(NSA):
        t0 = s * SLA
        last = (s + 1 == NSA)
        jq, jk, jv = 3 * s, 3 * s + 1, 3 * s + 2
        g0 = p.cnt["sPO"]
        pe0 = p.cnt["sPE"]
        a0 = p.cnt["sAE"]
        v0 = p.cnt["sVE"]
        s0 = p.cnt["sSP"]
        if not last:
            gP = dict(lnsq=g0 + 1, mu2=g0 + 2, sqq=g0 + 3, sqk=g0 + 4)
            peP = dict(tq=[pe0 + 1 + i for i in range(8)], lnmm=pe0 + 9,
                       tk=[pe0 + 10 + i for i in range(8)],
                       l2qA=pe0 + 18, l2qB=pe0 + 19,
                       tv=[pe0 + 20 + i for i in range(8)],
                       l2kA=pe0 + 28, l2kB=pe0 + 29, bmm=pe0 + 30)
            aP = dict(evq=[a0 + 1 + i for i in range(8)], mu=a0 + 9,
                      evk=[a0 + 10 + 2 * i for i in range(8)],
                      siq=[a0 + 11 + 2 * i for i in range(8)],
                      rsqA=a0 + 26, rsqB=a0 + 27,
                      evv=[a0 + 28 + 2 * i for i in range(8)],
                      sik=[a0 + 29 + 2 * i for i in range(8)],
                      rskA=a0 + 44, rskB=a0 + 45,
                      lnexp=a0 + 46, siv=[a0 + 47 + i for i in range(8)],
                      sig=a0 + 55)
            vP = dict(convq=[v0 + 1 + i for i in range(8)], vesub=v0 + 9,
                      convk=[v0 + 10 + i for i in range(8)], l2qmul=v0 + 18,
                      lnap=v0 + 19, convv=[v0 + 20 + i for i in range(8)],
                      l2kmul=v0 + 28)
            sP = dict(sto_q=s0 + 16, sto_k=s0 + 32, sto_v=s0 + 48,
                      bst=s0 + 112)
        else:
            gP = dict(sqq=g0 + 1, sqk=g0 + 2)
            peP = dict(tq=[pe0 + 1 + i for i in range(8)],
                       tk=[pe0 + 9 + i for i in range(8)],
                       l2qA=pe0 + 17, l2qB=pe0 + 18,
                       tv=[pe0 + 19 + i for i in range(8)],
                       l2kA=pe0 + 27, l2kB=pe0 + 28)
            aP = dict(evq=[a0 + 1 + i for i in range(8)],
                      evk=[a0 + 9 + 2 * i for i in range(8)],
                      siq=[a0 + 10 + 2 * i for i in range(8)],
                      rsqA=a0 + 25, rsqB=a0 + 26,
                      evv=[a0 + 27 + 2 * i for i in range(8)],
                      sik=[a0 + 28 + 2 * i for i in range(8)],
                      rskA=a0 + 43, rskB=a0 + 44,
                      siv=[a0 + 45 + i for i in range(8)])
            vP = dict(convq=[v0 + 1 + i for i in range(8)],
                      convk=[v0 + 9 + i for i in range(8)], l2qmul=v0 + 17,
                      convv=[v0 + 18 + i for i in range(8)], l2kmul=v0 + 26)
            sP = dict(sto_q=s0 + 16, sto_k=s0 + 32, sto_v=s0 + 48)

        @block.gpsimd
        def _(g, s=s, last=last, gP=gP, aP=aP, peP=peP, prev=dict(prev),
              jq=jq, jk=jk):
            if not last:
                if prev["l2kB"]:
                    g.wait_ge(L["sPE"], prev["l2kB"])
                for ft in range(FT):
                    g.tensor_mul(p.sq[:, ft * 512:(ft + 1) * 512],
                                 stile(ft, (s + 1) * SLA, SLA),
                                 stile(ft, (s + 1) * SLA, SLA))
                g.last.then_inc(L["sPO"], 1)
                g.wait_ge(L["sAE"], aP["mu"])
                g.tensor_mul(lnrs[:], lnmu[:], lnmu[:])
                g.last.then_inc(L["sPO"], 1)
            g.wait_ge(L["sAE"], aP["siq"][7])
            g.wait_ge(L["sPE"], peP["lnmm"] if not last else peP["tq"][7])
            for ft in range(FT):
                g.tensor_mul(p.sq[:, ft * 512:(ft + 1) * 512],
                             st_t(jq, ft), st_t(jq, ft))
            g.last.then_inc(L["sPO"], 1)
            g.wait_ge(L["sAE"], aP["sik"][7])
            g.wait_ge(L["sPE"], peP["l2qB"])
            for ft in range(FT):
                g.tensor_mul(p.sq[:, ft * 512:(ft + 1) * 512],
                             st_t(jk, ft), st_t(jk, ft))
            g.last.then_inc(L["sPO"], 1)
        p.cnt["sPO"] += 2 if last else 4

        @block.tensor
        def _(t, s=s, t0=t0, last=last, gP=gP, peP=peP, aP=aP, vP=vP,
              prev=dict(prev), jq=jq, jk=jk, jv=jv):
            t.wait_ge(L["sVE"], prev["lnap"])
            if prev["zipB_end"]:
                t.wait_ge(L["sAE"], prev["zipB_end"])
            if s == 0:
                t.wait_ge(L["sWT"], wt_w)
            slotq = [None] * 8

            def projmm(j, wsb, evl, prior):
                for i in range(8):
                    if i >= 2:
                        t.wait_ge(L["sAE"], evl[i - 2])
                    elif prior[i]:
                        t.wait_ge(L["sAE"], prior[i])
                    ps = p.pw1[:, 0:512] if i % 2 == 0 else p.pw1[:, 512:1024]
                    for fi in range(FT):
                        t.matmul(ps, wtile(wsb, fi, i), lnx_t(s, fi),
                                 start=(fi == 0), stop=(fi == FT - 1))
                    t.last.then_inc(L["sPE"], 1)

            projmm(jq, wq_sb, aP["evq"],
                   [prev["zipB_end"] and (prev["zipB_end"] - 1),
                    prev["zipB_end"]])
            if not last:
                t.wait_ge(L["sPO"], gP["lnsq"])
                t.wait_ge(L["sAE"], prev["rskB"] if prev["rskB"]
                          else aP["mu"] - 1)
                t.wait_ge(L["sVE"], prev["vesub"])
                for ft in range(FT):
                    t.matmul(p.pp2[:, 0:512], p.c_sw1024[:],
                             stile(ft, (s + 1) * SLA, SLA),
                             start=(ft == 0), stop=(ft == FT - 1))
                for ft in range(FT):
                    t.matmul(p.pp3[:, 0:512], p.c_sw1024[:],
                             p.sq[:, ft * 512:(ft + 1) * 512],
                             start=(ft == 0), stop=(ft == FT - 1))
                t.last.then_inc(L["sPE"], 1)
            projmm(jk, wk_sb, aP["evk"], [aP["evq"][6], aP["evq"][7]])
            t.wait_ge(L["sPO"], gP["sqq"])
            if prev["rsqB"]:
                t.wait_ge(L["sAE"], prev["rsqB"])
            for rnd in range(2):
                if rnd == 1:
                    t.wait_ge(L["sAE"], aP["rsqA"])
                for hh in range(2):
                    h = rnd * 2 + hh
                    ps = p.pp0 if hh == 0 else p.pp1
                    for k in range(2):
                        t.matmul(ps[:, 0:512], p.c_ones[:],
                                 p.sq[:, (2 * h + k) * 512:
                                      (2 * h + k + 1) * 512],
                                 start=(k == 0), stop=(k == 1))
                t.last.then_inc(L["sPE"], 1)
            projmm(jv, wv_sb, aP["evv"], [aP["evk"][6], aP["evk"][7]])
            t.wait_ge(L["sPO"], gP["sqk"])
            if not last:
                t.wait_ge(L["sVE"], vP["vesub"])
                t.wait_ge(L["sAE"], aP["mu"])
            for rnd in range(2):
                if rnd == 1:
                    t.wait_ge(L["sAE"], aP["rskA"])
                for hh in range(2):
                    h = rnd * 2 + hh
                    ps = p.pp2 if hh == 0 else p.pp3
                    for k in range(2):
                        t.matmul(ps[:, 0:512], p.c_ones[:],
                                 p.sq[:, (2 * h + k) * 512:
                                      (2 * h + k + 1) * 512],
                                 start=(k == 0), stop=(k == 1))
                t.last.then_inc(L["sPE"], 1)
            if not last:
                t.wait_ge(L["sVE"], vP["lnap"])
                t.wait_ge(L["sAE"], prev["sig"])
                for fi in range(FT):
                    t.matmul(p.pw0[:, 0:512],
                             p.wb_bf[:].rearrange("p (ft m) -> p ft m",
                                                  ft=FT)[:, fi, :],
                             lnx_t(s + 1, fi), start=(fi == 0),
                             stop=(fi == FT - 1))
                t.last.then_inc(L["sPE"], 1)
        p.cnt["sPE"] += 30 if not last else 28

        @block.scalar
        def _(sc, s=s, last=last, peP=peP, aP=aP, vP=vP, prev=dict(prev),
              jq=jq, jk=jk, jv=jv):
            hap = p.halo[:].rearrange("p (pr ft j) -> p pr ft j", pr=3, ft=FT)
            for ft in range(8):
                sc.wait_ge(L["sPE"], peP["tq"][ft])
                if ft == 0 and prev["convk_end"]:
                    sc.wait_ge(L["sVE"], prev["convk_end"])
                ps = p.pw1[:, 0:512] if ft % 2 == 0 else p.pw1[:, 512:1024]
                sc.copy(xps[jq % 2][:, ft * 515 + 3:ft * 515 + 515],
                        ps).then_inc(L["sAE"], 1)
            if not last:
                sc.wait_ge(L["sPE"], peP["lnmm"])
                sc.copy(lnmu[:], p.pp2[:, 0:512]).then_inc(L["sAE"], 1)
            for ft in range(8):
                sc.wait_ge(L["sPE"], peP["tk"][ft])
                if ft == 0 and prev["convv_end"]:
                    sc.wait_ge(L["sVE"], prev["convv_end"])
                ps = p.pw1[:, 0:512] if ft % 2 == 0 else p.pw1[:, 512:1024]
                sc.copy(xps[jk % 2][:, ft * 515 + 3:ft * 515 + 515],
                        ps).then_inc(L["sAE"], 1)
                sc.wait_ge(L["sVE"], vP["convq"][ft])
                if ft == 0 and prev["sto_k"]:
                    sc.wait_ge(L["sSP"], prev["sto_k"])
                sc.activation(st_t(jq, ft), stf_t(ft),
                              AF.Silu).then_inc(L["sAE"], 1)
            sc.wait_ge(L["sPE"], peP["l2qA"])
            if prev["l2kmul"]:
                sc.wait_ge(L["sVE"], prev["l2kmul"])
            for hh in range(2):
                ps = p.pp0 if hh == 0 else p.pp1
                dst = [p.gs1, p.gs2][hh]
                sc.activation(dst[:, 0:512], ps[:, 0:512], AF.Ln)
                sc.activation(dst[:, 0:512], dst[:, 0:512], AF.Exp, scale=-0.5)
            sc.last.then_inc(L["sAE"], 1)
            sc.wait_ge(L["sPE"], peP["l2qB"])
            for hh in range(2):
                ps = p.pp0 if hh == 0 else p.pp1
                dst = [p.gs3, p.gs4][hh]
                sc.activation(dst[:, 0:512], ps[:, 0:512], AF.Ln)
                sc.activation(dst[:, 0:512], dst[:, 0:512], AF.Exp, scale=-0.5)
            sc.last.then_inc(L["sAE"], 1)
            for ft in range(8):
                sc.wait_ge(L["sPE"], peP["tv"][ft])
                sc.wait_ge(L["sVE"], vP["convq"][7])
                ps = p.pw1[:, 0:512] if ft % 2 == 0 else p.pw1[:, 512:1024]
                sc.copy(xps[jv % 2][:, ft * 515 + 3:ft * 515 + 515],
                        ps).then_inc(L["sAE"], 1)
                sc.wait_ge(L["sVE"], vP["convk"][ft])
                if ft == 0 and prev["sto_v"]:
                    sc.wait_ge(L["sSP"], prev["sto_v"])
                sc.activation(st_t(jk, ft), stf_t(ft),
                              AF.Silu).then_inc(L["sAE"], 1)
            sc.wait_ge(L["sPE"], peP["l2kA"])
            sc.wait_ge(L["sVE"], vP["l2qmul"])
            for hh in range(2):
                ps = p.pp2 if hh == 0 else p.pp3
                dst = [p.gs1, p.gs2][hh]
                sc.activation(dst[:, 0:512], ps[:, 0:512], AF.Ln)
                sc.activation(dst[:, 0:512], dst[:, 0:512], AF.Exp, scale=-0.5)
            sc.last.then_inc(L["sAE"], 1)
            sc.wait_ge(L["sPE"], peP["l2kB"])
            for hh in range(2):
                ps = p.pp2 if hh == 0 else p.pp3
                dst = [p.gs3, p.gs4][hh]
                sc.activation(dst[:, 0:512], ps[:, 0:512], AF.Ln)
                sc.activation(dst[:, 0:512], dst[:, 0:512], AF.Exp, scale=-0.5)
            sc.last.then_inc(L["sAE"], 1)
            if not last:
                sc.wait_ge(L["sVE"], vP["vesub"])
                sc.activation(lnrs[:], lnrs[:], AF.Ln, bias=LEPS)
                sc.activation(lnrs[:], lnrs[:], AF.Exp, scale=-0.5)
                sc.last.then_inc(L["sAE"], 1)
            for ft in range(8):
                sc.wait_ge(L["sVE"], vP["convv"][ft])
                if ft == 0:
                    sc.wait_ge(L["sSP"], sP["sto_q"])
                sc.activation(st_t(jv, ft), stf_t(ft),
                              AF.Silu).then_inc(L["sAE"], 1)
            if not last:
                sc.wait_ge(L["sPE"], peP["bmm"])
                sc.wait_ge(L["sSP"], prev["bst"])
                for h in range(4):
                    sc.activation(bet_a[h * 32:h * 32 + 1, :],
                                  p.pw0[h * 32:h * 32 + 1, 0:512], AF.Sigmoid)
                sc.last.then_inc(L["sAE"], 1)
        p.cnt["sAE"] += 55 if not last else 52

        @block.vector
        def _(v, s=s, last=last, aP=aP, vP=vP, gP=gP, peP=peP,
              prev=dict(prev), jq=jq, jk=jk, jv=jv):
            hap = p.halo[:].rearrange("p (pr ft j) -> p pr ft j", pr=3, ft=FT)
            cv = p.cv_pp[:].rearrange("p (pr ft j) -> p pr ft j", pr=3, ft=FT)

            def conv(j, pi, evl, s_own, s_prev):
                for ft in range(FT):
                    w_ = evl[ft]
                    if ft >= 4:
                        w_ = max(w_, s_own[ft - 4])
                    elif s_prev:
                        w_ = max(w_, s_prev[ft + 4])
                    v.wait_ge(L["sAE"], w_)
                    xpj = xps[j % 2]
                    v.tensor_copy(xpj[:, ft * 515:ft * 515 + 3],
                                  hap[:, pi, ft, :])
                    v.tensor_scalar_mul(stf_t(ft),
                                        xpj[:, ft * 515:ft * 515 + 512],
                                        cv[:, pi, ft, 0:1])
                    for jj in range(1, 4):
                        v.scalar_tensor_tensor(
                            stf_t(ft),
                            xpj[:, ft * 515 + jj:ft * 515 + jj + 512],
                            cv[:, pi, ft, jj:jj + 1], stf_t(ft),
                            ALU.mult, ALU.add)
                    v.tensor_copy(hap[:, pi, ft, :],
                                  xpj[:, ft * 515 + 512:ft * 515 + 515])
                    v.last.then_inc(L["sVE"], 1)

            def l2mul(j):
                for ft in range(FT):
                    rs = [p.gs1, p.gs2, p.gs3, p.gs4][ft // 2]
                    v.tensor_mul(st_t(j, ft), st_t(j, ft), rs[:, 0:512])
                v.last.then_inc(L["sVE"], 1)

            conv(jq, 0, aP["evq"], aP["siq"], prev["siv_l"])
            if not last:
                v.wait_ge(L["sPO"], gP["mu2"])
                v.tensor_sub(lnrs[:], p.pp3[:, 0:512], lnrs[:])
                v.last.then_inc(L["sVE"], 1)
            conv(jk, 1, aP["siq"], aP["sik"], aP["siq"])
            v.wait_ge(L["sAE"], aP["rsqB"])
            l2mul(jq)
            if not last:
                v.wait_ge(L["sAE"], aP["lnexp"])
                v.wait_ge(L["sPE"], peP["tv"][7])
                v.wait_ge(L["sSP"], prev["bst"])
                for ft in range(FT):
                    v.tensor_sub(lnt[:], stile(ft, (s + 1) * SLA, SLA),
                                 lnmu[:])
                    v.tensor_mul(lnx_t(s + 1, ft), lnt[:], lnrs[:])
                v.last.then_inc(L["sVE"], 1)
            conv(jv, 2, aP["sik"], aP["siv"], aP["sik"])
            v.wait_ge(L["sAE"], aP["rskB"])
            l2mul(jk)
        p.cnt["sVE"] += 28 if not last else 26

        @block.sync
        def _(sy, s=s, last=last, aP=aP, vP=vP, jq=jq, jk=jk, jv=jv):
            sy.wait_ge(L["sVE"], vP["l2qmul"])
            sy.dma_start(out=scr["qn_s"][:, :, s * SLA:(s + 1) * SLA],
                         in_=sts[jq % 2][:].rearrange("p (ft t) -> p ft t",
                                                      ft=FT)
                         ).then_inc(L["sSP"], 16)
            sy.wait_ge(L["sVE"], vP["l2kmul"])
            sy.dma_start(out=scr["kn_s"][:, :, s * SLA:(s + 1) * SLA],
                         in_=sts[jk % 2][:].rearrange("p (ft t) -> p ft t",
                                                      ft=FT)
                         ).then_inc(L["sSP"], 16)
            sy.wait_ge(L["sAE"], aP["siv"][7])
            sy.dma_start(out=scr["v_s"][:, :, s * SLA:(s + 1) * SLA],
                         in_=sts[jv % 2][:].rearrange("p (ft t) -> p ft t",
                                                      ft=FT)
                         ).then_inc(L["sSP"], 16)
            if not last:
                sy.wait_ge(L["sAE"], aP["sig"])
                for h in range(4):
                    sy.dma_start(out=scr["betar_s"][h:h + 1, (s + 1) * SLA:
                                                    (s + 2) * SLA],
                                 in_=bet_a[h * 32:h * 32 + 1, :]
                                 ).then_inc(L["sSP"], 16)
        p.cnt["sSP"] += 112 if not last else 48

        prev = dict(lnap=vP.get("lnap", prev["lnap"]),
                    zipB_end=aP["sik"][7], rsqB=aP["rsqB"], rskB=aP["rskB"],
                    vesub=vP.get("vesub", prev["vesub"]),
                    convk_end=vP["convk"][7], convv_end=vP["convv"][7],
                    l2kmul=vP["l2kmul"], sig=aP.get("sig", prev["sig"]),
                    sto_k=sP["sto_k"], sto_v=sP["sto_v"],
                    bst=sP.get("bst", prev["bst"]), siv_l=aP["siv"],
                    l2kB=peP["l2kB"], sie_v=0, a_end=p.cnt["sAE"])

    fence(p, block)
    es.close()


def emit_dn_B(p, block, din, scr, stile):
    nc = p.nc
    L = p.sems
    es = ExitStack()
    wo_sb = es.enter_context(nc.sbuf_tensor("wo_sb", [128, 8192], BF16))
    qn_b = es.enter_context(nc.sbuf_tensor("qn_b", [128, 2 * FT * SLB], BF16))
    kn_b = es.enter_context(nc.sbuf_tensor("kn_b", [128, 2 * FT * SLB], BF16))
    v_b = es.enter_context(nc.sbuf_tensor("v_b", [128, 2 * FT * SLB], BF16))
    w_b = es.enter_context(nc.sbuf_tensor("w_b", [128, 2 * FT * SLB], BF16))
    u_b = es.enter_context(nc.sbuf_tensor("u_b", [128, 2 * FT * SLB], BF16))
    kb_b = es.enter_context(nc.sbuf_tensor("kb_b", [128, FT * SLB], BF16))
    vb_b = es.enter_context(nc.sbuf_tensor("vb_b", [128, FT * SLB], BF16))
    o_b = es.enter_context(nc.sbuf_tensor("o_b", [128, FT * SLB], BF16))
    ao_b = es.enter_context(nc.sbuf_tensor("ao_b", [128, FT * SLB], BF16))
    kpm = es.enter_context(nc.sbuf_tensor("kpm", [128, 2 * FT * 128], BF16))
    vpm = es.enter_context(nc.sbuf_tensor("vpm", [128, 2 * FT * 128], BF16))
    dm_b = es.enter_context(nc.sbuf_tensor("dm_b", [128, SLB], F32))
    bet_b = es.enter_context(nc.sbuf_tensor("bet_b", [4, 2 * SLB], F32))
    aa_b = es.enter_context(nc.sbuf_tensor("aa_b", [128, FT * CPS], F32))
    bb_b = es.enter_context(nc.sbuf_tensor("bb_b", [128, FT * CPS], F32))
    ch = {}
    for nm in ("chL", "chLt", "chP", "chPt", "chQ", "chB"):
        ch[nm] = es.enter_context(nc.sbuf_tensor(nm, [128, CHW], F32))
    tt_bf = es.enter_context(nc.sbuf_tensor("tt_bf", [128, CHW], BF16))
    mu_a = es.enter_context(nc.sbuf_tensor("mu_a", [128, SLB], F32))
    rs_a = es.enter_context(nc.sbuf_tensor("rs_a", [128, SLB], F32))
    mu_c = es.enter_context(nc.sbuf_tensor("mu_c", [128, SLB], F32))
    rs_c = es.enter_context(nc.sbuf_tensor("rs_c", [128, SLB], F32))
    lt_b = es.enter_context(nc.sbuf_tensor("lt_b", [128, SLB], F32))

    @block.gpsimd
    def _(g):
        p.w(g, "sPE")
        p.w(g, "sVE")
        p.w(g, "sAE")
        g.dma_start(out=wo_sb[:].rearrange("p (a b q) -> p a b q", a=FT, b=FT),
                    in_=din["wod"].rearrange("(a p) (b q) -> p a b q",
                                             p=128, q=128)
                    ).then_inc(L["sWT"], 16)
    p.bump("sWT", 16)
    wo_w = p.cnt["sWT"]

    def wotile(fi, fo):
        return wo_sb[:].rearrange("p (a b q) -> p a b q", a=FT, b=FT)[:, fi,
                                                                      fo, :]

    def slab(buf, c, ft, lo=0, n=SLB):
        base = ((c % 2) * FT + ft) * SLB
        return buf[:, base + lo:base + lo + n]

    b3 = lambda x: x.rearrange("p (n c) -> p n c", c=64)
    PL = {}

    for c in range(NSB + 2):
        c1, c2 = c - 1, c - 2
        hasB1 = c < NSB
        hasA = 0 <= c1 < NSB
        hasB = 0 <= c2 < NSB
        pl = {}
        pe = p.cnt["sPE"]
        ve = p.cnt["sVE"]
        ac = p.cnt["sAE"]
        po = p.cnt["sPO"]
        if hasB:
            pl["subln1"] = ve = ve + 1
            pl["resid"] = ve = ve + 1
            pl["ln2mm"] = pe = pe + 1
        if hasA:
            pl["rowsum"] = pe = pe + 1
            pl["dm"] = ve = ve + 1
        if hasB1:
            pl["beta"] = pe = pe + 1
            pl["kbvb"] = ve = ve + 1
        if hasB:
            pl["subln2"] = ve = ve + 1
        if hasB1:
            pl["tr"] = [pe + 1 + i for i in range(8)]
            pe += 8
            pl["aat"] = pe = pe + 1
            pl["mask"] = ve = ve + 1
            pl["pair"] = []
            pl["qupd"] = []
            for lvl in range(5):
                pe += 1
                pl["pair"].append(pe)
                pe += 1
                pl["qupd"].append(pe)
            pl["wu"] = [pe + 1 + i for i in range(CPS)]
            pe += CPS
        if hasA:
            pl["scan"] = ve = ve + 1
        if hasB1:
            pl["tt"] = ve = ve + 1
            pl["wuev"] = [ve + 1 + i for i in range(CPS)]
            ve += CPS
        if hasA:
            pl["rmsA"] = pe = pe + 1
            pl["rmsB"] = pe = pe + 1
            pl["scale"] = ve = ve + 1
            pl["wo"] = [pe + 1 + i for i in range(8)]
            pe += 8
            pl["ln1mm"] = pe = pe + 1
        if hasB:
            pl["apply"] = ve = ve + 1
        if hasB:
            pl["muln2"] = ac = ac + 1
            pl["lnexp2"] = ac = ac + 1
        if hasB1:
            pl["trev"] = [ac + 1 + i for i in range(8)]
            ac += 8
            pl["chcp"] = []
            for lvl in range(5):
                ac += 1
                pl["chcp"].append(ac)
                if lvl < 4:
                    ac += 1
                    pl["chcp"].append(ac)
        if hasA:
            pl["rstdA"] = ac = ac + 1
            pl["rstdB"] = ac = ac + 1
            pl["woev"] = [ac + 1 + i for i in range(8)]
            ac += 8
            pl["muln1"] = ac = ac + 1
            pl["lnexp1"] = ac = ac + 1
        if hasB:
            pl["ln2sq"] = po = po + 1
            pl["mu2ln2"] = po = po + 1
        if hasA:
            pl["qk"] = po = po + 1
            pl["sqrms"] = po = po + 1
            pl["ln1sq"] = po = po + 1
            pl["mu2ln1"] = po = po + 1
        pl["loads"] = p.cnt["sLD"] + (64 if hasB1 else 0)
        pl["x1st"] = p.cnt["sSP"] + (16 if hasB else 0)
        PL[c] = pl
        q1 = PL.get(c1, {})
        q2 = PL.get(c2, {})

        if DEV_DNB and c == 1:
            @block.sync
            def _(sy, q=PL[0]):
                sy.wait_ge(L["sVE"], q["kbvb"])
                sy.dma_start(out=scr["dbg"][:, 0:2048],
                             in_=kb_b[:]).then_inc(L["sDB"], 16)
                sy.dma_start(out=scr["dbg"][:, 2048:4096],
                             in_=vb_b[:]).then_inc(L["sDB"], 16)
                sy.wait_ge(L["sVE"], q["wuev"][-1])
                sy.dma_start(out=scr["dbg"][:, 4096:6144],
                             in_=w_b[:, 0:2048]).then_inc(L["sDB"], 16)
                sy.dma_start(out=scr["dbg"][:, 6144:8192],
                             in_=u_b[:, 0:2048]).then_inc(L["sDB"], 16)
                sy.wait_ge(L["sVE"], q["tt"])
                sy.dma_start(out=scr["dbg"][:, 8192:8320],
                             in_=tt_bf[:]).then_inc(L["sDB"], 16)
            p.cnt["sDB"] += 80
        if DEV_DNB and c == 2:
            @block.sync
            def _(sy, q=PL[1]):
                sy.wait_ge(L["sVE"], q["scale"])
                sy.dma_start(out=scr["dbg"][:, 8704:10752],
                             in_=o_b[:]).then_inc(L["sDB"], 16)
                sy.wait_ge(L["sAE"], q["woev"][7])
                sy.dma_start(out=scr["dbg"][:, 10752:12800],
                             in_=ao_b[:]).then_inc(L["sDB"], 16)
            p.cnt["sDB"] += 32

        @block.sync
        def _(sy, c=c, c2=c2, pl=pl, q1=q1, q2=q2, hasB1=hasB1, hasB=hasB):
            if hasB1:
                if q1 and "scan" in q1:
                    sy.wait_ge(L["sVE"], q1["scan"])
                    sy.wait_ge(L["sPO"], q1["qk"])
                if q2 and "beta" in q2:
                    sy.wait_ge(L["sPE"], q2["beta"])
                for src_nm, dst in (("qn_s", qn_b), ("kn_s", kn_b),
                                    ("v_s", v_b)):
                    sy.dma_start(
                        out=dst[:].rearrange("p (sl ft t) -> p sl ft t",
                                             sl=2, ft=FT)[:, c % 2, :, :],
                        in_=scr[src_nm][:, :, c * SLB:(c + 1) * SLB]
                        ).then_inc(L["sLD"], 16)
                sy.dma_start(out=bet_b[0:4, (c % 2) * SLB:(c % 2 + 1) * SLB],
                             in_=scr["betar_s"][:, c * SLB:(c + 1) * SLB]
                             ).then_inc(L["sLD"], 16)
            if hasB:
                sy.wait_ge(L["sVE"], pl["resid"])
                sy.dma_start(out=scr["x1_s"][:, :, c2 * SLB:(c2 + 1) * SLB],
                             in_=p.stream[:].rearrange("p (ft cc) -> p ft cc",
                                                       ft=FT)
                             [:, :, 1 + c2 * SLB:1 + (c2 + 1) * SLB]
                             ).then_inc(L["sSP"], 16)
        p.cnt["sLD"] = pl["loads"]
        p.cnt["sSP"] = pl["x1st"]

        @block.gpsimd
        def _(g, c=c, c1=c1, c2=c2, pl=pl, q1=q1, hasB1=hasB1, hasA=hasA,
              hasB=hasB):
            if hasB:
                g.wait_ge(L["sVE"], pl["resid"])
                if q1 and "ln1mm" in q1:
                    g.wait_ge(L["sPE"], q1["ln1mm"])
                for ft in range(FT):
                    g.tensor_mul(p.sq[:, ft * SLB:(ft + 1) * SLB],
                                 stile(ft, c2 * SLB, SLB),
                                 stile(ft, c2 * SLB, SLB))
                g.last.then_inc(L["sPO"], 1)
                g.wait_ge(L["sAE"], pl["muln2"])
                g.tensor_mul(rs_c[:], mu_c[:], mu_c[:])
                g.last.then_inc(L["sPO"], 1)
            if hasA:
                if "ln2mm" in pl:
                    g.wait_ge(L["sPE"], pl["ln2mm"])
                g.wait_ge(L["sLD"], q1["loads"])
                for ft in range(FT):
                    g.tensor_mul(p.sq[:, ft * SLB:(ft + 1) * SLB],
                                 slab(qn_b, c1, ft), slab(kn_b, c1, ft))
                g.last.then_inc(L["sPO"], 1)
                g.wait_ge(L["sVE"], pl["scan"])
                g.wait_ge(L["sPE"], pl["rowsum"])
                for ft in range(FT):
                    g.tensor_mul(p.sq[:, ft * SLB:(ft + 1) * SLB],
                                 o_b[:, ft * SLB:(ft + 1) * SLB],
                                 o_b[:, ft * SLB:(ft + 1) * SLB])
                g.last.then_inc(L["sPO"], 1)
                g.wait_ge(L["sAE"], pl["woev"][7])
                g.wait_ge(L["sPE"], pl["rmsB"])
                for ft in range(FT):
                    g.tensor_mul(p.sq[:, ft * SLB:(ft + 1) * SLB],
                                 ao_b[:, ft * SLB:(ft + 1) * SLB],
                                 ao_b[:, ft * SLB:(ft + 1) * SLB])
                g.last.then_inc(L["sPO"], 1)
                g.wait_ge(L["sAE"], pl["muln1"])
                g.tensor_mul(rs_a[:], mu_a[:], mu_a[:])
                g.last.then_inc(L["sPO"], 1)
        p.cnt["sPO"] = po

        @block.tensor
        def _(t, c=c, c1=c1, c2=c2, pl=pl, q1=q1, hasB1=hasB1, hasA=hasA,
              hasB=hasB):
            if hasB:
                t.wait_ge(L["sPO"], pl["ln2sq"])
                if q1 and "muln1" in q1:
                    t.wait_ge(L["sAE"], q1["muln1"])
                t.wait_ge(L["sVE"], pl["subln1"])
                for ft in range(FT):
                    t.matmul(p.pp0[:, 0:SLB], p.c_sw1024[:],
                             stile(ft, c2 * SLB, SLB),
                             start=(ft == 0), stop=(ft == FT - 1))
                for ft in range(FT):
                    t.matmul(p.pp1[:, 0:SLB], p.c_sw1024[:],
                             p.sq[:, ft * SLB:(ft + 1) * SLB],
                             start=(ft == 0), stop=(ft == FT - 1))
                t.last.then_inc(L["sPE"], 1)
            if hasA:
                t.wait_ge(L["sPO"], pl["qk"])
                if hasB:
                    t.wait_ge(L["sAE"], pl["muln2"])
                for ft in range(FT):
                    t.matmul(p.pp0[:, 0:SLB], p.c_ones[:],
                             p.sq[:, ft * SLB:(ft + 1) * SLB],
                             start=(ft == 0), stop=(ft == FT - 1))
                t.last.then_inc(L["sPE"], 1)
            if hasB1:
                t.wait_ge(L["sLD"], pl["loads"])
                if q1 and "wuev" in q1:
                    t.wait_ge(L["sVE"], q1["wuev"][CPS - 1])
                for h in range(4):
                    t.matmul(p.pw1[:, h * SLB:(h + 1) * SLB],
                             p.c_sel4[0:4, h * 128:(h + 1) * 128],
                             bet_b[0:4,
                                   (c % 2) * SLB:(c % 2) * SLB + SLB],
                             start=(h % 2 == 0), stop=(h % 2 == 1),
                             tile_position=(0, 0))
                t.last.then_inc(L["sPE"], 1)
                t.wait_ge(L["sVE"], pl["kbvb"])
                if q1 and "trev" in q1:
                    t.wait_ge(L["sAE"], q1["trev"][7])
                for i in range(8):
                    if i > 0:
                        t.wait_ge(L["sAE"], pl["trev"][i - 1])
                    sb = kb_b if i < 4 else vb_b
                    bb = (i % 4) // 2
                    fh = i % 2
                    for ft in range(fh * 4, fh * 4 + 4):
                        t.matmul(p.pp3[:, (ft % 4) * 128:(ft % 4 + 1) * 128],
                                 sb[:, ft * SLB + bb * 128:
                                    ft * SLB + (bb + 1) * 128],
                                 p.c_id_bf[:], start=(ft % 4 == 0),
                                 stop=(ft % 4 == 3))
                    t.last.then_inc(L["sPE"], 1)
                for rb in (0, 64):
                    order = [(0, 0), (64, 0), (0, 256), (64, 256)]
                    for gi, (cb, base) in enumerate(order):
                        cc = (cb // 64) * 2 + (rb // 64)
                        for fi in range(FT):
                            if base == 0:
                                lhs = kb_b[:, fi * SLB + cc * 64:
                                           fi * SLB + (cc + 1) * 64]
                                rhs = slab(kn_b, c, fi, cc * 64, 64)
                            else:
                                lhs = slab(kn_b, c, fi, cc * 64, 64)
                                rhs = kb_b[:, fi * SLB + cc * 64:
                                           fi * SLB + (cc + 1) * 64]
                            t.matmul(p.pw0[rb:rb + 64,
                                           base + cb:base + cb + 64],
                                     lhs, rhs,
                                     start=(gi == 0 and fi == 0),
                                     stop=(gi == 3 and fi == FT - 1),
                                     tile_position=(0, rb))
                t.last.then_inc(L["sPE"], 1)
                curP, curPt = ch["chL"], ch["chLt"]
                for lvl in range(5):
                    if lvl == 0:
                        t.wait_ge(L["sVE"], pl["mask"])
                    else:
                        t.wait_ge(L["sAE"], pl["chcp"][2 * lvl - 1])
                    for rb in (0, 64):
                        groups = [(0, 0, curPt, curP), (64, 0, curPt, curP)]
                        if lvl < 4:
                            groups += [(0, 256, curP, curPt),
                                       (64, 256, curP, curPt)]
                        for gi, (cb, base, lh, rh) in enumerate(groups):
                            t.matmul(p.pw1[rb:rb + 64,
                                           base + cb:base + cb + 64],
                                     lh[rb:rb + 64, cb:cb + 64],
                                     rh[rb:rb + 64, cb:cb + 64],
                                     start=(gi == 0),
                                     stop=(gi == len(groups) - 1),
                                     tile_position=(rb, rb))
                    t.last.then_inc(L["sPE"], 1)
                    t.wait_ge(L["sAE"],
                              pl["chcp"][2 * lvl] if lvl < 4
                              else pl["chcp"][8])
                    for rb in (0, 64):
                        for gi, cb in enumerate((0, 64)):
                            t.matmul(p.pw0[rb:rb + 64, cb:cb + 64],
                                     p.c_id_f[rb:rb + 64, rb:rb + 64],
                                     ch["chQ"][rb:rb + 64, cb:cb + 64],
                                     start=(gi == 0), stop=False,
                                     tile_position=(rb, rb))
                            t.matmul(p.pw0[rb:rb + 64, cb:cb + 64],
                                     ch["chP"][rb:rb + 64, cb:cb + 64],
                                     ch["chQ"][rb:rb + 64, cb:cb + 64],
                                     start=False, stop=(gi == 1),
                                     tile_position=(rb, rb))
                    t.last.then_inc(L["sPE"], 1)
                    curP, curPt = ch["chP"], ch["chPt"]
                t.wait_ge(L["sVE"], pl["tt"])
                for cc in range(CPS):
                    if cc >= 2:
                        t.wait_ge(L["sVE"], pl["wuev"][cc - 2])
                    rb, cb = (cc % 2) * 64, (cc // 2) * 64
                    bb = cc // 2
                    wps = (p.pw1[:, 0:512] if cc % 2 == 0
                           else p.pw1[:, 512:1024])
                    ups = (p.pw0[:, 512:1024] if cc % 2 == 0
                           else p.pp2[:, 0:512])
                    for ft in range(FT):
                        t.matmul(wps[:, ft * 64:(ft + 1) * 64],
                                 kpm[rb:rb + 64, bb * FT * 128 + ft * 128:
                                     bb * FT * 128 + (ft + 1) * 128],
                                 tt_bf[rb:rb + 64, cb:cb + 64],
                                 start=(ft == 0), stop=(ft == FT - 1),
                                 tile_position=(rb, 0))
                    for ft in range(FT):
                        t.matmul(ups[:, ft * 64:(ft + 1) * 64],
                                 vpm[rb:rb + 64, bb * FT * 128 + ft * 128:
                                     bb * FT * 128 + (ft + 1) * 128],
                                 tt_bf[rb:rb + 64, cb:cb + 64],
                                 start=(ft == 0), stop=(ft == FT - 1),
                                 tile_position=(rb, 0))
                    t.last.then_inc(L["sPE"], 1)
            if hasA:
                t.wait_ge(L["sPO"], pl["sqrms"])
                t.wait_ge(L["sVE"], pl["dm"])
                if hasB:
                    t.wait_ge(L["sVE"], pl["subln2"])
                for rnd in range(2):
                    if rnd == 1:
                        t.wait_ge(L["sAE"], pl["rstdA"])
                    for hh in range(2):
                        h = rnd * 2 + hh
                        ps = p.pp0 if hh == 0 else p.pp1
                        for k in range(2):
                            t.matmul(ps[:, 0:SLB], p.c_sw256[:],
                                     p.sq[:, (2 * h + k) * SLB:
                                          (2 * h + k + 1) * SLB],
                                     start=(k == 0), stop=(k == 1))
                    t.last.then_inc(L["sPE"], 1)
                if c == 1:
                    t.wait_ge(L["sWT"], wo_w)
                t.wait_ge(L["sVE"], pl["scale"])
                for fo in range(FT):
                    if fo >= 2:
                        t.wait_ge(L["sAE"], pl["woev"][fo - 2])
                    ps = p.pp0 if fo % 2 == 0 else p.pp1
                    for fi in range(FT):
                        t.matmul(ps[:, 0:SLB], wotile(fi, fo),
                                 o_b[:, fi * SLB:(fi + 1) * SLB],
                                 start=(fi == 0), stop=(fi == FT - 1))
                    t.last.then_inc(L["sPE"], 1)
                t.wait_ge(L["sPO"], pl["ln1sq"])
                for ft in range(FT):
                    t.matmul(p.pp0[:, 0:SLB], p.c_sw1024[:],
                             ao_b[:, ft * SLB:(ft + 1) * SLB],
                             start=(ft == 0), stop=(ft == FT - 1))
                for ft in range(FT):
                    t.matmul(p.pp1[:, 0:SLB], p.c_sw1024[:],
                             p.sq[:, ft * SLB:(ft + 1) * SLB],
                             start=(ft == 0), stop=(ft == FT - 1))
                t.last.then_inc(L["sPE"], 1)
        p.cnt["sPE"] = pe

        @block.scalar
        def _(sc, c=c, c1=c1, c2=c2, pl=pl, q1=q1, hasB1=hasB1, hasA=hasA,
              hasB=hasB, ve_end=ve):
            if hasB:
                sc.wait_ge(L["sPE"], pl["ln2mm"])
                sc.copy(mu_c[:], p.pp0[:, 0:SLB]).then_inc(L["sAE"], 1)
                sc.wait_ge(L["sVE"], pl["subln2"])
                sc.activation(rs_c[:], rs_c[:], AF.Ln, bias=LEPS)
                sc.activation(rs_c[:], rs_c[:], AF.Exp, scale=-0.5)
                sc.last.then_inc(L["sAE"], 1)
            if hasB1:
                for i in range(8):
                    sc.wait_ge(L["sPE"], pl["tr"][i])
                    dst = kpm if i < 4 else vpm
                    bb = (i % 4) // 2
                    fh = i % 2
                    sc.copy(dst[:, bb * FT * 128 + fh * 512:
                                bb * FT * 128 + fh * 512 + 512],
                            p.pp3[:, 0:512]).then_inc(L["sAE"], 1)
                for lvl in range(5):
                    sc.wait_ge(L["sPE"], pl["pair"][lvl])
                    sc.copy(ch["chP"][:], p.pw1[:, 0:CHW])
                    if lvl < 4:
                        sc.copy(ch["chPt"][:], p.pw1[:, 256:256 + CHW])
                    sc.last.then_inc(L["sAE"], 1)
                    if lvl < 4:
                        sc.wait_ge(L["sPE"], pl["qupd"][lvl])
                        sc.copy(ch["chQ"][:],
                                p.pw0[:, 0:CHW]).then_inc(L["sAE"], 1)
            if hasA:
                sc.wait_ge(L["sPE"], pl["rmsA"])
                if q1 and "scale" in q1:
                    sc.wait_ge(L["sVE"], q1["scale"])
                for hh in range(2):
                    ps = p.pp0 if hh == 0 else p.pp1
                    dst = [p.gs1, p.gs2][hh]
                    sc.activation(dst[:, 0:SLB], ps[:, 0:SLB], AF.Ln,
                                  bias=LEPS)
                    sc.activation(dst[:, 0:SLB], dst[:, 0:SLB], AF.Exp,
                                  scale=-0.5)
                sc.last.then_inc(L["sAE"], 1)
                sc.wait_ge(L["sPE"], pl["rmsB"])
                for hh in range(2):
                    ps = p.pp0 if hh == 0 else p.pp1
                    dst = [p.gs3, p.gs4][hh]
                    sc.activation(dst[:, 0:SLB], ps[:, 0:SLB], AF.Ln,
                                  bias=LEPS)
                    sc.activation(dst[:, 0:SLB], dst[:, 0:SLB], AF.Exp,
                                  scale=-0.5)
                sc.last.then_inc(L["sAE"], 1)
                for fo in range(FT):
                    sc.wait_ge(L["sPE"], pl["wo"][fo])
                    if fo == 0 and DEV_DNB and c == 2:
                        sc.wait_ge(L["sDB"], p.cnt["sDB"])
                    if fo == 0 and hasB:
                        sc.wait_ge(L["sVE"], pl["resid"])
                    ps = p.pp0 if fo % 2 == 0 else p.pp1
                    sc.copy(ao_b[:, fo * SLB:(fo + 1) * SLB],
                            ps[:, 0:SLB]).then_inc(L["sAE"], 1)
                sc.wait_ge(L["sPE"], pl["ln1mm"])
                if hasB:
                    sc.wait_ge(L["sVE"], pl["resid"])
                sc.copy(mu_a[:], p.pp0[:, 0:SLB]).then_inc(L["sAE"], 1)
                sc.wait_ge(L["sVE"], ve_end + 1)
                sc.activation(rs_a[:], rs_a[:], AF.Ln, bias=LEPS)
                sc.activation(rs_a[:], rs_a[:], AF.Exp, scale=-0.5)
                sc.last.then_inc(L["sAE"], 1)
        p.cnt["sAE"] = ac

        @block.vector
        def _(v, c=c, c1=c1, c2=c2, pl=pl, q1=q1, q2=q2, hasB1=hasB1,
              hasA=hasA, hasB=hasB):
            if hasB:
                v.wait_ge(L["sPO"], q1["mu2ln1"])
                v.wait_ge(L["sPE"], q1["ln1mm"])
                v.tensor_sub(rs_a[:], p.pp1[:, 0:SLB], rs_a[:])
                v.last.then_inc(L["sVE"], 1)
                v.wait_ge(L["sAE"], q1["lnexp1"])
                for ft in range(FT):
                    v.tensor_sub(lt_b[:], ao_b[:, ft * SLB:(ft + 1) * SLB],
                                 mu_a[:])
                    v.tensor_mul(lt_b[:], lt_b[:], rs_a[:])
                    v.tensor_add(stile(ft, c2 * SLB, SLB), lt_b[:],
                                 stile(ft, c2 * SLB, SLB))
                v.last.then_inc(L["sVE"], 1)
            if hasA:
                v.wait_ge(L["sPE"], pl["rowsum"])
                dm = p.c_dmask[:].unsqueeze(1).broadcast_to((128, CPS, 64))
                v.tensor_mul(b3(dm_b[:]), b3(p.pp0[:, 0:SLB]), dm)
                v.last.then_inc(L["sVE"], 1)
            if hasB1:
                v.wait_ge(L["sPE"], pl["beta"])
                if DEV_DNB and c == 1:
                    v.wait_ge(L["sDB"], p.cnt["sDB"])
                for ft in range(FT):
                    ps = p.pw1[:, (ft // 2) * SLB:(ft // 2 + 1) * SLB]
                    v.tensor_mul(kb_b[:, ft * SLB:(ft + 1) * SLB],
                                 slab(kn_b, c, ft), ps)
                    v.tensor_mul(vb_b[:, ft * SLB:(ft + 1) * SLB],
                                 slab(v_b, c, ft), ps)
                v.last.then_inc(L["sVE"], 1)
            if hasB:
                v.wait_ge(L["sPO"], pl["mu2ln2"])
                v.tensor_sub(rs_c[:], p.pp1[:, 0:SLB], rs_c[:])
                v.last.then_inc(L["sVE"], 1)
            if hasB1:
                v.wait_ge(L["sPE"], pl["aat"])
                bc = lambda x: x[:].unsqueeze(1).broadcast_to(
                    (128, NPAIR, 64))
                v.tensor_mul(b3(ch["chL"][:]), b3(p.pw0[:, 0:CHW]),
                             bc(p.c_maskL))
                v.tensor_mul(b3(ch["chLt"][:]), b3(p.pw0[:, 256:256 + CHW]),
                             bc(p.c_maskU))
                v.scalar_tensor_tensor(b3(ch["chB"][:]), b3(ch["chLt"][:]),
                                       -1.0, bc(p.c_eye1), ALU.mult, ALU.add)
                v.tensor_copy(ch["chQ"][:], ch["chB"][:])
                v.last.then_inc(L["sVE"], 1)
            if hasA:
                if DEV_DNB and c == 2:
                    v.wait_ge(L["sDB"], p.cnt["sDB"])
                if q1 and "wo" in q1:
                    v.wait_ge(L["sPE"], q1["wo"][7])
                    v.wait_ge(L["sPO"], q1["sqrms"])
                for ft in range(FT):
                    v.tensor_mul(p.gs3[:, 0:SLB], slab(kn_b, c1, ft),
                                 slab(w_b, c1, ft))
                    v.reduce_sum(aa_b[:, ft * CPS:(ft + 1) * CPS],
                                 b3(p.gs3[:, 0:SLB]), axis=AX.X)
                    v.tensor_mul(p.gs3[:, 0:SLB], slab(kn_b, c1, ft),
                                 slab(u_b, c1, ft))
                    v.reduce_sum(bb_b[:, ft * CPS:(ft + 1) * CPS],
                                 b3(p.gs3[:, 0:SLB]), axis=AX.X)
                v.tensor_scalar(aa_b[:], aa_b[:], -1.0, 1.0, ALU.mult,
                                ALU.add)
                sd = p.sdt[:].rearrange("p (ft cc) -> p ft cc", ft=FT)
                for ft in range(FT):
                    v.tensor_tensor_scan(sd[:, ft, 1:5],
                                         aa_b[:, ft * CPS:(ft + 1) * CPS],
                                         bb_b[:, ft * CPS:(ft + 1) * CPS],
                                         sd[:, ft, 0:1], ALU.mult, ALU.add)
                for ft in range(FT):
                    sdexp = sd[:, ft, 0:4].unsqueeze(2).broadcast_to(
                        (128, CPS, 64))
                    v.tensor_mul(b3(p.gs3[:, 0:SLB]),
                                 b3(slab(w_b, c1, ft)), sdexp)
                    v.tensor_sub(p.gs4[:, 0:SLB], slab(u_b, c1, ft),
                                 p.gs3[:, 0:SLB])
                    v.tensor_mul(p.gs4[:, 0:SLB], p.gs4[:, 0:SLB], dm_b[:])
                    v.tensor_mul(b3(p.gs3[:, 0:SLB]),
                                 b3(slab(qn_b, c1, ft)), sdexp)
                    v.tensor_add(o_b[:, ft * SLB:(ft + 1) * SLB],
                                 p.gs3[:, 0:SLB], p.gs4[:, 0:SLB])
                for ft in range(FT):
                    v.tensor_copy(sd[:, ft, 0:1], sd[:, ft, 4:5])
                v.last.then_inc(L["sVE"], 1)
            if hasB1:
                v.wait_ge(L["sPE"], pl["qupd"][4])
                e2 = p.c_eye2[:].unsqueeze(1).broadcast_to((128, NPAIR, 64))
                v.scalar_tensor_tensor(b3(tt_bf[:]), b3(p.pw0[:, 0:CHW]), 3.0,
                                       e2, ALU.mult, ALU.subtract)
                v.last.then_inc(L["sVE"], 1)
                for cc in range(CPS):
                    v.wait_ge(L["sPE"], pl["wu"][cc])
                    wps = (p.pw1[:, 0:512] if cc % 2 == 0
                           else p.pw1[:, 512:1024])
                    ups = (p.pw0[:, 512:1024] if cc % 2 == 0
                           else p.pp2[:, 0:512])
                    v.tensor_copy(
                        w_b[:].rearrange("p (sl ft t) -> p sl ft t", sl=2,
                                         ft=FT)[:, c % 2, :,
                                                cc * 64:(cc + 1) * 64],
                        wps.rearrange("p (ft k) -> p ft k", ft=FT))
                    v.tensor_copy(
                        u_b[:].rearrange("p (sl ft t) -> p sl ft t", sl=2,
                                         ft=FT)[:, c % 2, :,
                                                cc * 64:(cc + 1) * 64],
                        ups.rearrange("p (ft k) -> p ft k", ft=FT))
                    v.last.then_inc(L["sVE"], 1)
            if hasA:
                v.wait_ge(L["sAE"], pl["rstdB"])
                for ft in range(FT):
                    rs = [p.gs1, p.gs2, p.gs3, p.gs4][ft // 2]
                    v.scalar_tensor_tensor(o_b[:, ft * SLB:(ft + 1) * SLB],
                                           o_b[:, ft * SLB:(ft + 1) * SLB],
                                           p.g_pp[:, (ft % 2):(ft % 2) + 1],
                                           rs[:, 0:SLB], ALU.mult, ALU.mult)
                v.last.then_inc(L["sVE"], 1)
            if hasB:
                v.wait_ge(L["sAE"], pl["lnexp2"])
                v.wait_ge(L["sSP"], pl["x1st"])
                v.tensor_mul(lt_b[:], mu_c[:], rs_c[:])
                for ft in range(FT):
                    v.tensor_mul(stile(ft, c2 * SLB, SLB),
                                 stile(ft, c2 * SLB, SLB), rs_c[:])
                    v.tensor_sub(stile(ft, c2 * SLB, SLB),
                                 stile(ft, c2 * SLB, SLB), lt_b[:])
                v.last.then_inc(L["sVE"], 1)
        p.cnt["sVE"] = ve

    fence(p, block)
    es.close()


def emit_ffn_attn(p, block, din, scr, layer, stile):
    nc = p.nc
    L = p.sems
    es = ExitStack()
    wqkv_sb = es.enter_context(nc.sbuf_tensor(f"wqkv_sb{layer}", [128, FT * 12 * 128],
                                              BF16))
    fwo_sb = es.enter_context(nc.sbuf_tensor(f"fwo_sb{layer}", [128, 4 * FT * 128],
                                             BF16))
    tbl = es.enter_context(nc.sbuf_tensor(f"tbl{layer}", [128, 4 * 512], BF16))
    qw = es.enter_context(nc.sbuf_tensor(f"qw{layer}", [128, 4 * 512], BF16))
    qraw = es.enter_context(nc.sbuf_tensor(f"qraw{layer}", [128, 2 * 4 * 512], BF16))
    kr = es.enter_context(nc.sbuf_tensor(f"kr{layer}", [128, 3 * 4 * 512], BF16))
    vtm = es.enter_context(nc.sbuf_tensor(f"vtm{layer}", [128, 3 * 4 * 512], BF16))
    prT = es.enter_context(nc.sbuf_tensor(f"prT{layer}", [128, 8 * 512], BF16))
    o_n = es.enter_context(nc.sbuf_tensor(f"o_n{layer}", [128, 4 * 512], BF16))
    wo_buf = es.enter_context(nc.sbuf_tensor(f"wo_buf{layer}", [128, FT * 512], BF16))
    negmax = es.enter_context(nc.sbuf_tensor(f"negmax{layer}", [128, 8], F32))
    negrow = es.enter_context(nc.sbuf_tensor(f"negrow{layer}", [1, 512], BF16))
    p.c_mqk = es.enter_context(nc.sbuf_tensor(f"mqk{layer}", [128, 4096],
                                               BF16))
    p.c_mt = es.enter_context(nc.sbuf_tensor(f"mt{layer}", [128, 4096], BF16))
    p.gs5 = es.enter_context(nc.sbuf_tensor(f"gs5{layer}", [128, 512], F32))
    p.gs6 = es.enter_context(nc.sbuf_tensor(f"gs6{layer}", [128, 512], F32))
    p.gs7 = es.enter_context(nc.sbuf_tensor(f"gs7{layer}", [128, 512], F32))

    def krs(w3, ft):
        return kr[:, (w3 * 4 + ft) * 512:(w3 * 4 + ft + 1) * 512]

    def vts(w3, tb):
        return vtm[:, (w3 * 4 + tb) * 512:(w3 * 4 + tb + 1) * 512]

    def qrs(w2, ft):
        return qraw[:, (w2 * 4 + ft) * 512:(w2 * 4 + ft + 1) * 512]

    def qws(w2, ft):
        return qw[:, ft * 512:(ft + 1) * 512]

    @block.gpsimd
    def _(g):
        p.w(g, "sPE")
        p.w(g, "sVE")
        p.w(g, "sAE")
        g.dma_start(out=wqkv_sb[:].rearrange("p (a b q) -> p a b q", a=FT,
                                             b=12),
                    in_=din["fwqkv"][layer].rearrange("(a p) (b q) -> p a b q",
                                                      p=128, q=128)
                    ).then_inc(L["sWT"], 16)
        g.dma_start(out=fwo_sb[:].rearrange("p (a b q) -> p a b q", a=4, b=FT),
                    in_=din["fwo"][layer].rearrange("(a p) (b q) -> p a b q",
                                                    p=128, q=128)
                    ).then_inc(L["sWT"], 16)
    p.bump("sWT", 32)
    wt_w = p.cnt["sWT"]

    @block.sync
    def _(sy):
        p.w(sy, "sPE")
        p.w(sy, "sVE")
        p.w(sy, "sAE")
        sy.dma_start(out=p.c_mqk[:], in_=din["mqk"]).then_inc(L["sLD"], 16)
        sy.dma_start(out=p.c_mt[:], in_=din["mt"]).then_inc(L["sLD"], 16)
    p.bump("sLD", 32)
    msk_w = p.cnt["sLD"]

    def qkvtile(fi, fo):
        return wqkv_sb[:].rearrange("p (a b q) -> p a b q", a=FT,
                                    b=12)[:, fi, fo, :]

    def wotile(fi, fo):
        return fwo_sb[:].rearrange("p (a b q) -> p a b q", a=4,
                                   b=FT)[:, fi, fo, :]

    NWE = DEV_NW if DEV_NW else NW

    def emit_proj(w, pre_counts=None):
        t0 = w * WIN
        w2, w3 = w % 2, w % 3
        if pre_counts is None:
            pre_counts = [(nm, p.cnt[nm]) for nm in ("sVE", "sAE")]

        def slot(i):
            return p.pw0[:, 0:512] if i % 2 == 0 else p.pw0[:, 512:1024]

        def hsl(fi, tt, n):
            off = 0 if fi < 4 else -1
            return stile(fi, tt + off, n)

        def mm_qk(t, i):
            j, fo = divmod(i, 4)
            for fi in range(FT):
                t.matmul(slot(i), qkvtile(fi, 4 * j + fo),
                         hsl(fi, t0, WIN), start=(fi == 0), stop=(fi == FT - 1))

        def ev_qk(e, i):
            j, fo = divmod(i, 4)
            dst = qrs(w2, fo) if j == 0 else krs(w3, fo)
            return e.copy(dst, slot(i))

        if w == 0:
            @block.tensor
            def _(t):
                t.wait_ge(L["sWT"], wt_w)
        pipe(p, block, 8, mm_qk, ev_qk, "scalar", depth=2,
             pre_counts=pre_counts)

        def mm_v(t, tb):
            for fi in range(FT):
                off = 0 if fi < 4 else -1
                t.matmul(slot(tb), stile(fi, t0 + tb * 128 + off, 128),
                         wqkv_sb[:].rearrange("p (a bq) -> p a bq", a=FT)
                         [:, fi, 8 * 128:12 * 128],
                         start=(fi == 0), stop=(fi == FT - 1))

        def ev_v(e, tb):
            return e.copy(vts(w3, tb), slot(tb))

        pipe(p, block, 4, mm_v, ev_v, "scalar", depth=2, pre_wait=("sAE",))

        @block.gpsimd
        def _(g):
            p.w(g, "sAE")
            for i in range(8):
                j, fo = divmod(i, 4)
                srcb = qrs(w2, fo) if j == 0 else krs(w3, fo)
                g.tensor_mul(p.sq[:, i * 512:(i + 1) * 512], srcb, srcb)
            g.last.then_inc(L["sPO"], 1)
        p.bump("sPO")

    for w in range(NWE):
        t0 = w * WIN
        w2, w3 = w % 2, w % 3
        w3p = (w - 1) % 3
        first = (w == 0)

        @block.gpsimd
        def _(g, t0=t0):
            p.w(g, "sVE")
            for i, nm in enumerate(("csq", "ssq", "csk", "ssk")):
                g.dma_start(out=tbl[:, i * 512:(i + 1) * 512],
                            in_=din[nm][:, t0:t0 + WIN]).then_inc(L["sLG"], 16)
        p.bump("sLG", 64)
        tbl_w = p.cnt["sLG"]

        if w == 0:
            emit_proj(0)

        peb_r = p.cnt["sPE"]
        aeb_r = p.cnt["sAE"]
        veb_r = p.cnt["sVE"]
        pob_r = p.cnt["sPO"]

        @block.tensor
        def _(t, w2=w2, w3=w3):
            p.w(t, "sPO", pob_r)
            p.w(t, "sAE", aeb_r)
            p.w(t, "sVE", veb_r)
            for j in range(2):
                for ft in range(4):
                    ps = [p.pp0, p.pp1, p.pp2, p.pp3][ft]
                    t.matmul(ps[:, 0:512], p.c_blk64[:],
                             p.sq[:, (4 * j + ft) * 512:(4 * j + ft + 1) * 512],
                             start=True, stop=True)
                t.last.then_inc(L["sPE"], 1)
                for ft in range(4):
                    src = qrs(w2, ft) if j == 0 else krs(w3, ft)
                    half = (j * 4 + ft) % 2
                    if j * 4 + ft >= 2:
                        t.wait_ge(L["sVE"], veb_r + (j * 4 + ft) - 1)
                    t.matmul(p.pw1[:, half * 512:half * 512 + 512],
                             p.c_prot[:], src, start=True, stop=True)
                    t.last.then_inc(L["sPE"], 1)
        p.cnt["sPE"] += 2 * 5

        @block.scalar
        def _(sc):
            for j in range(2):
                sc.wait_ge(L["sPE"], peb_r + j * 5 + 1)
                for ft in range(4):
                    ps = [p.pp0, p.pp1, p.pp2, p.pp3][ft]
                    dst = [p.gs1, p.gs2, p.gs3, p.gs4][ft]
                    sc.activation(dst[:, 0:512], ps[:, 0:512], AF.Ln)
                    sc.activation(dst[:, 0:512], dst[:, 0:512], AF.Exp,
                                  scale=-0.5)
                sc.last.then_inc(L["sAE"], 1)
        p.cnt["sAE"] += 2

        @block.vector
        def _(v, w2=w2, w3=w3, tbl_w=tbl_w):
            v.wait_ge(L["sLG"], tbl_w)
            for j in range(2):
                spp = p.qs_pp if j == 0 else p.ks_pp
                srpp = p.qsr_pp if j == 0 else p.ksr_pp
                csoff = 0 if j == 0 else 1024
                ssoff = 512 if j == 0 else 1536
                v.wait_ge(L["sAE"], aeb_r + j + 1)
                for ft in range(4):
                    rs = [p.gs1, p.gs2, p.gs3, p.gs4][ft]
                    v.tensor_scalar_min(rs[:, 0:512], rs[:, 0:512], 1e12)
                    src = qrs(w2, ft) if j == 0 else krs(w3, ft)
                    dst = qws(w2, ft) if j == 0 else src
                    half = (j * 4 + ft) % 2
                    v.wait_ge(L["sPE"], peb_r + j * 5 + 1 + ft + 1)
                    v.scalar_tensor_tensor(p.gs5[:, 0:512], rs[:, 0:512],
                                           spp[:, layer:layer + 1],
                                           tbl[:, csoff:csoff + 512],
                                           ALU.mult, ALU.mult)
                    v.scalar_tensor_tensor(p.gs6[:, 0:512], rs[:, 0:512],
                                           srpp[:, layer:layer + 1],
                                           tbl[:, ssoff:ssoff + 512],
                                           ALU.mult, ALU.mult)
                    v.tensor_mul(p.gs7[:, 0:512], src, p.gs5[:, 0:512])
                    v.tensor_mul(p.gs5[:, 0:512],
                                 p.pw1[:, half * 512:half * 512 + 512],
                                 p.gs6[:, 0:512])
                    v.tensor_add(dst, p.gs7[:, 0:512],
                                 p.gs5[:, 0:512]).then_inc(L["sVE"], 1)
        p.cnt["sVE"] += 8

        if w + 1 < NWE:
            emit_proj(w + 1, pre_counts=[("sVE", veb_r), ("sAE", aeb_r)])

        if w > 0:
            @block.vector
            def _(v, t0=t0):
                p.w(v, "sPE")
                p.w(v, "sAE")
                for ft in range(FT):
                    v.tensor_add(stile(ft, t0 - WIN, WIN),
                                 stile(ft, t0 - WIN, WIN),
                                 wo_buf[:, ft * 512:(ft + 1) * 512])
                v.last.then_inc(L["sVE"], 1)
            p.bump("sVE")

        if w == 1 or (NWE == 1 and w == 0):
            for side, (srcn, dstn) in enumerate((("w1a", "w1a_bf"),
                                                 ("w1g", "w1g_bf"))):
                for piece in range(8):
                    @block.gpsimd
                    def _(g, srcn=srcn, piece=piece, side=side):
                        if piece == 0 and side == 0:
                            p.w(g, "sPE")
                            p.w(g, "sAE")
                            p.w(g, "sVE")
                        g.wait_ge(L["sSP"], p.cnt["sSP"])
                        g.dma_start(out=prT[:, 0:HID2],
                                    in_=din[srcn][layer].rearrange(
                                        "(a p) h -> p a h", p=128)[:, piece, :]
                                    ).then_inc(L["sLG"], 16)
                    p.bump("sLG", 16)

                    @block.sync
                    def _(sy, dstn=dstn, piece=piece):
                        p.w(sy, "sLG")
                        sy.dma_start(out=scr[dstn][layer].rearrange(
                            "(a p) h -> p a h", p=128)[:, piece, :],
                            in_=prT[:, 0:HID2]).then_inc(L["sSP"], 16)
                    p.bump("sSP", 16)
            p.w1cast_done = p.cnt["sSP"]

        heads = [] if DEV_SKIP_HEADS else list(range(H2))
        kbs = list(range(8)) if not first else list(range(4, 8))
        lenk = len(kbs)
        peb = p.cnt["sPE"]
        veb = p.cnt["sVE"]
        aeb = p.cnt["sAE"]
        pe_per = 4 + 1 + lenk + 1 + 1
        ae_per = lenk + 1

        def ve_cnt(h):
            full = 0
            for hh in range(h + 1):
                full += 5 + (1 if hh % 2 == 1 else 0)
            return veb + full

        if heads:
            @block.tensor
            def _(t, w2=w2, w3=w3, w3p=w3p, par_first=first, kbs=kbs,
                  lenk=lenk, peb=peb, veb=veb, aeb=aeb, pe_per=pe_per,
                  ae_per=ae_per, ve_cnt=ve_cnt, w=w):
                p.w(t, "sVE", veb)
                p.w(t, "sAE", aeb)
                if w == 0:
                    t.wait_ge(L["sLD"], msk_w)
                for h in range(H2):
                    ftq = h // 2
                    rb = (h % 2) * 64
                    hb = peb + h * pe_per
                    vh = ve_cnt(h - 1) if h > 0 else veb
                    for qt in range(4):
                        g = h * 4 + qt
                        if g >= 2:
                            h2, qt2 = divmod(g - 2, 4)
                            t.wait_ge(L["sVE"],
                                      (ve_cnt(h2 - 1) if h2 > 0 else veb)
                                      + qt2 + 1)
                        if qt % 2 == 1 and h > 0:
                            t.wait_ge(L["sAE"], aeb + h * ae_per)
                        if par_first:
                            ranges = ((512, w3),)
                        else:
                            ranges = ((0, w3p), (512, w3))
                        for koff, kp in ranges:
                            if qt % 2 == 0:
                                dst = p.pw0[:, koff:koff + 512]
                            elif par_first or koff == 512:
                                dst = p.pp3[:, 0:512]
                            else:
                                dst = p.pp2[:, 0:512]
                            t.matmul(dst, p.c_id_bf[:],
                                     p.c_mqk[:, qt * 1024 + koff:
                                             qt * 1024 + koff + 512],
                                     start=True, stop=False)
                            t.matmul(dst,
                                     qw[rb:rb + 64,
                                        ftq * 512 + qt * 128:
                                        ftq * 512 + (qt + 1) * 128],
                                     krs(kp, ftq)[rb:rb + 64, :],
                                     start=False, stop=True,
                                     tile_position=(rb, 0))
                        t.last.then_inc(L["sPE"], 1)
                    t.wait_ge(L["sVE"], vh + 4)
                    if h > 0:
                        t.wait_ge(L["sAE"], aeb + h * ae_per)
                    for qt in range(4):
                        t.matmul(p.pp2[0:1, qt * 128:(qt + 1) * 128],
                                 negmax[:, qt:qt + 1], p.c_id_f[:],
                                 start=(qt == 0), stop=(qt == 3))
                    t.last.then_inc(L["sPE"], 1)
                    for i, kb in enumerate(kbs):
                        gi = h * lenk + i
                        if gi >= 2:
                            t.wait_ge(L["sAE"], aeb + (gi - 2) + 1 +
                                      ((gi - 2) // lenk))
                        if i == 0:
                            t.wait_ge(L["sVE"], vh + 5)
                        ps = p.pp0 if i % 2 == 0 else p.pp1
                        t.matmul(ps[:, 0:512], p.c_id_bf[:],
                                 p.c_mt[:, kb * 512:(kb + 1) * 512],
                                 start=True, stop=False)
                        t.matmul(ps[:, 0:512], p.c_ones1[:], negrow[0:1, :],
                                 start=False, stop=False)
                        kp = w3p if kb < 4 else w3
                        ko = (kb % 4) * 128
                        t.matmul(ps[:, 0:512],
                                 krs(kp, ftq)[rb:rb + 64, ko:ko + 128],
                                 qw[rb:rb + 64,
                                    ftq * 512:(ftq + 1) * 512],
                                 start=False, stop=True, tile_position=(rb, 0))
                        t.last.then_inc(L["sPE"], 1)
                    t.wait_ge(L["sAE"], aeb + h * ae_per + lenk)
                    for i, kb in enumerate(kbs):
                        t.matmul(p.pp2[:, 0:512], p.c_ones[:],
                                 prT[:, kb * 512:(kb + 1) * 512],
                                 start=(i == 0), stop=(i == lenk - 1))
                    t.last.then_inc(L["sPE"], 1)
                    if h % 2 == 0 and h > 0:
                        t.wait_ge(L["sVE"], ve_cnt(h - 1))
                    for i, kb in enumerate(kbs):
                        vp = w3p if kb < 4 else w3
                        vb = kb % 4
                        t.matmul(p.pw1[rb:rb + 64, 0:512],
                                 vts(vp, vb)[:, h * 64:(h + 1) * 64],
                                 prT[:, kb * 512:(kb + 1) * 512],
                                 start=(i == 0), stop=(i == lenk - 1),
                                 tile_position=(0, rb))
                    t.last.then_inc(L["sPE"], 1)
            p.cnt["sPE"] += H2 * pe_per

            @block.vector
            def _(v, par_first=first, lenk=lenk, peb=peb, pe_per=pe_per,
                  aeb=aeb, ae_per=ae_per):
                for h in range(H2):
                    hb = peb + h * pe_per
                    for qt in range(4):
                        v.wait_ge(L["sPE"], hb + qt + 1)
                        if qt % 2 == 0:
                            src_ = (p.pw0[:, 512:1024] if par_first else
                                    p.pw0[:, 0:1024])
                            v.reduce_max(negmax[:, qt:qt + 1], src_, axis=AX.X,
                                         negate=True).then_inc(L["sVE"], 1)
                        elif par_first:
                            v.reduce_max(negmax[:, qt:qt + 1], p.pp3[:, 0:512],
                                         axis=AX.X,
                                         negate=True).then_inc(L["sVE"], 1)
                        else:
                            v.reduce_max(negmax[:, 4:5], p.pp2[:, 0:512],
                                         axis=AX.X, negate=True)
                            v.reduce_max(negmax[:, 5:6], p.pp3[:, 0:512],
                                         axis=AX.X, negate=True)
                            v.tensor_tensor(negmax[:, qt:qt + 1],
                                            negmax[:, 4:5], negmax[:, 5:6],
                                            ALU.min).then_inc(L["sVE"], 1)
                    v.wait_ge(L["sPE"], hb + 5)
                    v.tensor_copy(negrow[0:1, :],
                                  p.pp2[0:1, 0:512]).then_inc(L["sVE"], 1)
                    if h % 2 == 1:
                        ftq = h // 2
                        v.wait_ge(L["sPE"], hb + pe_per)
                        v.wait_ge(L["sAE"], aeb + (h + 1) * ae_per)
                        v.tensor_mul(o_n[:, ftq * 512:(ftq + 1) * 512],
                                     p.pw1[:, 0:512],
                                     p.gs7[:, 0:512]).then_inc(L["sVE"], 1)
            p.cnt["sVE"] = ve_cnt(H2 - 1)

            @block.scalar
            def _(sc, w=w, kbs=kbs, lenk=lenk, peb=peb, pe_per=pe_per,
                  ve_cnt=ve_cnt):
                if w == 1:
                    sc.wait_ge(L["sSP"], p.w1cast_done)
                for h in range(H2):
                    hb = peb + h * pe_per
                    rb = (h % 2) * 64
                    for i, kb in enumerate(kbs):
                        sc.wait_ge(L["sPE"], hb + 5 + i + 1)
                        sc.activation(prT[:, kb * 512:(kb + 1) * 512],
                                      (p.pp0 if i % 2 == 0 else p.pp1)
                                      [:, 0:512],
                                      AF.Exp).then_inc(L["sAE"], 1)
                    sc.wait_ge(L["sPE"], hb + 5 + lenk + 1)
                    if h >= 2:
                        sc.wait_ge(L["sVE"], ve_cnt(h - 1))
                    sc.activation(p.gs7[rb:rb + 64, 0:512],
                                  p.pp2[rb:rb + 64, 0:512], AF.Ln)
                    sc.activation(p.gs7[rb:rb + 64, 0:512],
                                  p.gs7[rb:rb + 64, 0:512],
                                  AF.Exp, scale=-1.0).then_inc(L["sAE"], 1)
            p.cnt["sAE"] += H2 * ae_per

        def mm_wo2(t, fo):
            ps = p.pw0 if fo % 2 == 0 else p.pp0
            for fi in range(4):
                t.matmul(ps[:, 0:512], wotile(fi, fo),
                         o_n[:, fi * 512:(fi + 1) * 512],
                         start=(fi == 0), stop=(fi == 3))

        def ev_wo2(e, fo):
            ps = p.pw0 if fo % 2 == 0 else p.pp0
            return e.copy(wo_buf[:, fo * 512:(fo + 1) * 512], ps[:, 0:512])

        if not DEV_SKIP_HEADS:
            pipe(p, block, FT, mm_wo2, ev_wo2, "scalar", depth=2,
                 pre_wait=("sVE", "sAE"))

    if not DEV_SKIP_HEADS:
        @block.vector
        def _(v):
            p.w(v, "sAE")
            for ft in range(FT):
                v.tensor_add(stile(ft, (NWE - 1) * WIN, WIN),
                             stile(ft, (NWE - 1) * WIN, WIN),
                             wo_buf[:, ft * 512:(ft + 1) * 512])
            v.last.then_inc(L["sVE"], 1)
        p.bump("sVE")
    fence(p, block)
    es.close()


def emit_ffn_mlp(p, block, din, scr, layer, stile):
    nc = p.nc
    L = p.sems
    es = ExitStack()
    h2n = es.enter_context(nc.sbuf_tensor(f"h2n{layer}", [128, FT * TB], BF16))
    mbuf = es.enter_context(nc.sbuf_tensor(f"mbuf{layer}", [128, NS1 * TB], BF16))
    w2res = es.enter_context(nc.sbuf_tensor(f"w2res{layer}", [128, NS1 * FT * 128],
                                            BF16))
    w2buf = es.enter_context(nc.sbuf_tensor(f"w2buf{layer}", [128, FT * TB], BF16))
    p.stat_mu = es.enter_context(nc.sbuf_tensor(f"smu{layer}", [128, 2 * TB], F32))
    p.stat_rs = es.enter_context(nc.sbuf_tensor(f"srs{layer}", [128, 2 * TB], F32))
    slab = es.enter_context(nc.sbuf_tensor(f"slab{layer}", [128, 2 * 2 * FT * 128],
                                           BF16))
    gelb = es.enter_context(nc.sbuf_tensor(f"gelb{layer}", [128, 2 * TB], BF16))

    @block.gpsimd
    def _(g):
        p.w(g, "sPE")
        p.w(g, "sVE")
        p.w(g, "sAE")
        g.dma_start(out=w2res[:].rearrange("p (s f q) -> p s f q", s=NS1, f=FT),
                    in_=din["w2p"][layer].rearrange("(s p) (f q) -> p s f q",
                                                    p=128, q=128)
                    ).then_inc(L["sWT"], 16)
    p.bump("sWT", 16)
    w2_w = p.cnt["sWT"]

    def w2tile(s, fo):
        return w2res[:].rearrange("p (s f q) -> p s f q", s=NS1,
                                  f=FT)[:, s, fo, :]

    def slab_tile(par, side, fi):
        return slab[:].rearrange("p (r d f q) -> p r d f q", r=2, d=2,
                                 f=FT)[:, par, side, fi, :]

    def load_slab(sy, s, par):
        for side, src in ((0, "w1a_bf"), (1, "w1g_bf")):
            sy.dma_start(
                out=slab[:].rearrange("p (r d f q) -> p r d f q", r=2, d=2,
                                      f=FT)[:, par, side, :, :],
                in_=scr[src][layer].rearrange("(f p) (s q) -> p f s q",
                                              p=128, q=128)[:, :, s, :]
                ).then_inc(L["sSP"], 16)
            p.bump("sSP", 16)

    mu2 = [p.stat_mu[:, 0:TB], p.stat_mu[:, TB:2 * TB]]
    rs2 = [p.stat_rs[:, 0:TB], p.stat_rs[:, TB:2 * TB]]

    def rhs_sh_tb(tb):
        def f(ft, t0=tb * TB):
            off = 0 if ft < 4 else -1
            return stile(ft, t0 + off, TB)
        return f

    for tb in range(NTB):
        t0 = tb * TB
        first_tb = (tb == 0)
        sslot = tb % 2

        if tb == 0:
            ln_stats_sb(p, block, rhs_sh_tb(0), TB, mu2[0], rs2[0],
                        p.pw1[:, 0:512], p.pw1[:, 512:1024], sq_off=0)

        @block.vector
        def _(v, t0=t0, sslot=sslot, h2c=h2n):
            p.w(v, "sAE")
            for ft in range(FT):
                off = 0 if ft < 4 else -1
                v.tensor_sub(p.gs3[:, 0:TB], stile(ft, t0 + off, TB),
                             mu2[sslot])
                v.scalar_tensor_tensor(h2c[:, ft * TB:(ft + 1) * TB],
                                       p.gs3[:, 0:TB],
                                       p.lng_pp[:, layer * FT + ft:
                                                layer * FT + ft + 1],
                                       rs2[sslot], ALU.mult, ALU.mult)
            v.last.then_inc(L["sVE"], 1)
        p.bump("sVE")

        if tb > 0:
            @block.vector
            def _(v, t0=t0):
                for ft in range(FT):
                    v.tensor_add(stile(ft, t0 - TB, TB),
                                 stile(ft, t0 - TB, TB),
                                 w2buf[:, ft * TB:(ft + 1) * TB])
                v.last.then_inc(L["sVE"], 1)
            p.bump("sVE")

        sp_targets = []

        @block.sync
        def _(sy, tb=tb):
            sy.wait_ge(L["sSP"], p.w1cast_done)
            p.w(sy, "sPE")
            p.w(sy, "sVE")
            p.w(sy, "sAE")
            load_slab(sy, 0, 0)
        sp_targets.append(p.cnt["sSP"])

        pe0 = p.cnt["sPE"]
        ae0 = p.cnt["sAE"]
        ve0 = p.cnt["sVE"]
        if first_tb:
            h2c = h2n
            for s in range(NS1):
                par = s % 2
                aps = p.pp0 if par == 0 else p.pp2
                gps = p.pp1 if par == 0 else p.pw0
                gel = gelb[:, par * TB:(par + 1) * TB]

                if s + 1 < NS1:
                    @block.sync
                    def _(sy, s=s, par=par):
                        p.w(sy, "sPE")
                        load_slab(sy, s + 1, par ^ 1)
                    sp_targets.append(p.cnt["sSP"])

                @block.tensor
                def _(t, s=s, par=par, aps=aps, gps=gps, h2c=h2c):
                    t.wait_ge(L["sSP"], sp_targets[s])
                    if s == 0:
                        t.wait_ge(L["sWT"], w2_w)
                    p.w(t, "sVE")
                    p.w(t, "sAE")
                    for fi in range(FT):
                        t.matmul(aps[:, 0:TB], slab_tile(par, 0, fi),
                                 h2c[:, fi * TB:(fi + 1) * TB],
                                 start=(fi == 0), stop=(fi == FT - 1))
                    for fi in range(FT):
                        t.matmul(gps[:, 0:TB], slab_tile(par, 1, fi),
                                 h2c[:, fi * TB:(fi + 1) * TB],
                                 start=(fi == 0), stop=(fi == FT - 1))
                    for side, ps in ((0, p.pp1 if par else p.pp2),
                                     (1, p.pw1)):
                        for fi in range(FT):
                            t.matmul(ps[0:1, 0:128],
                                     p.lnb_bf[:, layer * FT + fi:
                                              layer * FT + fi + 1],
                                     slab_tile(par, side, fi),
                                     start=(fi == 0), stop=(fi == FT - 1))
                    t.last.then_inc(L["sPE"], 1)
                p.bump("sPE")

                @block.vector
                def _(v, par=par):
                    p.w(v, "sPE")
                    v.tensor_copy(p.rowtmp[0:1, :],
                                  (p.pp1 if par else p.pp2)[0:1, 0:128])
                    v.last.then_inc(L["sVE"], 1)
                p.bump("sVE")

                @block.tensor
                def _(t):
                    p.w(t, "sVE")
                    t.matmul(p.pw1[:, 512:513], p.rowtmp[0:1, :],
                             p.c_ones1[0:1, 0:1], start=True, stop=True)
                    t.last.then_inc(L["sPE"], 1)
                p.bump("sPE")

                @block.vector
                def _(v, s=s):
                    p.w(v, "sPE")
                    v.tensor_add(p.b1ae[:, s:s + 1], p.pw1[:, 512:513],
                                 p.b1ac[:, layer * NS1 + s:layer * NS1 + s + 1])
                    v.tensor_copy(p.rowtmp[0:1, :], p.pw1[0:1, 0:128])
                    v.last.then_inc(L["sVE"], 1)
                p.bump("sVE")

                @block.tensor
                def _(t):
                    p.w(t, "sVE")
                    t.matmul(p.pw1[:, 512:513], p.rowtmp[0:1, :],
                             p.c_ones1[0:1, 0:1], start=True, stop=True)
                    t.last.then_inc(L["sPE"], 1)
                p.bump("sPE")

                @block.vector
                def _(v, s=s):
                    p.w(v, "sPE")
                    v.tensor_add(p.b1ge[:, s:s + 1], p.pw1[:, 512:513],
                                 p.b1gc[:, layer * NS1 + s:layer * NS1 + s + 1])
                    v.last.then_inc(L["sVE"], 1)
                p.bump("sVE")

                @block.scalar
                def _(sc, s=s, gps=gps, gel=gel):
                    p.w(sc, "sPE")
                    p.w(sc, "sVE")
                    sc.activation(gel, gps[:, 0:TB], AF.Gelu,
                                  bias=p.b1ge[:, s:s + 1])
                    sc.last.then_inc(L["sAE"], 1)
                p.bump("sAE")

                @block.vector
                def _(v, s=s, aps=aps, gel=gel):
                    p.w(v, "sAE")
                    v.scalar_tensor_tensor(mbuf[:, s * TB:(s + 1) * TB],
                                           aps[:, 0:TB], p.b1ae[:, s:s + 1],
                                           gel, ALU.add, ALU.mult)
                    v.last.then_inc(L["sVE"], 1)
                p.bump("sVE")
        else:
            @block.sync
            def _(sy):
                sy.wait_ge(L["sSP"], p.w1cast_done)
                for s in range(NS1 - 1):
                    sy.wait_ge(L["sPE"], pe0 + s)
                    load_slab(sy, s + 1, (s + 1) % 2)
            for s in range(NS1 - 1):
                sp_targets.append(sp_targets[0] + 32 * (s + 1))

            h2c = h2n

            @block.tensor
            def _(t, h2c=h2c):
                for s in range(NS1):
                    par = s % 2
                    aps = p.pp0 if par == 0 else p.pp2
                    gps = p.pp1 if par == 0 else p.pw0
                    t.wait_ge(L["sSP"], sp_targets[s])
                    if s == 0:
                        t.wait_ge(L["sWT"], w2_w)
                        p.w(t, "sVE", ve0)
                        p.w(t, "sAE", ae0)
                    else:
                        t.wait_ge(L["sAE"], ae0 + max(0, s - 1))
                        t.wait_ge(L["sVE"], ve0 + max(0, s - 1))
                    for fi in range(FT):
                        t.matmul(aps[:, 0:TB], slab_tile(par, 0, fi),
                                 h2c[:, fi * TB:(fi + 1) * TB],
                                 start=(fi == 0), stop=(fi == FT - 1))
                    for fi in range(FT):
                        t.matmul(gps[:, 0:TB], slab_tile(par, 1, fi),
                                 h2c[:, fi * TB:(fi + 1) * TB],
                                 start=(fi == 0), stop=(fi == FT - 1))
                    t.last.then_inc(L["sPE"], 1)
            p.cnt["sPE"] += NS1

            @block.scalar
            def _(sc):
                for s in range(NS1):
                    gps = p.pp1 if s % 2 == 0 else p.pw0
                    gel = gelb[:, (s % 2) * TB:(s % 2 + 1) * TB]
                    sc.wait_ge(L["sPE"], pe0 + s + 1)
                    sc.activation(gel, gps[:, 0:TB], AF.Gelu,
                                  bias=p.b1ge[:, s:s + 1]).then_inc(L["sAE"], 1)
            p.cnt["sAE"] += NS1

            @block.vector
            def _(v):
                for s in range(NS1):
                    aps = p.pp0 if s % 2 == 0 else p.pp2
                    gel = gelb[:, (s % 2) * TB:(s % 2 + 1) * TB]
                    v.wait_ge(L["sAE"], ae0 + s + 1)
                    v.scalar_tensor_tensor(mbuf[:, s * TB:(s + 1) * TB],
                                           aps[:, 0:TB], p.b1ae[:, s:s + 1],
                                           gel, ALU.add,
                                           ALU.mult).then_inc(L["sVE"], 1)
            p.cnt["sVE"] += NS1

        if tb + 1 < NTB:
            ln_stats_sb(p, block, rhs_sh_tb(tb + 1), TB,
                        mu2[(tb + 1) % 2], rs2[(tb + 1) % 2],
                        p.pw1[:, 0:512], p.pw1[:, 512:1024], sq_off=0)

        def mm_w2(t, fo):
            ps = p.pp0 if fo % 2 == 0 else p.pp1
            for s in range(NS1):
                t.matmul(ps[:, 0:TB], w2tile(s, fo),
                         mbuf[:, s * TB:(s + 1) * TB],
                         start=(s == 0), stop=(s == NS1 - 1))

        def ev_w2(e, fo):
            ps = p.pp0 if fo % 2 == 0 else p.pp1
            return e.tensor_scalar_add(w2buf[:, fo * TB:(fo + 1) * TB],
                                       ps[:, 0:TB],
                                       p.b2_pp[:, layer * FT + fo:
                                               layer * FT + fo + 1])

        pipe(p, block, FT, mm_w2, ev_w2, "vector", depth=2, pre_wait=("sVE",))

        fence(p, block)

    @block.vector
    def _(v):
        for ft in range(FT):
            v.tensor_add(stile(ft, (NTB - 1) * TB, TB),
                         stile(ft, (NTB - 1) * TB, TB),
                         w2buf[:, ft * TB:(ft + 1) * TB])
        v.last.then_inc(L["sVE"], 1)
    p.bump("sVE")
    fence(p, block)
    es.close()


def emit_final(p, block, scr, out_d, stile):
    nc = p.nc
    L = p.sems
    es = ExitStack()
    x1b = es.enter_context(nc.sbuf_tensor("x1b", [128, FT * 512], BF16))
    outf = es.enter_context(nc.sbuf_tensor("outf", [128, FT * 512], F32))

    for s in range(8):
        t0 = s * 512

        @block.sync
        def _(sy, s=s):
            p.w(sy, "sVE")
            p.w(sy, "sPE")
            p.w(sy, "sAE")
            sy.dma_start(out=x1b[:].rearrange("p (ft t) -> p ft t", ft=FT),
                         in_=scr["x1_s"][:, :, s * 512:(s + 1) * 512]
                         ).then_inc(L["sLD"], 16)
        p.bump("sLD", 16)

        ln_stats(p, block, lambda ft, t0=t0: stile(ft, t0, 512), 512,
                 p.gs2[:, 0:512])

        @block.vector
        def _(v, t0=t0):
            p.w(v, "sAE")
            p.w(v, "sLD")
            v.tensor_mul(p.gs1[:, 0:512], p.pp0[:, 0:512], p.gs2[:, 0:512])
            for ft in range(FT):
                v.tensor_mul(p.gs3[:, 0:512], stile(ft, t0, 512),
                             p.gs2[:, 0:512])
                v.tensor_sub(p.gs3[:, 0:512], p.gs3[:, 0:512], p.gs1[:, 0:512])
                v.tensor_add(outf[:, ft * 512:(ft + 1) * 512], p.gs3[:, 0:512],
                             x1b[:, ft * 512:(ft + 1) * 512])
            v.last.then_inc(L["sVE"], 1)
        p.bump("sVE")

        @block.sync
        def _(sy, s=s):
            p.w(sy, "sVE")
            sy.dma_start(out=out_d.rearrange("(ft p) t -> p ft t", p=128)
                         [:, :, s * 512:(s + 1) * 512],
                         in_=outf[:].rearrange("p (ft t) -> p ft t", ft=FT)
                         ).then_inc(L["sST"], 16)
        p.bump("sST", 16)
        fence(p, block)

    @block.sync
    def _(sy):
        p.w(sy, "sST")
    es.close()


def kernel(**inputs):
    consts = _host_consts()
    w = _prep_weights(inputs)
    x = inputs["L0_attn_x"]
    p = build_program()
    in_maps = []
    for core in range(8):
        m = {"x_in": np.ascontiguousarray(x[core].T)}
        m.update(w)
        m.update(consts)
        dm = np.zeros((128, 64), np.float32)
        dm[:, :core + 1] = 1.0
        m["dmask"] = dm
        in_maps.append(m)
    res = run_bass_kernel_spmd(p.nc, in_maps, core_ids=list(range(8)))
    out = np.stack([np.ascontiguousarray(r["out_d"].T) for r in res.results])
    return out


# version 14
# speedup vs baseline: 1.3280x; 1.0091x over previous
import numpy as np
import ml_dtypes
from contextlib import ExitStack

import concourse.bass as bass
import concourse.mybir as mybir
from concourse.bass_utils import run_bass_kernel_spmd

F32 = mybir.dt.float32
BF16 = mybir.dt.bfloat16
AF = mybir.ActivationFunctionType
ALU = mybir.AluOpType
AX = mybir.AxisListType

T, D, FT = 4096, 1024, 8
COLS = 1 + T
C = 64
SLA = 512
NSA = T // SLA
SLB = 256
NSB = T // SLB
CPS = SLB // C
NPAIR = CPS // 2
CHW = NPAIR * 64
H2, DH = 8, 64
WIN = 512
NW = T // WIN
HID2 = 2816
NS1 = HID2 // 128
TB = 512
NTB = T // TB
LEPS = 1e-5
NEG = -1e30

DEV_STOP = None
DEV_SKIP_HEADS = False
DEV_SKIP_V = False
DEV_DNB = False
DEV_SKIP_K = False
DEV_NW = None


def _host_consts():
    c = {}
    c["ident"] = np.eye(128, dtype=np.float32)
    c["ones1"] = np.ones((1, 128), dtype=np.float32)
    c["onesh"] = np.ones((128, 128), dtype=np.float32)
    c["sumw1024"] = np.full((128, 128), 1.0 / 1024.0, dtype=np.float32)
    c["sumw256"] = np.full((128, 128), 1.0 / 256.0, dtype=np.float32)
    blk = np.zeros((128, 128), dtype=np.float32)
    blk[:64, :64] = 1.0
    blk[64:, 64:] = 1.0
    c["blk64"] = blk
    tri = np.tril(np.ones((C, C), dtype=np.float32), -1)
    c["maskL"] = np.concatenate([tri, tri], 0)
    c["maskU"] = np.concatenate([tri.T, tri.T], 0)
    ey = np.eye(C, dtype=np.float32)
    c["eye1"] = np.concatenate([ey, ey], 0)
    c["eye2"] = 2.0 * np.concatenate([ey, ey], 0)
    prot = np.zeros((128, 128), dtype=np.float32)
    for b in range(2):
        b0 = b * 64
        for m in range(32):
            prot[b0 + m + 32, b0 + m] = -1.0
        for m in range(32, 64):
            prot[b0 + m - 32, b0 + m] = 1.0
    c["prot"] = prot
    c["epsv"] = np.full((128, 8), LEPS, dtype=np.float32)
    mqk = np.full((128, 4, 1024), NEG, dtype=np.float32)
    p_ = np.arange(128)[:, None]
    col = np.arange(1024)[None, :]
    for qb in range(4):
        qi = qb * 128 + p_
        mqk[:, qb, :] = np.where((col >= qi) & (col <= qi + WIN), 0.0, NEG)
    c["mqk"] = np.ascontiguousarray(mqk.reshape(128, 4096)).astype(
        ml_dtypes.bfloat16)
    mt = np.full((128, 8, 512), NEG, dtype=np.float32)
    q_ = np.arange(512)[None, :]
    for kb in range(8):
        cv = kb * 128 + p_
        mt[:, kb, :] = np.where((q_ <= cv) & (cv <= q_ + WIN), 0.0, NEG)
    c["mt"] = np.ascontiguousarray(mt.reshape(128, 4096)).astype(
        ml_dtypes.bfloat16)
    sel4 = np.zeros((4, 512), np.float32)
    for h in range(4):
        sel4[h, h * 128:(h + 1) * 128] = 1.0
    c["sel4"] = sel4
    half = np.arange(0, DH, 2, dtype=np.float64)
    inv_freq = 1.0 / 10000.0 ** (half / DH)
    tt = np.arange(T, dtype=np.float64)
    fr = tt[:, None] * inv_freq[None, :]
    fr = np.concatenate([fr, fr], -1)
    base = (half + 0.4 * DH) / (1.4 * DH)
    power = (tt - T // 2) / float(WIN // 2)
    sc = base[None, :] ** power[:, None]
    sc = np.concatenate([sc, sc], -1)
    cs_q = (np.cos(fr) * sc).T.astype(np.float32)
    ss_q = (np.sin(fr) * sc).T.astype(np.float32)
    cs_k = (np.cos(fr) / sc).T.astype(np.float32)
    ss_k = (np.sin(fr) / sc).T.astype(np.float32)
    c["csq"] = np.ascontiguousarray(8.0 * np.concatenate([cs_q, cs_q], 0))
    c["ssq"] = np.ascontiguousarray(8.0 * np.concatenate([ss_q, ss_q], 0))
    c["csk"] = np.ascontiguousarray(np.concatenate([cs_k, cs_k], 0))
    c["ssk"] = np.ascontiguousarray(np.concatenate([ss_k, ss_k], 0))
    return c


def _prep_weights(inp):
    w = {}

    def pm(a, nb):
        A = a.shape[0] // 128
        return np.ascontiguousarray(
            a.reshape(A, 128, nb, 128).transpose(1, 0, 2, 3).reshape(
                128, A * nb * 128))

    w["wq"] = pm(inp["L0_attn_Wq"], 8)
    w["wk"] = pm(inp["L0_attn_Wk"], 8)
    w["wv"] = pm(inp["L0_attn_Wv"], 8)
    w["wod"] = pm(inp["L0_attn_Wo"], 8)
    wbe = inp["L0_attn_Wb"].reshape(FT, 128, 4).transpose(1, 0, 2)
    w["wb"] = np.ascontiguousarray(np.repeat(wbe, 32, axis=2))
    for nm, key in (("cvq", "L0_attn_conv_q"), ("cvk", "L0_attn_conv_k"),
                    ("cvv", "L0_attn_conv_v")):
        cw = inp[key][:, 0, :]
        w[nm] = np.ascontiguousarray(cw.reshape(FT, 128, 4).transpose(1, 0, 2))
    w["g"] = np.ascontiguousarray(inp["L0_attn_g"].reshape(2, 128).T)
    w["fwqkv"] = np.stack([pm(inp["L0_ffn_wqkv"][l], 12) for l in range(2)])
    w["fwo"] = np.stack([pm(inp["L0_ffn_wo"][l], 8) for l in range(2)])
    qs = inp["L0_ffn_q_scale"]
    ks = inp["L0_ffn_k_scale"]
    rot = np.concatenate([np.arange(32, 64), np.arange(0, 32)])
    w["fqs"] = np.ascontiguousarray(np.concatenate([qs, qs], 1).T)
    w["fqsr"] = np.ascontiguousarray(np.concatenate([qs[:, rot], qs[:, rot]], 1).T)
    w["fks"] = np.ascontiguousarray(np.concatenate([ks, ks], 1).T)
    w["fksr"] = np.ascontiguousarray(np.concatenate([ks[:, rot], ks[:, rot]], 1).T)
    w["lng"] = np.ascontiguousarray(
        inp["L0_ffn_ln_g"].reshape(2, FT, 128).transpose(2, 0, 1).reshape(128, 16))
    w["lnb"] = np.ascontiguousarray(
        inp["L0_ffn_ln_b"].reshape(2, FT, 128).transpose(2, 0, 1).reshape(128, 16))
    w["b2"] = np.ascontiguousarray(
        inp["L0_ffn_b2"].reshape(2, FT, 128).transpose(2, 0, 1).reshape(128, 16))
    w1 = inp["L0_ffn_w1"]
    b1 = inp["L0_ffn_b1"]
    w1a = np.zeros((2, 1024, HID2), np.float32)
    w1g = np.zeros((2, 1024, HID2), np.float32)
    w1a[:, :, :2730] = w1[:, :, :2730]
    w1g[:, :, :2730] = w1[:, :, 2730:]
    w["w1a"] = w1a
    w["w1g"] = w1g
    b1a = np.zeros((2, HID2), np.float32)
    b1g = np.zeros((2, HID2), np.float32)
    b1a[:, :2730] = b1[:, :2730]
    b1g[:, :2730] = b1[:, 2730:]
    w["b1a"] = np.ascontiguousarray(
        b1a.reshape(2, NS1, 128).transpose(2, 0, 1).reshape(128, 2 * NS1))
    w["b1g"] = np.ascontiguousarray(
        b1g.reshape(2, NS1, 128).transpose(2, 0, 1).reshape(128, 2 * NS1))
    w2 = inp["L0_ffn_w2"]
    w2p = np.zeros((2, HID2, 1024), np.float32)
    w2p[:, :2730, :] = w2
    w["w2p"] = w2p
    return w


class P:
    def __init__(self):
        self.nc = bass.Bass("TRN2", target_bir_lowering=False,
                            detect_race_conditions=False)
        self.es = ExitStack()
        self.sems = {}
        self.cnt = {}

    def sem(self, name):
        h = self.es.enter_context(self.nc.semaphore(name))
        self.sems[name] = h
        self.cnt[name] = 0
        return h

    def bump(self, name, n=1):
        self.cnt[name] += n
        return self.cnt[name]

    def w(self, eng, name, val=None):
        eng.wait_ge(self.sems[name], self.cnt[name] if val is None else val)

    def sb(self, name, shape, dtype):
        t = self.es.enter_context(self.nc.sbuf_tensor(name, shape, dtype))
        setattr(self, name, t)
        return t

    def ps(self, name, shape, dtype):
        t = self.es.enter_context(self.nc.psum_tensor(name, shape, dtype))
        setattr(self, name, t)
        return t


_NOTRACK = {"wait_ge", "nop", "engine_nop", "sem_inc", "wait_op", "drain"}


class TRK:

    def __init__(self, eng):
        self._e = eng
        self.last = None

    def __getattr__(self, nm):
        f = getattr(self._e, nm)
        if not callable(f):
            return f

        def wrap(*a, **k):
            r = f(*a, **k)
            if nm not in _NOTRACK and hasattr(r, "then_inc"):
                self.last = r
            return r
        return wrap


class BlockProxy:
    def __init__(self, block):
        self._b = block

    def _mk(self, sec):
        def deco(fn):
            return sec(lambda eng: fn(TRK(eng)))
        return deco

    @property
    def tensor(self):
        return self._mk(self._b.tensor)

    @property
    def vector(self):
        return self._mk(self._b.vector)

    @property
    def scalar(self):
        return self._mk(self._b.scalar)

    @property
    def gpsimd(self):
        return self._mk(self._b.gpsimd)

    @property
    def sync(self):
        return self._mk(self._b.sync)


def fence(p, block):
    c = {nm: p.cnt[nm] for nm in ("sPE", "sVE", "sAE")}

    @block.tensor
    def _(t):
        t.wait_ge(p.sems["sVE"], c["sVE"])
        t.wait_ge(p.sems["sAE"], c["sAE"])

    @block.vector
    def _(v):
        v.wait_ge(p.sems["sPE"], c["sPE"])
        v.wait_ge(p.sems["sAE"], c["sAE"])

    @block.scalar
    def _(s):
        s.wait_ge(p.sems["sPE"], c["sPE"])
        s.wait_ge(p.sems["sVE"], c["sVE"])


def pipe(p, block, n, mm_emit, evac_emit, evac_engine="scalar", depth=2,
         pre_wait=(), pre_counts=()):
    pe0 = p.cnt["sPE"]
    ce_name = {"scalar": "sAE", "vector": "sVE"}[evac_engine]
    ce0 = p.cnt[ce_name]
    pw = [(nm, p.cnt[nm]) for nm in pre_wait] + list(pre_counts)

    @block.tensor
    def _(t):
        for nm, val in pw:
            t.wait_ge(p.sems[nm], val)
        for i in range(n):
            if i >= depth:
                t.wait_ge(p.sems[ce_name], ce0 + i - depth + 1)
            mm_emit(t, i)
            t.last.then_inc(p.sems["sPE"], 1)
    p.cnt["sPE"] += n

    sec = block.scalar if evac_engine == "scalar" else block.vector

    def body(e):
        for i in range(n):
            e.wait_ge(p.sems["sPE"], pe0 + i + 1)
            last = evac_emit(e, i)
            last.then_inc(p.sems[ce_name], 1)
    sec(body)
    p.cnt[ce_name] += n


def build_program():
    p = P()
    nc = p.nc
    din = {}

    def DI(name, shape):
        din[name] = nc.dram_tensor(name, list(shape), F32,
                                   kind="ExternalInput").ap()

    DI("x_in", (D, T))
    DI("wq", (128, 8192)); DI("wk", (128, 8192)); DI("wv", (128, 8192))
    DI("wod", (128, 8192))
    DI("wb", (128, FT, 128))
    DI("cvq", (128, FT, 4)); DI("cvk", (128, FT, 4)); DI("cvv", (128, FT, 4))
    DI("g", (128, 2))
    DI("fwqkv", (2, 128, FT * 12 * 128)); DI("fwo", (2, 128, 4 * FT * 128))
    DI("fqs", (128, 2)); DI("fqsr", (128, 2))
    DI("fks", (128, 2)); DI("fksr", (128, 2))
    DI("lng", (128, 2 * FT)); DI("lnb", (128, 2 * FT)); DI("b2", (128, 2 * FT))
    DI("w1a", (2, D, HID2)); DI("w1g", (2, D, HID2))
    DI("b1a", (128, 2 * NS1)); DI("b1g", (128, 2 * NS1))
    DI("w2p", (2, HID2, D))
    for nm, shp in (("ident", (128, 128)), ("ones1", (1, 128)),
                    ("onesh", (128, 128)), ("sumw1024", (128, 128)),
                    ("sumw256", (128, 128)), ("blk64", (128, 128)),
                    ("maskL", (128, 64)), ("maskU", (128, 64)),
                    ("eye1", (128, 64)), ("eye2", (128, 64)),
                    ("prot", (128, 128)), ("epsv", (128, 8)),
                    ("csq", (128, T)), ("ssq", (128, T)),
                    ("csk", (128, T)), ("ssk", (128, T)), ("dmask", (128, 64))):
        DI(nm, shp)
    din["sel4"] = nc.dram_tensor("sel4", [4, 512], F32,
                                 kind="ExternalInput").ap()
    for nm in ("mqk", "mt"):
        din[nm] = nc.dram_tensor(nm, [128, 4096], BF16,
                                 kind="ExternalInput").ap()

    scr = {}
    for nm, shape, dt in (("qn_s", [128, FT, T], BF16),
                          ("kn_s", [128, FT, T], BF16),
                          ("v_s", [128, FT, T], BF16),
                          ("x1_s", [128, FT, T], BF16),
                          ("w1a_bf", [2, D, HID2], BF16),
                          ("w1g_bf", [2, D, HID2], BF16),
                          ("betar_s", [4, T], F32),
                          ("dbg", [128, 16384], BF16)):
        scr[nm] = nc.dram_tensor(nm, shape, dt, kind="Internal").ap()
    out_d = nc.dram_tensor("out_d", [D, T], F32, kind="ExternalOutput").ap()

    p.sb("stream", [128, FT * COLS], BF16)
    for nm, shp, dt in (("c_id_bf", [128, 128], BF16),
                        ("c_id_f", [128, 128], F32),
                        ("c_ones1", [1, 128], BF16),
                        ("c_ones", [128, 128], BF16),
                        ("c_sw1024", [128, 128], BF16),
                        ("c_sw256", [128, 128], BF16),
                        ("c_blk64", [128, 128], BF16),
                        ("c_maskL", [128, 64], F32),
                        ("c_maskU", [128, 64], F32),
                        ("c_eye1", [128, 64], F32),
                        ("c_eye2", [128, 64], F32),
                        ("c_prot", [128, 128], BF16),
                        ("c_epsv", [128, 8], F32),
                        ("c_sel4", [4, 512], F32),
                        ("c_dmask", [128, 64], F32),
                        ("sq", [128, 4096], BF16),
                        ("gs1", [128, 512], F32), ("gs2", [128, 512], F32),
                        ("gs3", [128, 512], F32), ("gs4", [128, 512], F32),

                        ("sdt", [128, FT * 5], F32),
                        ("halo", [128, 3 * FT * 3], F32),
                        ("g_pp", [128, 2], F32),
                        ("cv_pp", [128, 3 * FT * 4], F32),
                        ("wb_bf", [128, FT * 128], BF16),
                        ("lng_pp", [128, 2 * FT], F32),
                        ("lnb_pp", [128, 2 * FT], F32),
                        ("lnb_bf", [128, 2 * FT], BF16),
                        ("b2_pp", [128, 2 * FT], F32),
                        ("qs_pp", [128, 2], F32), ("qsr_pp", [128, 2], F32),
                        ("ks_pp", [128, 2], F32), ("ksr_pp", [128, 2], F32),
                        ("b1ae", [128, NS1], F32), ("b1ge", [128, NS1], F32),
                        ("b1ac", [128, 2 * NS1], F32),
                        ("b1gc", [128, 2 * NS1], F32),
                        ("rowtmp", [1, 128], BF16),
                        ("dumm", [1, 8], F32)):
        p.sb(nm, shp, dt)
    p.ps("pw0", [128, 1024], F32)
    p.ps("pw1", [128, 1024], F32)
    p.ps("pp0", [128, 512], F32)
    p.ps("pp1", [128, 512], F32)
    p.ps("pp2", [128, 512], F32)
    p.ps("pp3", [128, 512], F32)
    for s in ("sLD", "sLG", "sWT", "sSP", "sST", "sPE", "sVE", "sAE", "sPO", "sDB"):
        p.sem(s)
    nc.const_aps.register(F32, LEPS, p.c_epsv[:, 0:1])

    def stile(ft, t0, n):
        base = ft * COLS + 1 + t0
        return p.stream[:, base:base + n]

    with nc.Block() as block:
        emit_all(p, BlockProxy(block), din, scr, out_d, stile)
    p.es.close()
    return p


def ln_stats_sb(p, block, rhs_tiles, n, mu_sb, rstd_sb, psA, psB,
               sq_off=0, eps=LEPS):
    L = p.sems

    @block.gpsimd
    def _(g):
        p.w(g, "sPE")
        p.w(g, "sVE")
        for ft in range(FT):
            g.tensor_mul(p.sq[:, sq_off + ft * n:sq_off + (ft + 1) * n],
                         rhs_tiles(ft), rhs_tiles(ft))
        g.last.then_inc(L["sPO"], 1)
    p.bump("sPO")

    @block.tensor
    def _(t):
        p.w(t, "sPO")
        p.w(t, "sVE")
        p.w(t, "sAE")
        for ft in range(FT):
            t.matmul(psA[:, 0:n], p.c_sw1024[:], rhs_tiles(ft),
                     start=(ft == 0), stop=(ft == FT - 1))
        for ft in range(FT):
            t.matmul(psB[:, 0:n], p.c_sw1024[:],
                     p.sq[:, sq_off + ft * n:sq_off + (ft + 1) * n],
                     start=(ft == 0), stop=(ft == FT - 1))
        t.last.then_inc(L["sPE"], 1)
    p.bump("sPE")

    @block.scalar
    def _(s):
        p.w(s, "sPE")
        s.copy(mu_sb, psA[:, 0:n])
        s.last.then_inc(L["sAE"], 1)
    p.bump("sAE")

    @block.gpsimd
    def _(g):
        p.w(g, "sAE")
        g.tensor_mul(rstd_sb, mu_sb, mu_sb)
        g.last.then_inc(L["sPO"], 1)
    p.bump("sPO")

    @block.vector
    def _(v):
        p.w(v, "sPO")
        v.tensor_sub(rstd_sb, psB[:, 0:n], rstd_sb)
        v.last.then_inc(L["sVE"], 1)
    p.bump("sVE")

    @block.scalar
    def _(s):
        p.w(s, "sVE")
        s.activation(rstd_sb, rstd_sb, AF.Ln, bias=eps)
        s.activation(rstd_sb, rstd_sb, AF.Exp, scale=-0.5)
        s.last.then_inc(L["sAE"], 1)
    p.bump("sAE")


def ln_stats(p, block, rhs_tiles, n, out_rstd, eps=LEPS):
    L = p.sems

    @block.gpsimd
    def _(g):
        p.w(g, "sPE")
        p.w(g, "sVE")
        for ft in range(FT):
            g.tensor_mul(p.sq[:, ft * n:(ft + 1) * n], rhs_tiles(ft),
                         rhs_tiles(ft))
        g.last.then_inc(L["sPO"], 1)
    p.bump("sPO")

    @block.tensor
    def _(t):
        p.w(t, "sPO")
        p.w(t, "sVE")
        p.w(t, "sAE")
        for ft in range(FT):
            t.matmul(p.pp0[:, 0:n], p.c_sw1024[:], rhs_tiles(ft),
                     start=(ft == 0), stop=(ft == FT - 1))
        for ft in range(FT):
            t.matmul(p.pp1[:, 0:n], p.c_sw1024[:], p.sq[:, ft * n:(ft + 1) * n],
                     start=(ft == 0), stop=(ft == FT - 1))
        t.last.then_inc(L["sPE"], 1)
    p.bump("sPE")

    @block.scalar
    def _(s):
        p.w(s, "sPE")
        s.copy(p.gs2[:, 0:n], p.pp0[:, 0:n])
        s.last.then_inc(L["sAE"], 1)
    p.bump("sAE")

    @block.gpsimd
    def _(g):
        p.w(g, "sAE")
        g.tensor_mul(p.gs1[:, 0:n], p.gs2[:, 0:n], p.gs2[:, 0:n])
        g.last.then_inc(L["sPO"], 1)
    p.bump("sPO")

    @block.vector
    def _(v):
        p.w(v, "sPO")
        v.tensor_sub(p.gs1[:, 0:n], p.pp1[:, 0:n], p.gs1[:, 0:n])
        v.last.then_inc(L["sVE"], 1)
    p.bump("sVE")

    @block.scalar
    def _(s):
        p.w(s, "sVE")
        s.activation(p.gs1[:, 0:n], p.gs1[:, 0:n], AF.Ln, bias=eps)
        s.activation(out_rstd, p.gs1[:, 0:n], AF.Exp, scale=-0.5)
        s.last.then_inc(L["sAE"], 1)
    p.bump("sAE")


def emit_all(p, block, din, scr, out_d, stile):
    L = p.sems

    @block.gpsimd
    def _(g):
        for dst, nm in ((p.c_id_bf, "ident"), (p.c_ones1, "ones1"),
                        (p.c_ones, "onesh"), (p.c_sw1024, "sumw1024"),
                        (p.c_sw256, "sumw256"), (p.c_blk64, "blk64"),
                        (p.c_prot, "prot")):
            g.dma_start(out=dst[:], in_=din[nm]).then_inc(L["sLG"], 16)
            p.cnt["sLG"] += 16
        g.dma_start(out=p.wb_bf[:].rearrange("p (ft m) -> p ft m", ft=FT),
                    in_=din["wb"]).then_inc(L["sLG"], 16)
        g.dma_start(out=p.lnb_bf[:], in_=din["lnb"]).then_inc(L["sLG"], 16)
        g.dma_start(out=p.stream[:].rearrange("p (ft c) -> p ft c", ft=FT)
                    [:, :, 1:1 + T],
                    in_=din["x_in"].rearrange("(ft p) t -> p ft t", p=128)
                    ).then_inc(L["sLG"], 16)
        p.cnt["sLG"] += 48

    @block.sync
    def _(sy):
        for dst, nm in ((p.c_epsv, "epsv"), (p.c_id_f, "ident"),
                        (p.c_sel4, "sel4"), (p.c_maskL, "maskL"),
                        (p.c_maskU, "maskU"), (p.c_eye1, "eye1"),
                        (p.c_eye2, "eye2"), (p.c_dmask, "dmask"),
                        (p.g_pp, "g")):
            sy.dma_start(out=dst[:], in_=din[nm]).then_inc(L["sLD"], 16)
            p.cnt["sLD"] += 16
        cv = p.cv_pp[:].rearrange("p (pr ft j) -> p pr ft j", pr=3, ft=FT)
        for i, nm in enumerate(("cvq", "cvk", "cvv")):
            sy.dma_start(out=cv[:, i, :, :], in_=din[nm]).then_inc(L["sLD"], 16)
            p.cnt["sLD"] += 16
        for dst, nm in ((p.lng_pp, "lng"), (p.lnb_pp, "lnb"), (p.b2_pp, "b2"),
                        (p.qs_pp, "fqs"), (p.qsr_pp, "fqsr"),
                        (p.ks_pp, "fks"), (p.ksr_pp, "fksr"),
                        (p.b1ac, "b1a"), (p.b1gc, "b1g")):
            sy.dma_start(out=dst[:], in_=din[nm]).then_inc(L["sLD"], 16)
            p.cnt["sLD"] += 16

    ld0 = p.cnt["sLD"]
    lg0 = p.cnt["sLG"]

    @block.vector
    def _(v):
        v.wait_ge(L["sLG"], lg0)
        v.wait_ge(L["sLD"], ld0)
        v.memset(p.sdt[:], 0.0)
        v.memset(p.dumm[:], 0.0)
        v.memset(p.halo[:], 0.0)
        v.memset(p.stream[:].rearrange("p (ft c) -> p ft c", ft=FT)[:, :, 0:1],
                 0.0)
        v.last.then_inc(L["sVE"], 1)
    p.bump("sVE")
    ve0 = p.cnt["sVE"]

    @block.tensor
    def _(t):
        t.wait_ge(L["sVE"], ve0)

    @block.scalar
    def _(s):
        s.wait_ge(L["sVE"], ve0)

    emit_dn_A(p, block, din, scr, stile)
    if DEV_STOP == "A":
        emit_dev_out(p, block, out_d, stile)
        return
    emit_dn_B(p, block, din, scr, stile)
    if DEV_STOP == "B":
        emit_dev_out(p, block, out_d, stile)
        return
    for layer in range(2):
        emit_ffn_attn(p, block, din, scr, layer, stile)
        if DEV_STOP == f"L{layer}a":
            emit_dev_out(p, block, out_d, stile)
            return
        emit_ffn_mlp(p, block, din, scr, layer, stile)
        if DEV_STOP == f"L{layer}":
            emit_dev_out(p, block, out_d, stile)
            return
    emit_final(p, block, scr, out_d, stile)


def emit_dev_out(p, block, out_d, stile):
    L = p.sems
    fence(p, block)

    @block.gpsimd
    def _(g):
        p.w(g, "sVE")
        g.dma_start(out=out_d.rearrange("(ft p) t -> p ft t", p=128),
                    in_=p.stream[:].rearrange("p (ft c) -> p ft c", ft=FT)
                    [:, :, 1:1 + T]).then_inc(L["sST"], 16)
    p.bump("sST", 16)

    @block.sync
    def _(sy):
        p.w(sy, "sST")
        for nm in ("sSP", "sLD", "sLG", "sWT"):
            p.w(sy, nm)


def emit_dn_A(p, block, din, scr, stile):
    nc = p.nc
    L = p.sems
    es = ExitStack()
    wq_sb = es.enter_context(nc.sbuf_tensor("wq_sb", [128, 8192], BF16))
    wk_sb = es.enter_context(nc.sbuf_tensor("wk_sb", [128, 8192], BF16))
    wv_sb = es.enter_context(nc.sbuf_tensor("wv_sb", [128, 8192], BF16))
    xps = [es.enter_context(nc.sbuf_tensor(f"xpd{i}", [128, FT * 515], F32))
           for i in range(2)]
    lnxs = [es.enter_context(nc.sbuf_tensor("lnxd0", [128, FT * 512], BF16))]
    sts = [es.enter_context(nc.sbuf_tensor(f"std{i}", [128, FT * 512], BF16))
           for i in range(2)]
    stf = es.enter_context(nc.sbuf_tensor("stf", [128, 4 * 512], F32))
    lnmu = es.enter_context(nc.sbuf_tensor("lnmu", [128, 512], F32))
    lnt = es.enter_context(nc.sbuf_tensor("lnt", [128, 512], F32))
    lnrs = es.enter_context(nc.sbuf_tensor("lnrs", [128, 512], F32))
    bet_a = lnt

    @block.gpsimd
    def _(g):
        for wsb, nm in ((wq_sb, "wq"), (wk_sb, "wk"), (wv_sb, "wv")):
            g.dma_start(out=wsb[:], in_=din[nm]).then_inc(L["sWT"], 16)
    p.bump("sWT", 48)
    wt_w = p.cnt["sWT"]

    def wtile(wsb, fi, fo):
        return wsb[:].rearrange("p (a b q) -> p a b q", a=FT, b=FT)[:, fi,
                                                                    fo, :]

    ve_gate = p.cnt["sVE"]

    PLN = {}

    def lnx_t(sn, ft):
        return lnxs[0][:, ft * 512:(ft + 1) * 512]

    def st_t(j, ft):
        return sts[j % 2][:, ft * 512:(ft + 1) * 512]

    def stf_t(ft):
        return stf[:, (ft % 4) * 512:(ft % 4 + 1) * 512]

    def xp_t(j, ft):
        return xps[j % 2][:, ft * 515:(ft + 1) * 515]

    g0 = p.cnt["sPO"]
    pe0 = p.cnt["sPE"]
    a0 = p.cnt["sAE"]
    v0 = p.cnt["sVE"]
    s0 = p.cnt["sSP"]
    pro = dict(lnsq=g0 + 1, mu2=g0 + 2, lnmm=pe0 + 1, bmm=pe0 + 2,
               mu=a0 + 1, lnexp=a0 + 2, sig=a0 + 3, vesub=v0 + 1,
               lnap=v0 + 2, bst=s0 + 64)

    @block.gpsimd
    def _(g):
        g.wait_ge(L["sVE"], ve_gate)
        for ft in range(FT):
            g.tensor_mul(p.sq[:, ft * 512:(ft + 1) * 512],
                         stile(ft, 0, SLA), stile(ft, 0, SLA))
        g.last.then_inc(L["sPO"], 1)
        g.wait_ge(L["sAE"], pro["mu"])
        g.tensor_mul(lnrs[:], lnmu[:], lnmu[:])
        g.last.then_inc(L["sPO"], 1)
    p.cnt["sPO"] += 2

    @block.tensor
    def _(t):
        t.wait_ge(L["sPO"], pro["lnsq"])
        for ft in range(FT):
            t.matmul(p.pp2[:, 0:512], p.c_sw1024[:], stile(ft, 0, SLA),
                     start=(ft == 0), stop=(ft == FT - 1))
        for ft in range(FT):
            t.matmul(p.pp3[:, 0:512], p.c_sw1024[:],
                     p.sq[:, ft * 512:(ft + 1) * 512],
                     start=(ft == 0), stop=(ft == FT - 1))
        t.last.then_inc(L["sPE"], 1)
        t.wait_ge(L["sVE"], pro["lnap"])
        t.wait_ge(L["sWT"], wt_w)
        for fi in range(FT):
            t.matmul(p.pw0[:, 0:512],
                     p.wb_bf[:].rearrange("p (ft m) -> p ft m", ft=FT)[:, fi, :],
                     lnx_t(0, fi), start=(fi == 0), stop=(fi == FT - 1))
        t.last.then_inc(L["sPE"], 1)
    p.cnt["sPE"] += 2

    @block.scalar
    def _(sc):
        sc.wait_ge(L["sPE"], pro["lnmm"])
        sc.copy(lnmu[:], p.pp2[:, 0:512])
        sc.last.then_inc(L["sAE"], 1)
        sc.wait_ge(L["sVE"], pro["vesub"])
        sc.activation(lnrs[:], lnrs[:], AF.Ln, bias=LEPS)
        sc.activation(lnrs[:], lnrs[:], AF.Exp, scale=-0.5)
        sc.last.then_inc(L["sAE"], 1)
        sc.wait_ge(L["sPE"], pro["bmm"])
        for h in range(4):
            sc.activation(bet_a[h * 32:h * 32 + 1, :],
                          p.pw0[h * 32:h * 32 + 1, 0:512], AF.Sigmoid)
        sc.last.then_inc(L["sAE"], 1)
    p.cnt["sAE"] += 3

    @block.vector
    def _(v):
        v.wait_ge(L["sPO"], pro["mu2"])
        v.tensor_sub(lnrs[:], p.pp3[:, 0:512], lnrs[:])
        v.last.then_inc(L["sVE"], 1)
        v.wait_ge(L["sAE"], pro["lnexp"])
        for ft in range(FT):
            v.tensor_sub(lnt[:], stile(ft, 0, SLA), lnmu[:])
            v.tensor_mul(lnx_t(0, ft), lnt[:], lnrs[:])
        v.last.then_inc(L["sVE"], 1)
    p.cnt["sVE"] += 2

    @block.sync
    def _(sy):
        sy.wait_ge(L["sAE"], pro["sig"])
        for h in range(4):
            sy.dma_start(out=scr["betar_s"][h:h + 1, 0:SLA],
                         in_=bet_a[h * 32:h * 32 + 1, :]
                         ).then_inc(L["sSP"], 16)
    p.cnt["sSP"] += 64

    prev = dict(lnap=pro["lnap"], zipB_end=0, rsqB=0, rskB=0,
                vesub=pro["vesub"], convk_end=0, convv_end=0, l2kmul=0,
                sig=pro["sig"], sto_k=0, sto_v=0, bst=pro["bst"],
                siv_l=None, l2kB=pro["lnmm"], sie_v=0, a_end=p.cnt["sAE"])
    outs_nm = ("qn_s", "kn_s", "v_s")

    for s in range(NSA):
        t0 = s * SLA
        last = (s + 1 == NSA)
        jq, jk, jv = 3 * s, 3 * s + 1, 3 * s + 2
        g0 = p.cnt["sPO"]
        pe0 = p.cnt["sPE"]
        a0 = p.cnt["sAE"]
        v0 = p.cnt["sVE"]
        s0 = p.cnt["sSP"]
        if not last:
            gP = dict(lnsq=g0 + 1, mu2=g0 + 2, sqq=g0 + 3, sqk=g0 + 4)
            peP = dict(tq=[pe0 + 1 + i for i in range(8)], lnmm=pe0 + 9,
                       tk=[pe0 + 10 + i for i in range(8)],
                       l2qA=pe0 + 18, l2qB=pe0 + 19,
                       tv=[pe0 + 20 + i for i in range(8)],
                       l2kA=pe0 + 28, l2kB=pe0 + 29, bmm=pe0 + 30)
            aP = dict(evq=[a0 + 1 + i for i in range(8)], mu=a0 + 9,
                      evk=[a0 + 10 + 2 * i for i in range(8)],
                      siq=[a0 + 11 + 2 * i for i in range(8)],
                      rsqA=a0 + 26, rsqB=a0 + 27,
                      evv=[a0 + 28 + 2 * i for i in range(8)],
                      sik=[a0 + 29 + 2 * i for i in range(8)],
                      rskA=a0 + 44, rskB=a0 + 45,
                      lnexp=a0 + 46, siv=[a0 + 47 + i for i in range(8)],
                      sig=a0 + 55)
            vP = dict(convq=[v0 + 1 + i for i in range(8)], vesub=v0 + 9,
                      convk=[v0 + 10 + i for i in range(8)], l2qmul=v0 + 18,
                      lnap=v0 + 19, convv=[v0 + 20 + i for i in range(8)],
                      l2kmul=v0 + 28)
            sP = dict(sto_q=s0 + 16, sto_k=s0 + 32, sto_v=s0 + 48,
                      bst=s0 + 112)
        else:
            gP = dict(sqq=g0 + 1, sqk=g0 + 2)
            peP = dict(tq=[pe0 + 1 + i for i in range(8)],
                       tk=[pe0 + 9 + i for i in range(8)],
                       l2qA=pe0 + 17, l2qB=pe0 + 18,
                       tv=[pe0 + 19 + i for i in range(8)],
                       l2kA=pe0 + 27, l2kB=pe0 + 28)
            aP = dict(evq=[a0 + 1 + i for i in range(8)],
                      evk=[a0 + 9 + 2 * i for i in range(8)],
                      siq=[a0 + 10 + 2 * i for i in range(8)],
                      rsqA=a0 + 25, rsqB=a0 + 26,
                      evv=[a0 + 27 + 2 * i for i in range(8)],
                      sik=[a0 + 28 + 2 * i for i in range(8)],
                      rskA=a0 + 43, rskB=a0 + 44,
                      siv=[a0 + 45 + i for i in range(8)])
            vP = dict(convq=[v0 + 1 + i for i in range(8)],
                      convk=[v0 + 9 + i for i in range(8)], l2qmul=v0 + 17,
                      convv=[v0 + 18 + i for i in range(8)], l2kmul=v0 + 26)
            sP = dict(sto_q=s0 + 16, sto_k=s0 + 32, sto_v=s0 + 48)

        @block.gpsimd
        def _(g, s=s, last=last, gP=gP, aP=aP, peP=peP, prev=dict(prev),
              jq=jq, jk=jk):
            if not last:
                if prev["l2kB"]:
                    g.wait_ge(L["sPE"], prev["l2kB"])
                for ft in range(FT):
                    g.tensor_mul(p.sq[:, ft * 512:(ft + 1) * 512],
                                 stile(ft, (s + 1) * SLA, SLA),
                                 stile(ft, (s + 1) * SLA, SLA))
                g.last.then_inc(L["sPO"], 1)
                g.wait_ge(L["sAE"], aP["mu"])
                g.tensor_mul(lnrs[:], lnmu[:], lnmu[:])
                g.last.then_inc(L["sPO"], 1)
            g.wait_ge(L["sAE"], aP["siq"][7])
            g.wait_ge(L["sPE"], peP["lnmm"] if not last else peP["tq"][7])
            for ft in range(FT):
                g.tensor_mul(p.sq[:, ft * 512:(ft + 1) * 512],
                             st_t(jq, ft), st_t(jq, ft))
            g.last.then_inc(L["sPO"], 1)
            g.wait_ge(L["sAE"], aP["sik"][7])
            g.wait_ge(L["sPE"], peP["l2qB"])
            for ft in range(FT):
                g.tensor_mul(p.sq[:, ft * 512:(ft + 1) * 512],
                             st_t(jk, ft), st_t(jk, ft))
            g.last.then_inc(L["sPO"], 1)
        p.cnt["sPO"] += 2 if last else 4

        @block.tensor
        def _(t, s=s, t0=t0, last=last, gP=gP, peP=peP, aP=aP, vP=vP,
              prev=dict(prev), jq=jq, jk=jk, jv=jv):
            t.wait_ge(L["sVE"], prev["lnap"])
            if prev["zipB_end"]:
                t.wait_ge(L["sAE"], prev["zipB_end"])
            if s == 0:
                t.wait_ge(L["sWT"], wt_w)
            slotq = [None] * 8

            def projmm(j, wsb, evl, prior):
                for i in range(8):
                    if i >= 2:
                        t.wait_ge(L["sAE"], evl[i - 2])
                    elif prior[i]:
                        t.wait_ge(L["sAE"], prior[i])
                    ps = p.pw1[:, 0:512] if i % 2 == 0 else p.pw1[:, 512:1024]
                    for fi in range(FT):
                        t.matmul(ps, wtile(wsb, fi, i), lnx_t(s, fi),
                                 start=(fi == 0), stop=(fi == FT - 1))
                    t.last.then_inc(L["sPE"], 1)

            projmm(jq, wq_sb, aP["evq"],
                   [prev["zipB_end"] and (prev["zipB_end"] - 1),
                    prev["zipB_end"]])
            if not last:
                t.wait_ge(L["sPO"], gP["lnsq"])
                t.wait_ge(L["sAE"], prev["rskB"] if prev["rskB"]
                          else aP["mu"] - 1)
                t.wait_ge(L["sVE"], prev["vesub"])
                for ft in range(FT):
                    t.matmul(p.pp2[:, 0:512], p.c_sw1024[:],
                             stile(ft, (s + 1) * SLA, SLA),
                             start=(ft == 0), stop=(ft == FT - 1))
                for ft in range(FT):
                    t.matmul(p.pp3[:, 0:512], p.c_sw1024[:],
                             p.sq[:, ft * 512:(ft + 1) * 512],
                             start=(ft == 0), stop=(ft == FT - 1))
                t.last.then_inc(L["sPE"], 1)
            projmm(jk, wk_sb, aP["evk"], [aP["evq"][6], aP["evq"][7]])
            t.wait_ge(L["sPO"], gP["sqq"])
            if prev["rsqB"]:
                t.wait_ge(L["sAE"], prev["rsqB"])
            for rnd in range(2):
                if rnd == 1:
                    t.wait_ge(L["sAE"], aP["rsqA"])
                for hh in range(2):
                    h = rnd * 2 + hh
                    ps = p.pp0 if hh == 0 else p.pp1
                    for k in range(2):
                        t.matmul(ps[:, 0:512], p.c_ones[:],
                                 p.sq[:, (2 * h + k) * 512:
                                      (2 * h + k + 1) * 512],
                                 start=(k == 0), stop=(k == 1))
                t.last.then_inc(L["sPE"], 1)
            projmm(jv, wv_sb, aP["evv"], [aP["evk"][6], aP["evk"][7]])
            t.wait_ge(L["sPO"], gP["sqk"])
            if not last:
                t.wait_ge(L["sVE"], vP["vesub"])
                t.wait_ge(L["sAE"], aP["mu"])
            for rnd in range(2):
                if rnd == 1:
                    t.wait_ge(L["sAE"], aP["rskA"])
                for hh in range(2):
                    h = rnd * 2 + hh
                    ps = p.pp2 if hh == 0 else p.pp3
                    for k in range(2):
                        t.matmul(ps[:, 0:512], p.c_ones[:],
                                 p.sq[:, (2 * h + k) * 512:
                                      (2 * h + k + 1) * 512],
                                 start=(k == 0), stop=(k == 1))
                t.last.then_inc(L["sPE"], 1)
            if not last:
                t.wait_ge(L["sVE"], vP["lnap"])
                t.wait_ge(L["sAE"], prev["sig"])
                for fi in range(FT):
                    t.matmul(p.pw0[:, 0:512],
                             p.wb_bf[:].rearrange("p (ft m) -> p ft m",
                                                  ft=FT)[:, fi, :],
                             lnx_t(s + 1, fi), start=(fi == 0),
                             stop=(fi == FT - 1))
                t.last.then_inc(L["sPE"], 1)
        p.cnt["sPE"] += 30 if not last else 28

        @block.scalar
        def _(sc, s=s, last=last, peP=peP, aP=aP, vP=vP, prev=dict(prev),
              jq=jq, jk=jk, jv=jv):
            hap = p.halo[:].rearrange("p (pr ft j) -> p pr ft j", pr=3, ft=FT)
            for ft in range(8):
                sc.wait_ge(L["sPE"], peP["tq"][ft])
                if ft == 0 and prev["convk_end"]:
                    sc.wait_ge(L["sVE"], prev["convk_end"])
                ps = p.pw1[:, 0:512] if ft % 2 == 0 else p.pw1[:, 512:1024]
                sc.copy(xps[jq % 2][:, ft * 515 + 3:ft * 515 + 515],
                        ps).then_inc(L["sAE"], 1)
            if not last:
                sc.wait_ge(L["sPE"], peP["lnmm"])
                sc.copy(lnmu[:], p.pp2[:, 0:512]).then_inc(L["sAE"], 1)
            for ft in range(8):
                sc.wait_ge(L["sPE"], peP["tk"][ft])
                if ft == 0 and prev["convv_end"]:
                    sc.wait_ge(L["sVE"], prev["convv_end"])
                ps = p.pw1[:, 0:512] if ft % 2 == 0 else p.pw1[:, 512:1024]
                sc.copy(xps[jk % 2][:, ft * 515 + 3:ft * 515 + 515],
                        ps).then_inc(L["sAE"], 1)
                sc.wait_ge(L["sVE"], vP["convq"][ft])
                if ft == 0 and prev["sto_k"]:
                    sc.wait_ge(L["sSP"], prev["sto_k"])
                sc.activation(st_t(jq, ft), stf_t(ft),
                              AF.Silu).then_inc(L["sAE"], 1)
            sc.wait_ge(L["sPE"], peP["l2qA"])
            if prev["l2kmul"]:
                sc.wait_ge(L["sVE"], prev["l2kmul"])
            for hh in range(2):
                ps = p.pp0 if hh == 0 else p.pp1
                dst = [p.gs1, p.gs2][hh]
                sc.activation(dst[:, 0:512], ps[:, 0:512], AF.Ln)
                sc.activation(dst[:, 0:512], dst[:, 0:512], AF.Exp, scale=-0.5)
            sc.last.then_inc(L["sAE"], 1)
            sc.wait_ge(L["sPE"], peP["l2qB"])
            for hh in range(2):
                ps = p.pp0 if hh == 0 else p.pp1
                dst = [p.gs3, p.gs4][hh]
                sc.activation(dst[:, 0:512], ps[:, 0:512], AF.Ln)
                sc.activation(dst[:, 0:512], dst[:, 0:512], AF.Exp, scale=-0.5)
            sc.last.then_inc(L["sAE"], 1)
            for ft in range(8):
                sc.wait_ge(L["sPE"], peP["tv"][ft])
                sc.wait_ge(L["sVE"], vP["convq"][7])
                ps = p.pw1[:, 0:512] if ft % 2 == 0 else p.pw1[:, 512:1024]
                sc.copy(xps[jv % 2][:, ft * 515 + 3:ft * 515 + 515],
                        ps).then_inc(L["sAE"], 1)
                sc.wait_ge(L["sVE"], vP["convk"][ft])
                if ft == 0 and prev["sto_v"]:
                    sc.wait_ge(L["sSP"], prev["sto_v"])
                sc.activation(st_t(jk, ft), stf_t(ft),
                              AF.Silu).then_inc(L["sAE"], 1)
            sc.wait_ge(L["sPE"], peP["l2kA"])
            sc.wait_ge(L["sVE"], vP["l2qmul"])
            for hh in range(2):
                ps = p.pp2 if hh == 0 else p.pp3
                dst = [p.gs1, p.gs2][hh]
                sc.activation(dst[:, 0:512], ps[:, 0:512], AF.Ln)
                sc.activation(dst[:, 0:512], dst[:, 0:512], AF.Exp, scale=-0.5)
            sc.last.then_inc(L["sAE"], 1)
            sc.wait_ge(L["sPE"], peP["l2kB"])
            for hh in range(2):
                ps = p.pp2 if hh == 0 else p.pp3
                dst = [p.gs3, p.gs4][hh]
                sc.activation(dst[:, 0:512], ps[:, 0:512], AF.Ln)
                sc.activation(dst[:, 0:512], dst[:, 0:512], AF.Exp, scale=-0.5)
            sc.last.then_inc(L["sAE"], 1)
            if not last:
                sc.wait_ge(L["sVE"], vP["vesub"])
                sc.activation(lnrs[:], lnrs[:], AF.Ln, bias=LEPS)
                sc.activation(lnrs[:], lnrs[:], AF.Exp, scale=-0.5)
                sc.last.then_inc(L["sAE"], 1)
            for ft in range(8):
                sc.wait_ge(L["sVE"], vP["convv"][ft])
                if ft == 0:
                    sc.wait_ge(L["sSP"], sP["sto_q"])
                sc.activation(st_t(jv, ft), stf_t(ft),
                              AF.Silu).then_inc(L["sAE"], 1)
            if not last:
                sc.wait_ge(L["sPE"], peP["bmm"])
                sc.wait_ge(L["sSP"], prev["bst"])
                for h in range(4):
                    sc.activation(bet_a[h * 32:h * 32 + 1, :],
                                  p.pw0[h * 32:h * 32 + 1, 0:512], AF.Sigmoid)
                sc.last.then_inc(L["sAE"], 1)
        p.cnt["sAE"] += 55 if not last else 52

        @block.vector
        def _(v, s=s, last=last, aP=aP, vP=vP, gP=gP, peP=peP,
              prev=dict(prev), jq=jq, jk=jk, jv=jv):
            hap = p.halo[:].rearrange("p (pr ft j) -> p pr ft j", pr=3, ft=FT)
            cv = p.cv_pp[:].rearrange("p (pr ft j) -> p pr ft j", pr=3, ft=FT)

            def conv(j, pi, evl, s_own, s_prev):
                for ft in range(FT):
                    w_ = evl[ft]
                    if ft >= 4:
                        w_ = max(w_, s_own[ft - 4])
                    elif s_prev:
                        w_ = max(w_, s_prev[ft + 4])
                    v.wait_ge(L["sAE"], w_)
                    xpj = xps[j % 2]
                    v.tensor_copy(xpj[:, ft * 515:ft * 515 + 3],
                                  hap[:, pi, ft, :])
                    v.tensor_scalar_mul(stf_t(ft),
                                        xpj[:, ft * 515:ft * 515 + 512],
                                        cv[:, pi, ft, 0:1])
                    for jj in range(1, 4):
                        v.scalar_tensor_tensor(
                            stf_t(ft),
                            xpj[:, ft * 515 + jj:ft * 515 + jj + 512],
                            cv[:, pi, ft, jj:jj + 1], stf_t(ft),
                            ALU.mult, ALU.add)
                    v.tensor_copy(hap[:, pi, ft, :],
                                  xpj[:, ft * 515 + 512:ft * 515 + 515])
                    v.last.then_inc(L["sVE"], 1)

            def l2mul(j):
                for ft in range(FT):
                    rs = [p.gs1, p.gs2, p.gs3, p.gs4][ft // 2]
                    v.tensor_mul(st_t(j, ft), st_t(j, ft), rs[:, 0:512])
                v.last.then_inc(L["sVE"], 1)

            conv(jq, 0, aP["evq"], aP["siq"], prev["siv_l"])
            if not last:
                v.wait_ge(L["sPO"], gP["mu2"])
                v.tensor_sub(lnrs[:], p.pp3[:, 0:512], lnrs[:])
                v.last.then_inc(L["sVE"], 1)
            conv(jk, 1, aP["siq"], aP["sik"], aP["siq"])
            v.wait_ge(L["sAE"], aP["rsqB"])
            l2mul(jq)
            if not last:
                v.wait_ge(L["sAE"], aP["lnexp"])
                v.wait_ge(L["sPE"], peP["tv"][7])
                v.wait_ge(L["sSP"], prev["bst"])
                for ft in range(FT):
                    v.tensor_sub(lnt[:], stile(ft, (s + 1) * SLA, SLA),
                                 lnmu[:])
                    v.tensor_mul(lnx_t(s + 1, ft), lnt[:], lnrs[:])
                v.last.then_inc(L["sVE"], 1)
            conv(jv, 2, aP["sik"], aP["siv"], aP["sik"])
            v.wait_ge(L["sAE"], aP["rskB"])
            l2mul(jk)
        p.cnt["sVE"] += 28 if not last else 26

        @block.sync
        def _(sy, s=s, last=last, aP=aP, vP=vP, jq=jq, jk=jk, jv=jv):
            sy.wait_ge(L["sVE"], vP["l2qmul"])
            sy.dma_start(out=scr["qn_s"][:, :, s * SLA:(s + 1) * SLA],
                         in_=sts[jq % 2][:].rearrange("p (ft t) -> p ft t",
                                                      ft=FT)
                         ).then_inc(L["sSP"], 16)
            sy.wait_ge(L["sVE"], vP["l2kmul"])
            sy.dma_start(out=scr["kn_s"][:, :, s * SLA:(s + 1) * SLA],
                         in_=sts[jk % 2][:].rearrange("p (ft t) -> p ft t",
                                                      ft=FT)
                         ).then_inc(L["sSP"], 16)
            sy.wait_ge(L["sAE"], aP["siv"][7])
            sy.dma_start(out=scr["v_s"][:, :, s * SLA:(s + 1) * SLA],
                         in_=sts[jv % 2][:].rearrange("p (ft t) -> p ft t",
                                                      ft=FT)
                         ).then_inc(L["sSP"], 16)
            if not last:
                sy.wait_ge(L["sAE"], aP["sig"])
                for h in range(4):
                    sy.dma_start(out=scr["betar_s"][h:h + 1, (s + 1) * SLA:
                                                    (s + 2) * SLA],
                                 in_=bet_a[h * 32:h * 32 + 1, :]
                                 ).then_inc(L["sSP"], 16)
        p.cnt["sSP"] += 112 if not last else 48

        prev = dict(lnap=vP.get("lnap", prev["lnap"]),
                    zipB_end=aP["sik"][7], rsqB=aP["rsqB"], rskB=aP["rskB"],
                    vesub=vP.get("vesub", prev["vesub"]),
                    convk_end=vP["convk"][7], convv_end=vP["convv"][7],
                    l2kmul=vP["l2kmul"], sig=aP.get("sig", prev["sig"]),
                    sto_k=sP["sto_k"], sto_v=sP["sto_v"],
                    bst=sP.get("bst", prev["bst"]), siv_l=aP["siv"],
                    l2kB=peP["l2kB"], sie_v=0, a_end=p.cnt["sAE"])

    fence(p, block)
    es.close()


def emit_dn_B(p, block, din, scr, stile):
    nc = p.nc
    L = p.sems
    es = ExitStack()
    wo_sb = es.enter_context(nc.sbuf_tensor("wo_sb", [128, 8192], BF16))
    qn_b = es.enter_context(nc.sbuf_tensor("qn_b", [128, 2 * FT * SLB], BF16))
    kn_b = es.enter_context(nc.sbuf_tensor("kn_b", [128, 2 * FT * SLB], BF16))
    v_b = es.enter_context(nc.sbuf_tensor("v_b", [128, 2 * FT * SLB], BF16))
    w_b = es.enter_context(nc.sbuf_tensor("w_b", [128, 2 * FT * SLB], BF16))
    u_b = es.enter_context(nc.sbuf_tensor("u_b", [128, 2 * FT * SLB], BF16))
    kb_b = es.enter_context(nc.sbuf_tensor("kb_b", [128, FT * SLB], BF16))
    vb_b = es.enter_context(nc.sbuf_tensor("vb_b", [128, FT * SLB], BF16))
    o_b = es.enter_context(nc.sbuf_tensor("o_b", [128, FT * SLB], BF16))
    ao_b = es.enter_context(nc.sbuf_tensor("ao_b", [128, FT * SLB], BF16))
    kpm = es.enter_context(nc.sbuf_tensor("kpm", [128, 2 * FT * 128], BF16))
    vpm = es.enter_context(nc.sbuf_tensor("vpm", [128, 2 * FT * 128], BF16))
    dm_b = es.enter_context(nc.sbuf_tensor("dm_b", [128, SLB], F32))
    bet_b = es.enter_context(nc.sbuf_tensor("bet_b", [4, 2 * SLB], F32))
    aa_b = es.enter_context(nc.sbuf_tensor("aa_b", [128, FT * CPS], F32))
    bb_b = es.enter_context(nc.sbuf_tensor("bb_b", [128, FT * CPS], F32))
    ch = {}
    for nm in ("chL", "chLt", "chP", "chPt", "chQ", "chB"):
        ch[nm] = es.enter_context(nc.sbuf_tensor(nm, [128, CHW], F32))
    tt_bf = es.enter_context(nc.sbuf_tensor("tt_bf", [128, CHW], BF16))
    mu_a = es.enter_context(nc.sbuf_tensor("mu_a", [128, SLB], F32))
    rs_a = es.enter_context(nc.sbuf_tensor("rs_a", [128, SLB], F32))
    mu_c = es.enter_context(nc.sbuf_tensor("mu_c", [128, SLB], F32))
    rs_c = es.enter_context(nc.sbuf_tensor("rs_c", [128, SLB], F32))
    lt_b = es.enter_context(nc.sbuf_tensor("lt_b", [128, SLB], F32))

    @block.gpsimd
    def _(g):
        p.w(g, "sPE")
        p.w(g, "sVE")
        p.w(g, "sAE")
        g.dma_start(out=wo_sb[:], in_=din["wod"]).then_inc(L["sWT"], 16)
    p.bump("sWT", 16)
    wo_w = p.cnt["sWT"]

    def wotile(fi, fo):
        return wo_sb[:].rearrange("p (a b q) -> p a b q", a=FT, b=FT)[:, fi,
                                                                      fo, :]

    def slab(buf, c, ft, lo=0, n=SLB):
        base = ((c % 2) * FT + ft) * SLB
        return buf[:, base + lo:base + lo + n]

    b3 = lambda x: x.rearrange("p (n c) -> p n c", c=64)
    PL = {}

    for c in range(NSB + 2):
        c1, c2 = c - 1, c - 2
        hasB1 = c < NSB
        hasA = 0 <= c1 < NSB
        hasB = 0 <= c2 < NSB
        pl = {}
        pe = p.cnt["sPE"]
        ve = p.cnt["sVE"]
        ac = p.cnt["sAE"]
        po = p.cnt["sPO"]
        if hasB:
            pl["subln1"] = ve = ve + 1
            pl["resid"] = ve = ve + 1
            pl["ln2mm"] = pe = pe + 1
        if hasA:
            pl["rowsum"] = pe = pe + 1
            pl["dm"] = ve = ve + 1
        if hasB1:
            pl["beta"] = pe = pe + 1
            pl["kbvb"] = ve = ve + 1
        if hasB:
            pl["subln2"] = ve = ve + 1
        if hasB1:
            pl["tr"] = [pe + 1 + i for i in range(8)]
            pe += 8
            pl["aat"] = pe = pe + 1
            pl["mask"] = ve = ve + 1
            pl["pair"] = []
            pl["qupd"] = []
            for lvl in range(5):
                pe += 1
                pl["pair"].append(pe)
                pe += 1
                pl["qupd"].append(pe)
            pl["wu"] = [pe + 1 + i for i in range(CPS)]
            pe += CPS
        if hasA:
            pl["scan"] = ve = ve + 1
        if hasB1:
            pl["tt"] = ve = ve + 1
            pl["wuev"] = [ve + 1 + i for i in range(CPS)]
            ve += CPS
        if hasA:
            pl["rmsA"] = pe = pe + 1
            pl["rmsB"] = pe = pe + 1
            pl["scale"] = ve = ve + 1
            pl["wo"] = [pe + 1 + i for i in range(8)]
            pe += 8
            pl["ln1mm"] = pe = pe + 1
        if hasB:
            pl["apply"] = ve = ve + 1
        if hasB:
            pl["muln2"] = ac = ac + 1
            pl["lnexp2"] = ac = ac + 1
        if hasB1:
            pl["trev"] = [ac + 1 + i for i in range(8)]
            ac += 8
            pl["chcp"] = []
            for lvl in range(5):
                ac += 1
                pl["chcp"].append(ac)
                if lvl < 4:
                    ac += 1
                    pl["chcp"].append(ac)
        if hasA:
            pl["rstdA"] = ac = ac + 1
            pl["rstdB"] = ac = ac + 1
            pl["woev"] = [ac + 1 + i for i in range(8)]
            ac += 8
            pl["muln1"] = ac = ac + 1
            pl["lnexp1"] = ac = ac + 1
        if hasB:
            pl["ln2sq"] = po = po + 1
            pl["mu2ln2"] = po = po + 1
        if hasA:
            pl["qk"] = po = po + 1
            pl["sqrms"] = po = po + 1
            pl["ln1sq"] = po = po + 1
            pl["mu2ln1"] = po = po + 1
        pl["loads"] = p.cnt["sLD"] + (64 if hasB1 else 0)
        pl["x1st"] = p.cnt["sSP"] + (16 if hasB else 0)
        PL[c] = pl
        q1 = PL.get(c1, {})
        q2 = PL.get(c2, {})

        if DEV_DNB and c == 1:
            @block.sync
            def _(sy, q=PL[0]):
                sy.wait_ge(L["sVE"], q["kbvb"])
                sy.dma_start(out=scr["dbg"][:, 0:2048],
                             in_=kb_b[:]).then_inc(L["sDB"], 16)
                sy.dma_start(out=scr["dbg"][:, 2048:4096],
                             in_=vb_b[:]).then_inc(L["sDB"], 16)
                sy.wait_ge(L["sVE"], q["wuev"][-1])
                sy.dma_start(out=scr["dbg"][:, 4096:6144],
                             in_=w_b[:, 0:2048]).then_inc(L["sDB"], 16)
                sy.dma_start(out=scr["dbg"][:, 6144:8192],
                             in_=u_b[:, 0:2048]).then_inc(L["sDB"], 16)
                sy.wait_ge(L["sVE"], q["tt"])
                sy.dma_start(out=scr["dbg"][:, 8192:8320],
                             in_=tt_bf[:]).then_inc(L["sDB"], 16)
            p.cnt["sDB"] += 80
        if DEV_DNB and c == 2:
            @block.sync
            def _(sy, q=PL[1]):
                sy.wait_ge(L["sVE"], q["scale"])
                sy.dma_start(out=scr["dbg"][:, 8704:10752],
                             in_=o_b[:]).then_inc(L["sDB"], 16)
                sy.wait_ge(L["sAE"], q["woev"][7])
                sy.dma_start(out=scr["dbg"][:, 10752:12800],
                             in_=ao_b[:]).then_inc(L["sDB"], 16)
            p.cnt["sDB"] += 32

        @block.sync
        def _(sy, c=c, c2=c2, pl=pl, q1=q1, q2=q2, hasB1=hasB1, hasB=hasB):
            if hasB1:
                if q1 and "scan" in q1:
                    sy.wait_ge(L["sVE"], q1["scan"])
                    sy.wait_ge(L["sPO"], q1["qk"])
                if q2 and "beta" in q2:
                    sy.wait_ge(L["sPE"], q2["beta"])
                for src_nm, dst in (("qn_s", qn_b), ("kn_s", kn_b),
                                    ("v_s", v_b)):
                    sy.dma_start(
                        out=dst[:].rearrange("p (sl ft t) -> p sl ft t",
                                             sl=2, ft=FT)[:, c % 2, :, :],
                        in_=scr[src_nm][:, :, c * SLB:(c + 1) * SLB]
                        ).then_inc(L["sLD"], 16)
                sy.dma_start(out=bet_b[0:4, (c % 2) * SLB:(c % 2 + 1) * SLB],
                             in_=scr["betar_s"][:, c * SLB:(c + 1) * SLB]
                             ).then_inc(L["sLD"], 16)
            if hasB:
                sy.wait_ge(L["sVE"], pl["resid"])
                sy.dma_start(out=scr["x1_s"][:, :, c2 * SLB:(c2 + 1) * SLB],
                             in_=p.stream[:].rearrange("p (ft cc) -> p ft cc",
                                                       ft=FT)
                             [:, :, 1 + c2 * SLB:1 + (c2 + 1) * SLB]
                             ).then_inc(L["sSP"], 16)
        p.cnt["sLD"] = pl["loads"]
        p.cnt["sSP"] = pl["x1st"]

        @block.gpsimd
        def _(g, c=c, c1=c1, c2=c2, pl=pl, q1=q1, hasB1=hasB1, hasA=hasA,
              hasB=hasB):
            if hasB:
                g.wait_ge(L["sVE"], pl["resid"])
                if q1 and "ln1mm" in q1:
                    g.wait_ge(L["sPE"], q1["ln1mm"])
                for ft in range(FT):
                    g.tensor_mul(p.sq[:, ft * SLB:(ft + 1) * SLB],
                                 stile(ft, c2 * SLB, SLB),
                                 stile(ft, c2 * SLB, SLB))
                g.last.then_inc(L["sPO"], 1)
                g.wait_ge(L["sAE"], pl["muln2"])
                g.tensor_mul(rs_c[:], mu_c[:], mu_c[:])
                g.last.then_inc(L["sPO"], 1)
            if hasA:
                if "ln2mm" in pl:
                    g.wait_ge(L["sPE"], pl["ln2mm"])
                g.wait_ge(L["sLD"], q1["loads"])
                for ft in range(FT):
                    g.tensor_mul(p.sq[:, ft * SLB:(ft + 1) * SLB],
                                 slab(qn_b, c1, ft), slab(kn_b, c1, ft))
                g.last.then_inc(L["sPO"], 1)
                g.wait_ge(L["sVE"], pl["scan"])
                g.wait_ge(L["sPE"], pl["rowsum"])
                for ft in range(FT):
                    g.tensor_mul(p.sq[:, ft * SLB:(ft + 1) * SLB],
                                 o_b[:, ft * SLB:(ft + 1) * SLB],
                                 o_b[:, ft * SLB:(ft + 1) * SLB])
                g.last.then_inc(L["sPO"], 1)
                g.wait_ge(L["sAE"], pl["woev"][7])
                g.wait_ge(L["sPE"], pl["rmsB"])
                for ft in range(FT):
                    g.tensor_mul(p.sq[:, ft * SLB:(ft + 1) * SLB],
                                 ao_b[:, ft * SLB:(ft + 1) * SLB],
                                 ao_b[:, ft * SLB:(ft + 1) * SLB])
                g.last.then_inc(L["sPO"], 1)
                g.wait_ge(L["sAE"], pl["muln1"])
                g.tensor_mul(rs_a[:], mu_a[:], mu_a[:])
                g.last.then_inc(L["sPO"], 1)
        p.cnt["sPO"] = po

        @block.tensor
        def _(t, c=c, c1=c1, c2=c2, pl=pl, q1=q1, hasB1=hasB1, hasA=hasA,
              hasB=hasB):
            if hasB:
                t.wait_ge(L["sPO"], pl["ln2sq"])
                if q1 and "muln1" in q1:
                    t.wait_ge(L["sAE"], q1["muln1"])
                t.wait_ge(L["sVE"], pl["subln1"])
                for ft in range(FT):
                    t.matmul(p.pp0[:, 0:SLB], p.c_sw1024[:],
                             stile(ft, c2 * SLB, SLB),
                             start=(ft == 0), stop=(ft == FT - 1))
                for ft in range(FT):
                    t.matmul(p.pp1[:, 0:SLB], p.c_sw1024[:],
                             p.sq[:, ft * SLB:(ft + 1) * SLB],
                             start=(ft == 0), stop=(ft == FT - 1))
                t.last.then_inc(L["sPE"], 1)
            if hasA:
                t.wait_ge(L["sPO"], pl["qk"])
                if hasB:
                    t.wait_ge(L["sAE"], pl["muln2"])
                for ft in range(FT):
                    t.matmul(p.pp0[:, 0:SLB], p.c_ones[:],
                             p.sq[:, ft * SLB:(ft + 1) * SLB],
                             start=(ft == 0), stop=(ft == FT - 1))
                t.last.then_inc(L["sPE"], 1)
            if hasB1:
                t.wait_ge(L["sLD"], pl["loads"])
                if q1 and "wuev" in q1:
                    t.wait_ge(L["sVE"], q1["wuev"][CPS - 1])
                for h in range(4):
                    t.matmul(p.pw1[:, h * SLB:(h + 1) * SLB],
                             p.c_sel4[0:4, h * 128:(h + 1) * 128],
                             bet_b[0:4,
                                   (c % 2) * SLB:(c % 2) * SLB + SLB],
                             start=(h % 2 == 0), stop=(h % 2 == 1),
                             tile_position=(0, 0))
                t.last.then_inc(L["sPE"], 1)
                t.wait_ge(L["sVE"], pl["kbvb"])
                if q1 and "trev" in q1:
                    t.wait_ge(L["sAE"], q1["trev"][7])
                for i in range(8):
                    if i > 0:
                        t.wait_ge(L["sAE"], pl["trev"][i - 1])
                    sb = kb_b if i < 4 else vb_b
                    bb = (i % 4) // 2
                    fh = i % 2
                    for ft in range(fh * 4, fh * 4 + 4):
                        t.matmul(p.pp3[:, (ft % 4) * 128:(ft % 4 + 1) * 128],
                                 sb[:, ft * SLB + bb * 128:
                                    ft * SLB + (bb + 1) * 128],
                                 p.c_id_bf[:], start=(ft % 4 == 0),
                                 stop=(ft % 4 == 3))
                    t.last.then_inc(L["sPE"], 1)
                for rb in (0, 64):
                    order = [(0, 0), (64, 0), (0, 256), (64, 256)]
                    for gi, (cb, base) in enumerate(order):
                        cc = (cb // 64) * 2 + (rb // 64)
                        for fi in range(FT):
                            if base == 0:
                                lhs = kb_b[:, fi * SLB + cc * 64:
                                           fi * SLB + (cc + 1) * 64]
                                rhs = slab(kn_b, c, fi, cc * 64, 64)
                            else:
                                lhs = slab(kn_b, c, fi, cc * 64, 64)
                                rhs = kb_b[:, fi * SLB + cc * 64:
                                           fi * SLB + (cc + 1) * 64]
                            t.matmul(p.pw0[rb:rb + 64,
                                           base + cb:base + cb + 64],
                                     lhs, rhs,
                                     start=(gi == 0 and fi == 0),
                                     stop=(gi == 3 and fi == FT - 1),
                                     tile_position=(0, rb))
                t.last.then_inc(L["sPE"], 1)
                curP, curPt = ch["chL"], ch["chLt"]
                for lvl in range(5):
                    if lvl == 0:
                        t.wait_ge(L["sVE"], pl["mask"])
                    else:
                        t.wait_ge(L["sAE"], pl["chcp"][2 * lvl - 1])
                    for rb in (0, 64):
                        groups = [(0, 0, curPt, curP), (64, 0, curPt, curP)]
                        if lvl < 4:
                            groups += [(0, 256, curP, curPt),
                                       (64, 256, curP, curPt)]
                        for gi, (cb, base, lh, rh) in enumerate(groups):
                            t.matmul(p.pw1[rb:rb + 64,
                                           base + cb:base + cb + 64],
                                     lh[rb:rb + 64, cb:cb + 64],
                                     rh[rb:rb + 64, cb:cb + 64],
                                     start=(gi == 0),
                                     stop=(gi == len(groups) - 1),
                                     tile_position=(rb, rb))
                    t.last.then_inc(L["sPE"], 1)
                    t.wait_ge(L["sAE"],
                              pl["chcp"][2 * lvl] if lvl < 4
                              else pl["chcp"][8])
                    for rb in (0, 64):
                        for gi, cb in enumerate((0, 64)):
                            t.matmul(p.pw0[rb:rb + 64, cb:cb + 64],
                                     p.c_id_f[rb:rb + 64, rb:rb + 64],
                                     ch["chQ"][rb:rb + 64, cb:cb + 64],
                                     start=(gi == 0), stop=False,
                                     tile_position=(rb, rb))
                            t.matmul(p.pw0[rb:rb + 64, cb:cb + 64],
                                     ch["chP"][rb:rb + 64, cb:cb + 64],
                                     ch["chQ"][rb:rb + 64, cb:cb + 64],
                                     start=False, stop=(gi == 1),
                                     tile_position=(rb, rb))
                    t.last.then_inc(L["sPE"], 1)
                    curP, curPt = ch["chP"], ch["chPt"]
                t.wait_ge(L["sVE"], pl["tt"])
                for cc in range(CPS):
                    if cc >= 2:
                        t.wait_ge(L["sVE"], pl["wuev"][cc - 2])
                    rb, cb = (cc % 2) * 64, (cc // 2) * 64
                    bb = cc // 2
                    wps = (p.pw1[:, 0:512] if cc % 2 == 0
                           else p.pw1[:, 512:1024])
                    ups = (p.pw0[:, 512:1024] if cc % 2 == 0
                           else p.pp2[:, 0:512])
                    for ft in range(FT):
                        t.matmul(wps[:, ft * 64:(ft + 1) * 64],
                                 kpm[rb:rb + 64, bb * FT * 128 + ft * 128:
                                     bb * FT * 128 + (ft + 1) * 128],
                                 tt_bf[rb:rb + 64, cb:cb + 64],
                                 start=(ft == 0), stop=(ft == FT - 1),
                                 tile_position=(rb, 0))
                    for ft in range(FT):
                        t.matmul(ups[:, ft * 64:(ft + 1) * 64],
                                 vpm[rb:rb + 64, bb * FT * 128 + ft * 128:
                                     bb * FT * 128 + (ft + 1) * 128],
                                 tt_bf[rb:rb + 64, cb:cb + 64],
                                 start=(ft == 0), stop=(ft == FT - 1),
                                 tile_position=(rb, 0))
                    t.last.then_inc(L["sPE"], 1)
            if hasA:
                t.wait_ge(L["sPO"], pl["sqrms"])
                t.wait_ge(L["sVE"], pl["dm"])
                if hasB:
                    t.wait_ge(L["sVE"], pl["subln2"])
                for rnd in range(2):
                    if rnd == 1:
                        t.wait_ge(L["sAE"], pl["rstdA"])
                    for hh in range(2):
                        h = rnd * 2 + hh
                        ps = p.pp0 if hh == 0 else p.pp1
                        for k in range(2):
                            t.matmul(ps[:, 0:SLB], p.c_sw256[:],
                                     p.sq[:, (2 * h + k) * SLB:
                                          (2 * h + k + 1) * SLB],
                                     start=(k == 0), stop=(k == 1))
                    t.last.then_inc(L["sPE"], 1)
                if c == 1:
                    t.wait_ge(L["sWT"], wo_w)
                t.wait_ge(L["sVE"], pl["scale"])
                for fo in range(FT):
                    if fo >= 2:
                        t.wait_ge(L["sAE"], pl["woev"][fo - 2])
                    ps = p.pp0 if fo % 2 == 0 else p.pp1
                    for fi in range(FT):
                        t.matmul(ps[:, 0:SLB], wotile(fi, fo),
                                 o_b[:, fi * SLB:(fi + 1) * SLB],
                                 start=(fi == 0), stop=(fi == FT - 1))
                    t.last.then_inc(L["sPE"], 1)
                t.wait_ge(L["sPO"], pl["ln1sq"])
                for ft in range(FT):
                    t.matmul(p.pp0[:, 0:SLB], p.c_sw1024[:],
                             ao_b[:, ft * SLB:(ft + 1) * SLB],
                             start=(ft == 0), stop=(ft == FT - 1))
                for ft in range(FT):
                    t.matmul(p.pp1[:, 0:SLB], p.c_sw1024[:],
                             p.sq[:, ft * SLB:(ft + 1) * SLB],
                             start=(ft == 0), stop=(ft == FT - 1))
                t.last.then_inc(L["sPE"], 1)
        p.cnt["sPE"] = pe

        @block.scalar
        def _(sc, c=c, c1=c1, c2=c2, pl=pl, q1=q1, hasB1=hasB1, hasA=hasA,
              hasB=hasB, ve_end=ve):
            if hasB:
                sc.wait_ge(L["sPE"], pl["ln2mm"])
                sc.copy(mu_c[:], p.pp0[:, 0:SLB]).then_inc(L["sAE"], 1)
                sc.wait_ge(L["sVE"], pl["subln2"])
                sc.activation(rs_c[:], rs_c[:], AF.Ln, bias=LEPS)
                sc.activation(rs_c[:], rs_c[:], AF.Exp, scale=-0.5)
                sc.last.then_inc(L["sAE"], 1)
            if hasB1:
                for i in range(8):
                    sc.wait_ge(L["sPE"], pl["tr"][i])
                    dst = kpm if i < 4 else vpm
                    bb = (i % 4) // 2
                    fh = i % 2
                    sc.copy(dst[:, bb * FT * 128 + fh * 512:
                                bb * FT * 128 + fh * 512 + 512],
                            p.pp3[:, 0:512]).then_inc(L["sAE"], 1)
                for lvl in range(5):
                    sc.wait_ge(L["sPE"], pl["pair"][lvl])
                    sc.copy(ch["chP"][:], p.pw1[:, 0:CHW])
                    if lvl < 4:
                        sc.copy(ch["chPt"][:], p.pw1[:, 256:256 + CHW])
                    sc.last.then_inc(L["sAE"], 1)
                    if lvl < 4:
                        sc.wait_ge(L["sPE"], pl["qupd"][lvl])
                        sc.copy(ch["chQ"][:],
                                p.pw0[:, 0:CHW]).then_inc(L["sAE"], 1)
            if hasA:
                sc.wait_ge(L["sPE"], pl["rmsA"])
                if q1 and "scale" in q1:
                    sc.wait_ge(L["sVE"], q1["scale"])
                for hh in range(2):
                    ps = p.pp0 if hh == 0 else p.pp1
                    dst = [p.gs1, p.gs2][hh]
                    sc.activation(dst[:, 0:SLB], ps[:, 0:SLB], AF.Ln,
                                  bias=LEPS)
                    sc.activation(dst[:, 0:SLB], dst[:, 0:SLB], AF.Exp,
                                  scale=-0.5)
                sc.last.then_inc(L["sAE"], 1)
                sc.wait_ge(L["sPE"], pl["rmsB"])
                for hh in range(2):
                    ps = p.pp0 if hh == 0 else p.pp1
                    dst = [p.gs3, p.gs4][hh]
                    sc.activation(dst[:, 0:SLB], ps[:, 0:SLB], AF.Ln,
                                  bias=LEPS)
                    sc.activation(dst[:, 0:SLB], dst[:, 0:SLB], AF.Exp,
                                  scale=-0.5)
                sc.last.then_inc(L["sAE"], 1)
                for fo in range(FT):
                    sc.wait_ge(L["sPE"], pl["wo"][fo])
                    if fo == 0 and DEV_DNB and c == 2:
                        sc.wait_ge(L["sDB"], p.cnt["sDB"])
                    if fo == 0 and hasB:
                        sc.wait_ge(L["sVE"], pl["resid"])
                    ps = p.pp0 if fo % 2 == 0 else p.pp1
                    sc.copy(ao_b[:, fo * SLB:(fo + 1) * SLB],
                            ps[:, 0:SLB]).then_inc(L["sAE"], 1)
                sc.wait_ge(L["sPE"], pl["ln1mm"])
                if hasB:
                    sc.wait_ge(L["sVE"], pl["resid"])
                sc.copy(mu_a[:], p.pp0[:, 0:SLB]).then_inc(L["sAE"], 1)
                sc.wait_ge(L["sVE"], ve_end + 1)
                sc.activation(rs_a[:], rs_a[:], AF.Ln, bias=LEPS)
                sc.activation(rs_a[:], rs_a[:], AF.Exp, scale=-0.5)
                sc.last.then_inc(L["sAE"], 1)
        p.cnt["sAE"] = ac

        @block.vector
        def _(v, c=c, c1=c1, c2=c2, pl=pl, q1=q1, q2=q2, hasB1=hasB1,
              hasA=hasA, hasB=hasB):
            if hasB:
                v.wait_ge(L["sPO"], q1["mu2ln1"])
                v.wait_ge(L["sPE"], q1["ln1mm"])
                v.tensor_sub(rs_a[:], p.pp1[:, 0:SLB], rs_a[:])
                v.last.then_inc(L["sVE"], 1)
                v.wait_ge(L["sAE"], q1["lnexp1"])
                for ft in range(FT):
                    v.tensor_sub(lt_b[:], ao_b[:, ft * SLB:(ft + 1) * SLB],
                                 mu_a[:])
                    v.tensor_mul(lt_b[:], lt_b[:], rs_a[:])
                    v.tensor_add(stile(ft, c2 * SLB, SLB), lt_b[:],
                                 stile(ft, c2 * SLB, SLB))
                v.last.then_inc(L["sVE"], 1)
            if hasA:
                v.wait_ge(L["sPE"], pl["rowsum"])
                dm = p.c_dmask[:].unsqueeze(1).broadcast_to((128, CPS, 64))
                v.tensor_mul(b3(dm_b[:]), b3(p.pp0[:, 0:SLB]), dm)
                v.last.then_inc(L["sVE"], 1)
            if hasB1:
                v.wait_ge(L["sPE"], pl["beta"])
                if DEV_DNB and c == 1:
                    v.wait_ge(L["sDB"], p.cnt["sDB"])
                for ft in range(FT):
                    ps = p.pw1[:, (ft // 2) * SLB:(ft // 2 + 1) * SLB]
                    v.tensor_mul(kb_b[:, ft * SLB:(ft + 1) * SLB],
                                 slab(kn_b, c, ft), ps)
                    v.tensor_mul(vb_b[:, ft * SLB:(ft + 1) * SLB],
                                 slab(v_b, c, ft), ps)
                v.last.then_inc(L["sVE"], 1)
            if hasB:
                v.wait_ge(L["sPO"], pl["mu2ln2"])
                v.tensor_sub(rs_c[:], p.pp1[:, 0:SLB], rs_c[:])
                v.last.then_inc(L["sVE"], 1)
            if hasB1:
                v.wait_ge(L["sPE"], pl["aat"])
                bc = lambda x: x[:].unsqueeze(1).broadcast_to(
                    (128, NPAIR, 64))
                v.tensor_mul(b3(ch["chL"][:]), b3(p.pw0[:, 0:CHW]),
                             bc(p.c_maskL))
                v.tensor_mul(b3(ch["chLt"][:]), b3(p.pw0[:, 256:256 + CHW]),
                             bc(p.c_maskU))
                v.scalar_tensor_tensor(b3(ch["chB"][:]), b3(ch["chLt"][:]),
                                       -1.0, bc(p.c_eye1), ALU.mult, ALU.add)
                v.tensor_copy(ch["chQ"][:], ch["chB"][:])
                v.last.then_inc(L["sVE"], 1)
            if hasA:
                if DEV_DNB and c == 2:
                    v.wait_ge(L["sDB"], p.cnt["sDB"])
                if q1 and "wo" in q1:
                    v.wait_ge(L["sPE"], q1["wo"][7])
                    v.wait_ge(L["sPO"], q1["sqrms"])
                for ft in range(FT):
                    v.tensor_mul(p.gs3[:, 0:SLB], slab(kn_b, c1, ft),
                                 slab(w_b, c1, ft))
                    v.reduce_sum(aa_b[:, ft * CPS:(ft + 1) * CPS],
                                 b3(p.gs3[:, 0:SLB]), axis=AX.X)
                    v.tensor_mul(p.gs3[:, 0:SLB], slab(kn_b, c1, ft),
                                 slab(u_b, c1, ft))
                    v.reduce_sum(bb_b[:, ft * CPS:(ft + 1) * CPS],
                                 b3(p.gs3[:, 0:SLB]), axis=AX.X)
                v.tensor_scalar(aa_b[:], aa_b[:], -1.0, 1.0, ALU.mult,
                                ALU.add)
                sd = p.sdt[:].rearrange("p (ft cc) -> p ft cc", ft=FT)
                for ft in range(FT):
                    v.tensor_tensor_scan(sd[:, ft, 1:5],
                                         aa_b[:, ft * CPS:(ft + 1) * CPS],
                                         bb_b[:, ft * CPS:(ft + 1) * CPS],
                                         sd[:, ft, 0:1], ALU.mult, ALU.add)
                for ft in range(FT):
                    sdexp = sd[:, ft, 0:4].unsqueeze(2).broadcast_to(
                        (128, CPS, 64))
                    v.tensor_mul(b3(p.gs3[:, 0:SLB]),
                                 b3(slab(w_b, c1, ft)), sdexp)
                    v.tensor_sub(p.gs4[:, 0:SLB], slab(u_b, c1, ft),
                                 p.gs3[:, 0:SLB])
                    v.tensor_mul(p.gs4[:, 0:SLB], p.gs4[:, 0:SLB], dm_b[:])
                    v.tensor_mul(b3(p.gs3[:, 0:SLB]),
                                 b3(slab(qn_b, c1, ft)), sdexp)
                    v.tensor_add(o_b[:, ft * SLB:(ft + 1) * SLB],
                                 p.gs3[:, 0:SLB], p.gs4[:, 0:SLB])
                for ft in range(FT):
                    v.tensor_copy(sd[:, ft, 0:1], sd[:, ft, 4:5])
                v.last.then_inc(L["sVE"], 1)
            if hasB1:
                v.wait_ge(L["sPE"], pl["qupd"][4])
                e2 = p.c_eye2[:].unsqueeze(1).broadcast_to((128, NPAIR, 64))
                v.scalar_tensor_tensor(b3(tt_bf[:]), b3(p.pw0[:, 0:CHW]), 3.0,
                                       e2, ALU.mult, ALU.subtract)
                v.last.then_inc(L["sVE"], 1)
                for cc in range(CPS):
                    v.wait_ge(L["sPE"], pl["wu"][cc])
                    wps = (p.pw1[:, 0:512] if cc % 2 == 0
                           else p.pw1[:, 512:1024])
                    ups = (p.pw0[:, 512:1024] if cc % 2 == 0
                           else p.pp2[:, 0:512])
                    v.tensor_copy(
                        w_b[:].rearrange("p (sl ft t) -> p sl ft t", sl=2,
                                         ft=FT)[:, c % 2, :,
                                                cc * 64:(cc + 1) * 64],
                        wps.rearrange("p (ft k) -> p ft k", ft=FT))
                    v.tensor_copy(
                        u_b[:].rearrange("p (sl ft t) -> p sl ft t", sl=2,
                                         ft=FT)[:, c % 2, :,
                                                cc * 64:(cc + 1) * 64],
                        ups.rearrange("p (ft k) -> p ft k", ft=FT))
                    v.last.then_inc(L["sVE"], 1)
            if hasA:
                v.wait_ge(L["sAE"], pl["rstdB"])
                for ft in range(FT):
                    rs = [p.gs1, p.gs2, p.gs3, p.gs4][ft // 2]
                    v.scalar_tensor_tensor(o_b[:, ft * SLB:(ft + 1) * SLB],
                                           o_b[:, ft * SLB:(ft + 1) * SLB],
                                           p.g_pp[:, (ft % 2):(ft % 2) + 1],
                                           rs[:, 0:SLB], ALU.mult, ALU.mult)
                v.last.then_inc(L["sVE"], 1)
            if hasB:
                v.wait_ge(L["sAE"], pl["lnexp2"])
                v.wait_ge(L["sSP"], pl["x1st"])
                v.tensor_mul(lt_b[:], mu_c[:], rs_c[:])
                for ft in range(FT):
                    v.tensor_mul(stile(ft, c2 * SLB, SLB),
                                 stile(ft, c2 * SLB, SLB), rs_c[:])
                    v.tensor_sub(stile(ft, c2 * SLB, SLB),
                                 stile(ft, c2 * SLB, SLB), lt_b[:])
                v.last.then_inc(L["sVE"], 1)
        p.cnt["sVE"] = ve

    fence(p, block)
    es.close()


def emit_ffn_attn(p, block, din, scr, layer, stile):
    nc = p.nc
    L = p.sems
    es = ExitStack()
    wqkv_sb = es.enter_context(nc.sbuf_tensor(f"wqkv_sb{layer}", [128, FT * 12 * 128],
                                              BF16))
    fwo_sb = es.enter_context(nc.sbuf_tensor(f"fwo_sb{layer}", [128, 4 * FT * 128],
                                             BF16))
    tbl = es.enter_context(nc.sbuf_tensor(f"tbl{layer}", [128, 4 * 512], BF16))
    qw = es.enter_context(nc.sbuf_tensor(f"qw{layer}", [128, 4 * 512], BF16))
    qraw = es.enter_context(nc.sbuf_tensor(f"qraw{layer}", [128, 2 * 4 * 512], BF16))
    kr = es.enter_context(nc.sbuf_tensor(f"kr{layer}", [128, 3 * 4 * 512], BF16))
    vtm = es.enter_context(nc.sbuf_tensor(f"vtm{layer}", [128, 3 * 4 * 512], BF16))
    prT = es.enter_context(nc.sbuf_tensor(f"prT{layer}", [128, 8 * 512], BF16))
    o_n = es.enter_context(nc.sbuf_tensor(f"o_n{layer}", [128, 4 * 512], BF16))
    wo_buf = es.enter_context(nc.sbuf_tensor(f"wo_buf{layer}", [128, FT * 512], BF16))
    negmax = es.enter_context(nc.sbuf_tensor(f"negmax{layer}", [128, 8], F32))
    negrow = es.enter_context(nc.sbuf_tensor(f"negrow{layer}", [1, 512], BF16))
    p.c_mqk = es.enter_context(nc.sbuf_tensor(f"mqk{layer}", [128, 4096],
                                               BF16))
    p.c_mt = es.enter_context(nc.sbuf_tensor(f"mt{layer}", [128, 4096], BF16))
    p.gs5 = es.enter_context(nc.sbuf_tensor(f"gs5{layer}", [128, 512], F32))
    p.gs6 = es.enter_context(nc.sbuf_tensor(f"gs6{layer}", [128, 512], F32))
    p.gs7 = es.enter_context(nc.sbuf_tensor(f"gs7{layer}", [128, 512], F32))

    def krs(w3, ft):
        return kr[:, (w3 * 4 + ft) * 512:(w3 * 4 + ft + 1) * 512]

    def vts(w3, tb):
        return vtm[:, (w3 * 4 + tb) * 512:(w3 * 4 + tb + 1) * 512]

    def qrs(w2, ft):
        return qraw[:, (w2 * 4 + ft) * 512:(w2 * 4 + ft + 1) * 512]

    def qws(w2, ft):
        return qw[:, ft * 512:(ft + 1) * 512]

    @block.gpsimd
    def _(g):
        p.w(g, "sPE")
        p.w(g, "sVE")
        p.w(g, "sAE")
        g.dma_start(out=wqkv_sb[:],
                    in_=din["fwqkv"][layer]).then_inc(L["sWT"], 16)
        g.dma_start(out=fwo_sb[:],
                    in_=din["fwo"][layer]).then_inc(L["sWT"], 16)
    p.bump("sWT", 32)
    wt_w = p.cnt["sWT"]

    @block.sync
    def _(sy):
        p.w(sy, "sPE")
        p.w(sy, "sVE")
        p.w(sy, "sAE")
        sy.dma_start(out=p.c_mqk[:], in_=din["mqk"]).then_inc(L["sLD"], 16)
        sy.dma_start(out=p.c_mt[:], in_=din["mt"]).then_inc(L["sLD"], 16)
    p.bump("sLD", 32)
    msk_w = p.cnt["sLD"]

    def qkvtile(fi, fo):
        return wqkv_sb[:].rearrange("p (a b q) -> p a b q", a=FT,
                                    b=12)[:, fi, fo, :]

    def wotile(fi, fo):
        return fwo_sb[:].rearrange("p (a b q) -> p a b q", a=4,
                                   b=FT)[:, fi, fo, :]

    NWE = DEV_NW if DEV_NW else NW

    def emit_proj(w, pre_counts=None):
        t0 = w * WIN
        w2, w3 = w % 2, w % 3
        if pre_counts is None:
            pre_counts = [(nm, p.cnt[nm]) for nm in ("sVE", "sAE")]

        def slot(i):
            return p.pw0[:, 0:512] if i % 2 == 0 else p.pw0[:, 512:1024]

        def hsl(fi, tt, n):
            off = 0 if fi < 4 else -1
            return stile(fi, tt + off, n)

        def mm_qk(t, i):
            j, fo = divmod(i, 4)
            for fi in range(FT):
                t.matmul(slot(i), qkvtile(fi, 4 * j + fo),
                         hsl(fi, t0, WIN), start=(fi == 0), stop=(fi == FT - 1))

        def ev_qk(e, i):
            j, fo = divmod(i, 4)
            dst = qrs(w2, fo) if j == 0 else krs(w3, fo)
            return e.copy(dst, slot(i))

        if w == 0:
            @block.tensor
            def _(t):
                t.wait_ge(L["sWT"], wt_w)
        pipe(p, block, 8, mm_qk, ev_qk, "scalar", depth=2,
             pre_counts=pre_counts)

        def mm_v(t, tb):
            for fi in range(FT):
                off = 0 if fi < 4 else -1
                t.matmul(slot(tb), stile(fi, t0 + tb * 128 + off, 128),
                         wqkv_sb[:].rearrange("p (a bq) -> p a bq", a=FT)
                         [:, fi, 8 * 128:12 * 128],
                         start=(fi == 0), stop=(fi == FT - 1))

        def ev_v(e, tb):
            return e.copy(vts(w3, tb), slot(tb))

        pipe(p, block, 4, mm_v, ev_v, "scalar", depth=2, pre_wait=("sAE",))

        @block.gpsimd
        def _(g):
            p.w(g, "sAE")
            for i in range(8):
                j, fo = divmod(i, 4)
                srcb = qrs(w2, fo) if j == 0 else krs(w3, fo)
                g.tensor_mul(p.sq[:, i * 512:(i + 1) * 512], srcb, srcb)
            g.last.then_inc(L["sPO"], 1)
        p.bump("sPO")

    for w in range(NWE):
        t0 = w * WIN
        w2, w3 = w % 2, w % 3
        w3p = (w - 1) % 3
        first = (w == 0)

        @block.gpsimd
        def _(g, t0=t0):
            p.w(g, "sVE")
            for i, nm in enumerate(("csq", "ssq", "csk", "ssk")):
                g.dma_start(out=tbl[:, i * 512:(i + 1) * 512],
                            in_=din[nm][:, t0:t0 + WIN]).then_inc(L["sLG"], 16)
        p.bump("sLG", 64)
        tbl_w = p.cnt["sLG"]

        if w == 0:
            emit_proj(0)

        peb_r = p.cnt["sPE"]
        aeb_r = p.cnt["sAE"]
        veb_r = p.cnt["sVE"]
        pob_r = p.cnt["sPO"]

        @block.tensor
        def _(t, w2=w2, w3=w3):
            p.w(t, "sPO", pob_r)
            p.w(t, "sAE", aeb_r)
            p.w(t, "sVE", veb_r)
            for j in range(2):
                for ft in range(4):
                    ps = [p.pp0, p.pp1, p.pp2, p.pp3][ft]
                    t.matmul(ps[:, 0:512], p.c_blk64[:],
                             p.sq[:, (4 * j + ft) * 512:(4 * j + ft + 1) * 512],
                             start=True, stop=True)
                t.last.then_inc(L["sPE"], 1)
                for ft in range(4):
                    src = qrs(w2, ft) if j == 0 else krs(w3, ft)
                    half = (j * 4 + ft) % 2
                    if j * 4 + ft >= 2:
                        t.wait_ge(L["sVE"], veb_r + (j * 4 + ft) - 1)
                    t.matmul(p.pw1[:, half * 512:half * 512 + 512],
                             p.c_prot[:], src, start=True, stop=True)
                    t.last.then_inc(L["sPE"], 1)
        p.cnt["sPE"] += 2 * 5

        @block.scalar
        def _(sc):
            for j in range(2):
                sc.wait_ge(L["sPE"], peb_r + j * 5 + 1)
                for ft in range(4):
                    ps = [p.pp0, p.pp1, p.pp2, p.pp3][ft]
                    dst = [p.gs1, p.gs2, p.gs3, p.gs4][ft]
                    sc.activation(dst[:, 0:512], ps[:, 0:512], AF.Ln)
                    sc.activation(dst[:, 0:512], dst[:, 0:512], AF.Exp,
                                  scale=-0.5)
                sc.last.then_inc(L["sAE"], 1)
        p.cnt["sAE"] += 2

        @block.vector
        def _(v, w2=w2, w3=w3, tbl_w=tbl_w):
            v.wait_ge(L["sLG"], tbl_w)
            for j in range(2):
                spp = p.qs_pp if j == 0 else p.ks_pp
                srpp = p.qsr_pp if j == 0 else p.ksr_pp
                csoff = 0 if j == 0 else 1024
                ssoff = 512 if j == 0 else 1536
                v.wait_ge(L["sAE"], aeb_r + j + 1)
                for ft in range(4):
                    rs = [p.gs1, p.gs2, p.gs3, p.gs4][ft]
                    v.tensor_scalar_min(rs[:, 0:512], rs[:, 0:512], 1e12)
                    src = qrs(w2, ft) if j == 0 else krs(w3, ft)
                    dst = qws(w2, ft) if j == 0 else src
                    half = (j * 4 + ft) % 2
                    v.wait_ge(L["sPE"], peb_r + j * 5 + 1 + ft + 1)
                    v.scalar_tensor_tensor(p.gs5[:, 0:512], rs[:, 0:512],
                                           spp[:, layer:layer + 1],
                                           tbl[:, csoff:csoff + 512],
                                           ALU.mult, ALU.mult)
                    v.scalar_tensor_tensor(p.gs6[:, 0:512], rs[:, 0:512],
                                           srpp[:, layer:layer + 1],
                                           tbl[:, ssoff:ssoff + 512],
                                           ALU.mult, ALU.mult)
                    v.tensor_mul(p.gs7[:, 0:512], src, p.gs5[:, 0:512])
                    v.tensor_mul(p.gs5[:, 0:512],
                                 p.pw1[:, half * 512:half * 512 + 512],
                                 p.gs6[:, 0:512])
                    v.tensor_add(dst, p.gs7[:, 0:512],
                                 p.gs5[:, 0:512]).then_inc(L["sVE"], 1)
        p.cnt["sVE"] += 8

        if w + 1 < NWE:
            emit_proj(w + 1, pre_counts=[("sVE", veb_r), ("sAE", aeb_r)])

        if w > 0:
            @block.vector
            def _(v, t0=t0):
                p.w(v, "sPE")
                p.w(v, "sAE")
                for ft in range(FT):
                    v.tensor_add(stile(ft, t0 - WIN, WIN),
                                 stile(ft, t0 - WIN, WIN),
                                 wo_buf[:, ft * 512:(ft + 1) * 512])
                v.last.then_inc(L["sVE"], 1)
            p.bump("sVE")

        if w == 1 or (NWE == 1 and w == 0):
            for side, (srcn, dstn) in enumerate((("w1a", "w1a_bf"),
                                                 ("w1g", "w1g_bf"))):
                for piece in range(8):
                    @block.gpsimd
                    def _(g, srcn=srcn, piece=piece, side=side):
                        if piece == 0 and side == 0:
                            p.w(g, "sPE")
                            p.w(g, "sAE")
                            p.w(g, "sVE")
                        g.wait_ge(L["sSP"], p.cnt["sSP"])
                        g.dma_start(out=prT[:, 0:HID2],
                                    in_=din[srcn][layer].rearrange(
                                        "(a p) h -> p a h", p=128)[:, piece, :]
                                    ).then_inc(L["sLG"], 16)
                    p.bump("sLG", 16)

                    @block.sync
                    def _(sy, dstn=dstn, piece=piece):
                        p.w(sy, "sLG")
                        sy.dma_start(out=scr[dstn][layer].rearrange(
                            "(a p) h -> p a h", p=128)[:, piece, :],
                            in_=prT[:, 0:HID2]).then_inc(L["sSP"], 16)
                    p.bump("sSP", 16)
            p.w1cast_done = p.cnt["sSP"]

        heads = [] if DEV_SKIP_HEADS else list(range(H2))
        kbs = list(range(8)) if not first else list(range(4, 8))
        lenk = len(kbs)
        peb = p.cnt["sPE"]
        veb = p.cnt["sVE"]
        aeb = p.cnt["sAE"]
        pe_per = 4 + 1 + lenk + 1 + 1
        ae_per = lenk + 1

        def ve_cnt(h):
            full = 0
            for hh in range(h + 1):
                full += 5 + (1 if hh % 2 == 1 else 0)
            return veb + full

        if heads:
            @block.tensor
            def _(t, w2=w2, w3=w3, w3p=w3p, par_first=first, kbs=kbs,
                  lenk=lenk, peb=peb, veb=veb, aeb=aeb, pe_per=pe_per,
                  ae_per=ae_per, ve_cnt=ve_cnt, w=w):
                p.w(t, "sVE", veb)
                p.w(t, "sAE", aeb)
                if w == 0:
                    t.wait_ge(L["sLD"], msk_w)
                for h in range(H2):
                    ftq = h // 2
                    rb = (h % 2) * 64
                    hb = peb + h * pe_per
                    vh = ve_cnt(h - 1) if h > 0 else veb
                    for qt in range(4):
                        g = h * 4 + qt
                        if g >= 2:
                            h2, qt2 = divmod(g - 2, 4)
                            t.wait_ge(L["sVE"],
                                      (ve_cnt(h2 - 1) if h2 > 0 else veb)
                                      + qt2 + 1)
                        if qt % 2 == 1 and h > 0:
                            t.wait_ge(L["sAE"], aeb + h * ae_per)
                        if par_first:
                            ranges = ((512, w3),)
                        else:
                            ranges = ((0, w3p), (512, w3))
                        for koff, kp in ranges:
                            if qt % 2 == 0:
                                dst = p.pw0[:, koff:koff + 512]
                            elif par_first or koff == 512:
                                dst = p.pp3[:, 0:512]
                            else:
                                dst = p.pp2[:, 0:512]
                            t.matmul(dst, p.c_id_bf[:],
                                     p.c_mqk[:, qt * 1024 + koff:
                                             qt * 1024 + koff + 512],
                                     start=True, stop=False)
                            t.matmul(dst,
                                     qw[rb:rb + 64,
                                        ftq * 512 + qt * 128:
                                        ftq * 512 + (qt + 1) * 128],
                                     krs(kp, ftq)[rb:rb + 64, :],
                                     start=False, stop=True,
                                     tile_position=(rb, 0))
                        t.last.then_inc(L["sPE"], 1)
                    t.wait_ge(L["sVE"], vh + 4)
                    if h > 0:
                        t.wait_ge(L["sAE"], aeb + h * ae_per)
                    for qt in range(4):
                        t.matmul(p.pp2[0:1, qt * 128:(qt + 1) * 128],
                                 negmax[:, qt:qt + 1], p.c_id_f[:],
                                 start=(qt == 0), stop=(qt == 3))
                    t.last.then_inc(L["sPE"], 1)
                    for i, kb in enumerate(kbs):
                        gi = h * lenk + i
                        if gi >= 2:
                            t.wait_ge(L["sAE"], aeb + (gi - 2) + 1 +
                                      ((gi - 2) // lenk))
                        if i == 0:
                            t.wait_ge(L["sVE"], vh + 5)
                        ps = p.pp0 if i % 2 == 0 else p.pp1
                        t.matmul(ps[:, 0:512], p.c_id_bf[:],
                                 p.c_mt[:, kb * 512:(kb + 1) * 512],
                                 start=True, stop=False)
                        t.matmul(ps[:, 0:512], p.c_ones1[:], negrow[0:1, :],
                                 start=False, stop=False)
                        kp = w3p if kb < 4 else w3
                        ko = (kb % 4) * 128
                        t.matmul(ps[:, 0:512],
                                 krs(kp, ftq)[rb:rb + 64, ko:ko + 128],
                                 qw[rb:rb + 64,
                                    ftq * 512:(ftq + 1) * 512],
                                 start=False, stop=True, tile_position=(rb, 0))
                        t.last.then_inc(L["sPE"], 1)
                    t.wait_ge(L["sAE"], aeb + h * ae_per + lenk)
                    for i, kb in enumerate(kbs):
                        t.matmul(p.pp2[:, 0:512], p.c_ones[:],
                                 prT[:, kb * 512:(kb + 1) * 512],
                                 start=(i == 0), stop=(i == lenk - 1))
                    t.last.then_inc(L["sPE"], 1)
                    if h % 2 == 0 and h > 0:
                        t.wait_ge(L["sVE"], ve_cnt(h - 1))
                    for i, kb in enumerate(kbs):
                        vp = w3p if kb < 4 else w3
                        vb = kb % 4
                        t.matmul(p.pw1[rb:rb + 64, 0:512],
                                 vts(vp, vb)[:, h * 64:(h + 1) * 64],
                                 prT[:, kb * 512:(kb + 1) * 512],
                                 start=(i == 0), stop=(i == lenk - 1),
                                 tile_position=(0, rb))
                    t.last.then_inc(L["sPE"], 1)
            p.cnt["sPE"] += H2 * pe_per

            @block.vector
            def _(v, par_first=first, lenk=lenk, peb=peb, pe_per=pe_per,
                  aeb=aeb, ae_per=ae_per):
                for h in range(H2):
                    hb = peb + h * pe_per
                    for qt in range(4):
                        v.wait_ge(L["sPE"], hb + qt + 1)
                        if qt % 2 == 0:
                            src_ = (p.pw0[:, 512:1024] if par_first else
                                    p.pw0[:, 0:1024])
                            v.reduce_max(negmax[:, qt:qt + 1], src_, axis=AX.X,
                                         negate=True).then_inc(L["sVE"], 1)
                        elif par_first:
                            v.reduce_max(negmax[:, qt:qt + 1], p.pp3[:, 0:512],
                                         axis=AX.X,
                                         negate=True).then_inc(L["sVE"], 1)
                        else:
                            v.reduce_max(negmax[:, 4:5], p.pp2[:, 0:512],
                                         axis=AX.X, negate=True)
                            v.reduce_max(negmax[:, 5:6], p.pp3[:, 0:512],
                                         axis=AX.X, negate=True)
                            v.tensor_tensor(negmax[:, qt:qt + 1],
                                            negmax[:, 4:5], negmax[:, 5:6],
                                            ALU.min).then_inc(L["sVE"], 1)
                    v.wait_ge(L["sPE"], hb + 5)
                    v.tensor_copy(negrow[0:1, :],
                                  p.pp2[0:1, 0:512]).then_inc(L["sVE"], 1)
                    if h % 2 == 1:
                        ftq = h // 2
                        v.wait_ge(L["sPE"], hb + pe_per)
                        v.wait_ge(L["sAE"], aeb + (h + 1) * ae_per)
                        v.tensor_mul(o_n[:, ftq * 512:(ftq + 1) * 512],
                                     p.pw1[:, 0:512],
                                     p.gs7[:, 0:512]).then_inc(L["sVE"], 1)
            p.cnt["sVE"] = ve_cnt(H2 - 1)

            @block.scalar
            def _(sc, w=w, kbs=kbs, lenk=lenk, peb=peb, pe_per=pe_per,
                  ve_cnt=ve_cnt):
                if w == 1:
                    sc.wait_ge(L["sSP"], p.w1cast_done)
                for h in range(H2):
                    hb = peb + h * pe_per
                    rb = (h % 2) * 64
                    for i, kb in enumerate(kbs):
                        sc.wait_ge(L["sPE"], hb + 5 + i + 1)
                        sc.activation(prT[:, kb * 512:(kb + 1) * 512],
                                      (p.pp0 if i % 2 == 0 else p.pp1)
                                      [:, 0:512],
                                      AF.Exp).then_inc(L["sAE"], 1)
                    sc.wait_ge(L["sPE"], hb + 5 + lenk + 1)
                    if h >= 2:
                        sc.wait_ge(L["sVE"], ve_cnt(h - 1))
                    sc.activation(p.gs7[rb:rb + 64, 0:512],
                                  p.pp2[rb:rb + 64, 0:512], AF.Ln)
                    sc.activation(p.gs7[rb:rb + 64, 0:512],
                                  p.gs7[rb:rb + 64, 0:512],
                                  AF.Exp, scale=-1.0).then_inc(L["sAE"], 1)
            p.cnt["sAE"] += H2 * ae_per

        def mm_wo2(t, fo):
            ps = p.pw0 if fo % 2 == 0 else p.pp0
            for fi in range(4):
                t.matmul(ps[:, 0:512], wotile(fi, fo),
                         o_n[:, fi * 512:(fi + 1) * 512],
                         start=(fi == 0), stop=(fi == 3))

        def ev_wo2(e, fo):
            ps = p.pw0 if fo % 2 == 0 else p.pp0
            return e.copy(wo_buf[:, fo * 512:(fo + 1) * 512], ps[:, 0:512])

        if not DEV_SKIP_HEADS:
            pipe(p, block, FT, mm_wo2, ev_wo2, "scalar", depth=2,
                 pre_wait=("sVE", "sAE"))

    if not DEV_SKIP_HEADS:
        @block.vector
        def _(v):
            p.w(v, "sAE")
            for ft in range(FT):
                v.tensor_add(stile(ft, (NWE - 1) * WIN, WIN),
                             stile(ft, (NWE - 1) * WIN, WIN),
                             wo_buf[:, ft * 512:(ft + 1) * 512])
            v.last.then_inc(L["sVE"], 1)
        p.bump("sVE")
    fence(p, block)
    es.close()


def emit_ffn_mlp(p, block, din, scr, layer, stile):
    nc = p.nc
    L = p.sems
    es = ExitStack()
    h2n = es.enter_context(nc.sbuf_tensor(f"h2n{layer}", [128, FT * TB], BF16))
    mbuf = es.enter_context(nc.sbuf_tensor(f"mbuf{layer}", [128, NS1 * TB], BF16))
    w2res = es.enter_context(nc.sbuf_tensor(f"w2res{layer}", [128, NS1 * FT * 128],
                                            BF16))
    w2buf = es.enter_context(nc.sbuf_tensor(f"w2buf{layer}", [128, FT * TB], BF16))
    p.stat_mu = es.enter_context(nc.sbuf_tensor(f"smu{layer}", [128, 2 * TB], F32))
    p.stat_rs = es.enter_context(nc.sbuf_tensor(f"srs{layer}", [128, 2 * TB], F32))
    slab = es.enter_context(nc.sbuf_tensor(f"slab{layer}", [128, 2 * 2 * FT * 128],
                                           BF16))
    gelb = es.enter_context(nc.sbuf_tensor(f"gelb{layer}", [128, 2 * TB], BF16))

    @block.gpsimd
    def _(g):
        p.w(g, "sPE")
        p.w(g, "sVE")
        p.w(g, "sAE")
        g.dma_start(out=w2res[:].rearrange("p (s f q) -> p s f q", s=NS1, f=FT),
                    in_=din["w2p"][layer].rearrange("(s p) (f q) -> p s f q",
                                                    p=128, q=128)
                    ).then_inc(L["sWT"], 16)
    p.bump("sWT", 16)
    w2_w = p.cnt["sWT"]

    def w2tile(s, fo):
        return w2res[:].rearrange("p (s f q) -> p s f q", s=NS1,
                                  f=FT)[:, s, fo, :]

    def slab_tile(par, side, fi):
        return slab[:].rearrange("p (r d f q) -> p r d f q", r=2, d=2,
                                 f=FT)[:, par, side, fi, :]

    def load_slab(sy, s, par):
        for side, src in ((0, "w1a_bf"), (1, "w1g_bf")):
            sy.dma_start(
                out=slab[:].rearrange("p (r d f q) -> p r d f q", r=2, d=2,
                                      f=FT)[:, par, side, :, :],
                in_=scr[src][layer].rearrange("(f p) (s q) -> p f s q",
                                              p=128, q=128)[:, :, s, :]
                ).then_inc(L["sSP"], 16)
            p.bump("sSP", 16)

    mu2 = [p.stat_mu[:, 0:TB], p.stat_mu[:, TB:2 * TB]]
    rs2 = [p.stat_rs[:, 0:TB], p.stat_rs[:, TB:2 * TB]]

    def rhs_sh_tb(tb):
        def f(ft, t0=tb * TB):
            off = 0 if ft < 4 else -1
            return stile(ft, t0 + off, TB)
        return f

    for tb in range(NTB):
        t0 = tb * TB
        first_tb = (tb == 0)
        sslot = tb % 2

        if tb == 0:
            ln_stats_sb(p, block, rhs_sh_tb(0), TB, mu2[0], rs2[0],
                        p.pw1[:, 0:512], p.pw1[:, 512:1024], sq_off=0)

        @block.vector
        def _(v, t0=t0, sslot=sslot, h2c=h2n):
            p.w(v, "sAE")
            for ft in range(FT):
                off = 0 if ft < 4 else -1
                v.tensor_sub(p.gs3[:, 0:TB], stile(ft, t0 + off, TB),
                             mu2[sslot])
                v.scalar_tensor_tensor(h2c[:, ft * TB:(ft + 1) * TB],
                                       p.gs3[:, 0:TB],
                                       p.lng_pp[:, layer * FT + ft:
                                                layer * FT + ft + 1],
                                       rs2[sslot], ALU.mult, ALU.mult)
            v.last.then_inc(L["sVE"], 1)
        p.bump("sVE")

        if tb > 0:
            @block.vector
            def _(v, t0=t0):
                for ft in range(FT):
                    v.tensor_add(stile(ft, t0 - TB, TB),
                                 stile(ft, t0 - TB, TB),
                                 w2buf[:, ft * TB:(ft + 1) * TB])
                v.last.then_inc(L["sVE"], 1)
            p.bump("sVE")

        sp_targets = []

        @block.sync
        def _(sy, tb=tb):
            sy.wait_ge(L["sSP"], p.w1cast_done)
            p.w(sy, "sPE")
            p.w(sy, "sVE")
            p.w(sy, "sAE")
            load_slab(sy, 0, 0)
        sp_targets.append(p.cnt["sSP"])

        pe0 = p.cnt["sPE"]
        ae0 = p.cnt["sAE"]
        ve0 = p.cnt["sVE"]
        if first_tb:
            h2c = h2n
            for s in range(NS1):
                par = s % 2
                aps = p.pp0 if par == 0 else p.pp2
                gps = p.pp1 if par == 0 else p.pw0
                gel = gelb[:, par * TB:(par + 1) * TB]

                if s + 1 < NS1:
                    @block.sync
                    def _(sy, s=s, par=par):
                        p.w(sy, "sPE")
                        load_slab(sy, s + 1, par ^ 1)
                    sp_targets.append(p.cnt["sSP"])

                @block.tensor
                def _(t, s=s, par=par, aps=aps, gps=gps, h2c=h2c):
                    t.wait_ge(L["sSP"], sp_targets[s])
                    if s == 0:
                        t.wait_ge(L["sWT"], w2_w)
                    p.w(t, "sVE")
                    p.w(t, "sAE")
                    for fi in range(FT):
                        t.matmul(aps[:, 0:TB], slab_tile(par, 0, fi),
                                 h2c[:, fi * TB:(fi + 1) * TB],
                                 start=(fi == 0), stop=(fi == FT - 1))
                    for fi in range(FT):
                        t.matmul(gps[:, 0:TB], slab_tile(par, 1, fi),
                                 h2c[:, fi * TB:(fi + 1) * TB],
                                 start=(fi == 0), stop=(fi == FT - 1))
                    for side, ps in ((0, p.pp1 if par else p.pp2),
                                     (1, p.pw1)):
                        for fi in range(FT):
                            t.matmul(ps[0:1, 0:128],
                                     p.lnb_bf[:, layer * FT + fi:
                                              layer * FT + fi + 1],
                                     slab_tile(par, side, fi),
                                     start=(fi == 0), stop=(fi == FT - 1))
                    t.last.then_inc(L["sPE"], 1)
                p.bump("sPE")

                @block.vector
                def _(v, par=par):
                    p.w(v, "sPE")
                    v.tensor_copy(p.rowtmp[0:1, :],
                                  (p.pp1 if par else p.pp2)[0:1, 0:128])
                    v.last.then_inc(L["sVE"], 1)
                p.bump("sVE")

                @block.tensor
                def _(t):
                    p.w(t, "sVE")
                    t.matmul(p.pw1[:, 512:513], p.rowtmp[0:1, :],
                             p.c_ones1[0:1, 0:1], start=True, stop=True)
                    t.last.then_inc(L["sPE"], 1)
                p.bump("sPE")

                @block.vector
                def _(v, s=s):
                    p.w(v, "sPE")
                    v.tensor_add(p.b1ae[:, s:s + 1], p.pw1[:, 512:513],
                                 p.b1ac[:, layer * NS1 + s:layer * NS1 + s + 1])
                    v.tensor_copy(p.rowtmp[0:1, :], p.pw1[0:1, 0:128])
                    v.last.then_inc(L["sVE"], 1)
                p.bump("sVE")

                @block.tensor
                def _(t):
                    p.w(t, "sVE")
                    t.matmul(p.pw1[:, 512:513], p.rowtmp[0:1, :],
                             p.c_ones1[0:1, 0:1], start=True, stop=True)
                    t.last.then_inc(L["sPE"], 1)
                p.bump("sPE")

                @block.vector
                def _(v, s=s):
                    p.w(v, "sPE")
                    v.tensor_add(p.b1ge[:, s:s + 1], p.pw1[:, 512:513],
                                 p.b1gc[:, layer * NS1 + s:layer * NS1 + s + 1])
                    v.last.then_inc(L["sVE"], 1)
                p.bump("sVE")

                @block.scalar
                def _(sc, s=s, gps=gps, gel=gel):
                    p.w(sc, "sPE")
                    p.w(sc, "sVE")
                    sc.activation(gel, gps[:, 0:TB], AF.Gelu,
                                  bias=p.b1ge[:, s:s + 1])
                    sc.last.then_inc(L["sAE"], 1)
                p.bump("sAE")

                @block.vector
                def _(v, s=s, aps=aps, gel=gel):
                    p.w(v, "sAE")
                    v.scalar_tensor_tensor(mbuf[:, s * TB:(s + 1) * TB],
                                           aps[:, 0:TB], p.b1ae[:, s:s + 1],
                                           gel, ALU.add, ALU.mult)
                    v.last.then_inc(L["sVE"], 1)
                p.bump("sVE")
        else:
            @block.sync
            def _(sy):
                sy.wait_ge(L["sSP"], p.w1cast_done)
                for s in range(NS1 - 1):
                    sy.wait_ge(L["sPE"], pe0 + s)
                    load_slab(sy, s + 1, (s + 1) % 2)
            for s in range(NS1 - 1):
                sp_targets.append(sp_targets[0] + 32 * (s + 1))

            h2c = h2n

            @block.tensor
            def _(t, h2c=h2c):
                for s in range(NS1):
                    par = s % 2
                    aps = p.pp0 if par == 0 else p.pp2
                    gps = p.pp1 if par == 0 else p.pw0
                    t.wait_ge(L["sSP"], sp_targets[s])
                    if s == 0:
                        t.wait_ge(L["sWT"], w2_w)
                        p.w(t, "sVE", ve0)
                        p.w(t, "sAE", ae0)
                    else:
                        t.wait_ge(L["sAE"], ae0 + max(0, s - 1))
                        t.wait_ge(L["sVE"], ve0 + max(0, s - 1))
                    for fi in range(FT):
                        t.matmul(aps[:, 0:TB], slab_tile(par, 0, fi),
                                 h2c[:, fi * TB:(fi + 1) * TB],
                                 start=(fi == 0), stop=(fi == FT - 1))
                    for fi in range(FT):
                        t.matmul(gps[:, 0:TB], slab_tile(par, 1, fi),
                                 h2c[:, fi * TB:(fi + 1) * TB],
                                 start=(fi == 0), stop=(fi == FT - 1))
                    t.last.then_inc(L["sPE"], 1)
            p.cnt["sPE"] += NS1

            @block.scalar
            def _(sc):
                for s in range(NS1):
                    gps = p.pp1 if s % 2 == 0 else p.pw0
                    gel = gelb[:, (s % 2) * TB:(s % 2 + 1) * TB]
                    sc.wait_ge(L["sPE"], pe0 + s + 1)
                    sc.activation(gel, gps[:, 0:TB], AF.Gelu,
                                  bias=p.b1ge[:, s:s + 1]).then_inc(L["sAE"], 1)
            p.cnt["sAE"] += NS1

            @block.vector
            def _(v):
                for s in range(NS1):
                    aps = p.pp0 if s % 2 == 0 else p.pp2
                    gel = gelb[:, (s % 2) * TB:(s % 2 + 1) * TB]
                    v.wait_ge(L["sAE"], ae0 + s + 1)
                    v.scalar_tensor_tensor(mbuf[:, s * TB:(s + 1) * TB],
                                           aps[:, 0:TB], p.b1ae[:, s:s + 1],
                                           gel, ALU.add,
                                           ALU.mult).then_inc(L["sVE"], 1)
            p.cnt["sVE"] += NS1

        if tb + 1 < NTB:
            ln_stats_sb(p, block, rhs_sh_tb(tb + 1), TB,
                        mu2[(tb + 1) % 2], rs2[(tb + 1) % 2],
                        p.pw1[:, 0:512], p.pw1[:, 512:1024], sq_off=0)

        def mm_w2(t, fo):
            ps = p.pp0 if fo % 2 == 0 else p.pp1
            for s in range(NS1):
                t.matmul(ps[:, 0:TB], w2tile(s, fo),
                         mbuf[:, s * TB:(s + 1) * TB],
                         start=(s == 0), stop=(s == NS1 - 1))

        def ev_w2(e, fo):
            ps = p.pp0 if fo % 2 == 0 else p.pp1
            return e.tensor_scalar_add(w2buf[:, fo * TB:(fo + 1) * TB],
                                       ps[:, 0:TB],
                                       p.b2_pp[:, layer * FT + fo:
                                               layer * FT + fo + 1])

        pipe(p, block, FT, mm_w2, ev_w2, "vector", depth=2, pre_wait=("sVE",))

        fence(p, block)

    @block.vector
    def _(v):
        for ft in range(FT):
            v.tensor_add(stile(ft, (NTB - 1) * TB, TB),
                         stile(ft, (NTB - 1) * TB, TB),
                         w2buf[:, ft * TB:(ft + 1) * TB])
        v.last.then_inc(L["sVE"], 1)
    p.bump("sVE")
    fence(p, block)
    es.close()


def emit_final(p, block, scr, out_d, stile):
    nc = p.nc
    L = p.sems
    es = ExitStack()
    x1b = es.enter_context(nc.sbuf_tensor("x1b", [128, FT * 512], BF16))
    outf = es.enter_context(nc.sbuf_tensor("outf", [128, FT * 512], F32))

    for s in range(8):
        t0 = s * 512

        @block.sync
        def _(sy, s=s):
            p.w(sy, "sVE")
            p.w(sy, "sPE")
            p.w(sy, "sAE")
            sy.dma_start(out=x1b[:].rearrange("p (ft t) -> p ft t", ft=FT),
                         in_=scr["x1_s"][:, :, s * 512:(s + 1) * 512]
                         ).then_inc(L["sLD"], 16)
        p.bump("sLD", 16)

        ln_stats(p, block, lambda ft, t0=t0: stile(ft, t0, 512), 512,
                 p.gs2[:, 0:512])

        @block.vector
        def _(v, t0=t0):
            p.w(v, "sAE")
            p.w(v, "sLD")
            v.tensor_mul(p.gs1[:, 0:512], p.pp0[:, 0:512], p.gs2[:, 0:512])
            for ft in range(FT):
                v.tensor_mul(p.gs3[:, 0:512], stile(ft, t0, 512),
                             p.gs2[:, 0:512])
                v.tensor_sub(p.gs3[:, 0:512], p.gs3[:, 0:512], p.gs1[:, 0:512])
                v.tensor_add(outf[:, ft * 512:(ft + 1) * 512], p.gs3[:, 0:512],
                             x1b[:, ft * 512:(ft + 1) * 512])
            v.last.then_inc(L["sVE"], 1)
        p.bump("sVE")

        @block.sync
        def _(sy, s=s):
            p.w(sy, "sVE")
            sy.dma_start(out=out_d.rearrange("(ft p) t -> p ft t", p=128)
                         [:, :, s * 512:(s + 1) * 512],
                         in_=outf[:].rearrange("p (ft t) -> p ft t", ft=FT)
                         ).then_inc(L["sST"], 16)
        p.bump("sST", 16)
        fence(p, block)

    @block.sync
    def _(sy):
        p.w(sy, "sST")
    es.close()


def kernel(**inputs):
    consts = _host_consts()
    w = _prep_weights(inputs)
    x = inputs["L0_attn_x"]
    p = build_program()
    in_maps = []
    for core in range(8):
        m = {"x_in": np.ascontiguousarray(x[core].T)}
        m.update(w)
        m.update(consts)
        dm = np.zeros((128, 64), np.float32)
        dm[:, :core + 1] = 1.0
        m["dmask"] = dm
        in_maps.append(m)
    res = run_bass_kernel_spmd(p.nc, in_maps, core_ids=list(range(8)))
    out = np.stack([np.ascontiguousarray(r["out_d"].T) for r in res.results])
    return out


# version 15
# speedup vs baseline: 1.3286x; 1.0005x over previous
import numpy as np
import ml_dtypes
from contextlib import ExitStack

import concourse.bass as bass
import concourse.mybir as mybir
from concourse.bass_utils import run_bass_kernel_spmd

F32 = mybir.dt.float32
BF16 = mybir.dt.bfloat16
AF = mybir.ActivationFunctionType
ALU = mybir.AluOpType
AX = mybir.AxisListType

T, D, FT = 4096, 1024, 8
COLS = 1 + T
C = 64
SLA = 512
NSA = T // SLA
SLB = 256
NSB = T // SLB
CPS = SLB // C
NPAIR = CPS // 2
CHW = NPAIR * 64
H2, DH = 8, 64
WIN = 512
NW = T // WIN
HID2 = 2816
NS1 = HID2 // 128
TB = 512
NTB = T // TB
LEPS = 1e-5
NEG = -1e30

DEV_STOP = None
DEV_SKIP_HEADS = False
DEV_SKIP_V = False
DEV_DNB = False
DEV_SKIP_K = False
DEV_NW = None


def _host_consts():
    c = {}
    c["ident"] = np.eye(128, dtype=np.float32)
    c["ones1"] = np.ones((1, 128), dtype=np.float32)
    c["onesh"] = np.ones((128, 128), dtype=np.float32)
    c["sumw1024"] = np.full((128, 128), 1.0 / 1024.0, dtype=np.float32)
    c["sumw256"] = np.full((128, 128), 1.0 / 256.0, dtype=np.float32)
    blk = np.zeros((128, 128), dtype=np.float32)
    blk[:64, :64] = 1.0
    blk[64:, 64:] = 1.0
    c["blk64"] = blk
    tri = np.tril(np.ones((C, C), dtype=np.float32), -1)
    c["maskL"] = np.concatenate([tri, tri], 0)
    c["maskU"] = np.concatenate([tri.T, tri.T], 0)
    ey = np.eye(C, dtype=np.float32)
    c["eye1"] = np.concatenate([ey, ey], 0)
    c["eye2"] = 2.0 * np.concatenate([ey, ey], 0)
    prot = np.zeros((128, 128), dtype=np.float32)
    for b in range(2):
        b0 = b * 64
        for m in range(32):
            prot[b0 + m + 32, b0 + m] = -1.0
        for m in range(32, 64):
            prot[b0 + m - 32, b0 + m] = 1.0
    c["prot"] = prot
    c["epsv"] = np.full((128, 8), LEPS, dtype=np.float32)
    mqk = np.full((128, 4, 1024), NEG, dtype=np.float32)
    p_ = np.arange(128)[:, None]
    col = np.arange(1024)[None, :]
    for qb in range(4):
        qi = qb * 128 + p_
        mqk[:, qb, :] = np.where((col >= qi) & (col <= qi + WIN), 0.0, NEG)
    c["mqk"] = np.ascontiguousarray(mqk.reshape(128, 4096)).astype(
        ml_dtypes.bfloat16)
    mt = np.full((128, 8, 512), NEG, dtype=np.float32)
    q_ = np.arange(512)[None, :]
    for kb in range(8):
        cv = kb * 128 + p_
        mt[:, kb, :] = np.where((q_ <= cv) & (cv <= q_ + WIN), 0.0, NEG)
    c["mt"] = np.ascontiguousarray(mt.reshape(128, 4096)).astype(
        ml_dtypes.bfloat16)
    sel4 = np.zeros((4, 512), np.float32)
    for h in range(4):
        sel4[h, h * 128:(h + 1) * 128] = 1.0
    c["sel4"] = sel4
    half = np.arange(0, DH, 2, dtype=np.float64)
    inv_freq = 1.0 / 10000.0 ** (half / DH)
    tt = np.arange(T, dtype=np.float64)
    fr = tt[:, None] * inv_freq[None, :]
    fr = np.concatenate([fr, fr], -1)
    base = (half + 0.4 * DH) / (1.4 * DH)
    power = (tt - T // 2) / float(WIN // 2)
    sc = base[None, :] ** power[:, None]
    sc = np.concatenate([sc, sc], -1)
    cs_q = (np.cos(fr) * sc).T.astype(np.float32)
    ss_q = (np.sin(fr) * sc).T.astype(np.float32)
    cs_k = (np.cos(fr) / sc).T.astype(np.float32)
    ss_k = (np.sin(fr) / sc).T.astype(np.float32)
    c["csq"] = np.ascontiguousarray(8.0 * np.concatenate([cs_q, cs_q], 0))
    c["ssq"] = np.ascontiguousarray(8.0 * np.concatenate([ss_q, ss_q], 0))
    c["csk"] = np.ascontiguousarray(np.concatenate([cs_k, cs_k], 0))
    c["ssk"] = np.ascontiguousarray(np.concatenate([ss_k, ss_k], 0))
    return c


def _prep_weights(inp):
    w = {}

    def pm(a, nb):
        A = a.shape[0] // 128
        return np.ascontiguousarray(
            a.reshape(A, 128, nb, 128).transpose(1, 0, 2, 3).reshape(
                128, A * nb * 128))

    w["wq"] = pm(inp["L0_attn_Wq"], 8)
    w["wk"] = pm(inp["L0_attn_Wk"], 8)
    w["wv"] = pm(inp["L0_attn_Wv"], 8)
    w["wod"] = pm(inp["L0_attn_Wo"], 8)
    wbe = inp["L0_attn_Wb"].reshape(FT, 128, 4).transpose(1, 0, 2)
    w["wb"] = np.ascontiguousarray(np.repeat(wbe, 32, axis=2))
    for nm, key in (("cvq", "L0_attn_conv_q"), ("cvk", "L0_attn_conv_k"),
                    ("cvv", "L0_attn_conv_v")):
        cw = inp[key][:, 0, :]
        w[nm] = np.ascontiguousarray(cw.reshape(FT, 128, 4).transpose(1, 0, 2))
    w["g"] = np.ascontiguousarray(inp["L0_attn_g"].reshape(2, 128).T)
    w["fwqkv"] = np.stack([pm(inp["L0_ffn_wqkv"][l], 12) for l in range(2)])
    w["fwo"] = np.stack([pm(inp["L0_ffn_wo"][l], 8) for l in range(2)])
    qs = inp["L0_ffn_q_scale"]
    ks = inp["L0_ffn_k_scale"]
    rot = np.concatenate([np.arange(32, 64), np.arange(0, 32)])
    w["fqs"] = np.ascontiguousarray(np.concatenate([qs, qs], 1).T)
    w["fqsr"] = np.ascontiguousarray(np.concatenate([qs[:, rot], qs[:, rot]], 1).T)
    w["fks"] = np.ascontiguousarray(np.concatenate([ks, ks], 1).T)
    w["fksr"] = np.ascontiguousarray(np.concatenate([ks[:, rot], ks[:, rot]], 1).T)
    w["lng"] = np.ascontiguousarray(
        inp["L0_ffn_ln_g"].reshape(2, FT, 128).transpose(2, 0, 1).reshape(128, 16))
    w["lnb"] = np.ascontiguousarray(
        inp["L0_ffn_ln_b"].reshape(2, FT, 128).transpose(2, 0, 1).reshape(128, 16))
    w["b2"] = np.ascontiguousarray(
        inp["L0_ffn_b2"].reshape(2, FT, 128).transpose(2, 0, 1).reshape(128, 16))
    w1 = inp["L0_ffn_w1"]
    b1 = inp["L0_ffn_b1"]
    w1a = np.zeros((2, 1024, HID2), np.float32)
    w1g = np.zeros((2, 1024, HID2), np.float32)
    w1a[:, :, :2730] = w1[:, :, :2730]
    w1g[:, :, :2730] = w1[:, :, 2730:]
    w["w1a"] = w1a
    w["w1g"] = w1g
    b1a = np.zeros((2, HID2), np.float32)
    b1g = np.zeros((2, HID2), np.float32)
    b1a[:, :2730] = b1[:, :2730]
    b1g[:, :2730] = b1[:, 2730:]
    w["b1a"] = np.ascontiguousarray(
        b1a.reshape(2, NS1, 128).transpose(2, 0, 1).reshape(128, 2 * NS1))
    w["b1g"] = np.ascontiguousarray(
        b1g.reshape(2, NS1, 128).transpose(2, 0, 1).reshape(128, 2 * NS1))
    w2 = inp["L0_ffn_w2"]
    w2p = np.zeros((2, HID2, 1024), np.float32)
    w2p[:, :2730, :] = w2
    w["w2p"] = w2p
    return w


class P:
    def __init__(self):
        self.nc = bass.Bass("TRN2", target_bir_lowering=False,
                            detect_race_conditions=False)
        self.es = ExitStack()
        self.sems = {}
        self.cnt = {}

    def sem(self, name):
        h = self.es.enter_context(self.nc.semaphore(name))
        self.sems[name] = h
        self.cnt[name] = 0
        return h

    def bump(self, name, n=1):
        self.cnt[name] += n
        return self.cnt[name]

    def w(self, eng, name, val=None):
        eng.wait_ge(self.sems[name], self.cnt[name] if val is None else val)

    def sb(self, name, shape, dtype):
        t = self.es.enter_context(self.nc.sbuf_tensor(name, shape, dtype))
        setattr(self, name, t)
        return t

    def ps(self, name, shape, dtype):
        t = self.es.enter_context(self.nc.psum_tensor(name, shape, dtype))
        setattr(self, name, t)
        return t


_NOTRACK = {"wait_ge", "nop", "engine_nop", "sem_inc", "wait_op", "drain"}


class TRK:

    def __init__(self, eng):
        self._e = eng
        self.last = None

    def __getattr__(self, nm):
        f = getattr(self._e, nm)
        if not callable(f):
            return f

        def wrap(*a, **k):
            r = f(*a, **k)
            if nm not in _NOTRACK and hasattr(r, "then_inc"):
                self.last = r
            return r
        return wrap


class BlockProxy:
    def __init__(self, block):
        self._b = block

    def _mk(self, sec):
        def deco(fn):
            return sec(lambda eng: fn(TRK(eng)))
        return deco

    @property
    def tensor(self):
        return self._mk(self._b.tensor)

    @property
    def vector(self):
        return self._mk(self._b.vector)

    @property
    def scalar(self):
        return self._mk(self._b.scalar)

    @property
    def gpsimd(self):
        return self._mk(self._b.gpsimd)

    @property
    def sync(self):
        return self._mk(self._b.sync)


def fence(p, block):
    c = {nm: p.cnt[nm] for nm in ("sPE", "sVE", "sAE")}

    @block.tensor
    def _(t):
        t.wait_ge(p.sems["sVE"], c["sVE"])
        t.wait_ge(p.sems["sAE"], c["sAE"])

    @block.vector
    def _(v):
        v.wait_ge(p.sems["sPE"], c["sPE"])
        v.wait_ge(p.sems["sAE"], c["sAE"])

    @block.scalar
    def _(s):
        s.wait_ge(p.sems["sPE"], c["sPE"])
        s.wait_ge(p.sems["sVE"], c["sVE"])


def pipe(p, block, n, mm_emit, evac_emit, evac_engine="scalar", depth=2,
         pre_wait=(), pre_counts=()):
    pe0 = p.cnt["sPE"]
    ce_name = {"scalar": "sAE", "vector": "sVE"}[evac_engine]
    ce0 = p.cnt[ce_name]
    pw = [(nm, p.cnt[nm]) for nm in pre_wait] + list(pre_counts)

    @block.tensor
    def _(t):
        for nm, val in pw:
            t.wait_ge(p.sems[nm], val)
        for i in range(n):
            if i >= depth:
                t.wait_ge(p.sems[ce_name], ce0 + i - depth + 1)
            mm_emit(t, i)
            t.last.then_inc(p.sems["sPE"], 1)
    p.cnt["sPE"] += n

    sec = block.scalar if evac_engine == "scalar" else block.vector

    def body(e):
        for i in range(n):
            e.wait_ge(p.sems["sPE"], pe0 + i + 1)
            last = evac_emit(e, i)
            last.then_inc(p.sems[ce_name], 1)
    sec(body)
    p.cnt[ce_name] += n


def build_program():
    p = P()
    nc = p.nc
    din = {}

    def DI(name, shape):
        din[name] = nc.dram_tensor(name, list(shape), F32,
                                   kind="ExternalInput").ap()

    DI("x_in", (D, T))
    DI("wq", (128, 8192)); DI("wk", (128, 8192)); DI("wv", (128, 8192))
    DI("wod", (128, 8192))
    DI("wb", (128, FT, 128))
    DI("cvq", (128, FT, 4)); DI("cvk", (128, FT, 4)); DI("cvv", (128, FT, 4))
    DI("g", (128, 2))
    DI("fwqkv", (2, 128, FT * 12 * 128)); DI("fwo", (2, 128, 4 * FT * 128))
    DI("fqs", (128, 2)); DI("fqsr", (128, 2))
    DI("fks", (128, 2)); DI("fksr", (128, 2))
    DI("lng", (128, 2 * FT)); DI("lnb", (128, 2 * FT)); DI("b2", (128, 2 * FT))
    DI("w1a", (2, D, HID2)); DI("w1g", (2, D, HID2))
    DI("b1a", (128, 2 * NS1)); DI("b1g", (128, 2 * NS1))
    DI("w2p", (2, HID2, D))
    for nm, shp in (("ident", (128, 128)), ("ones1", (1, 128)),
                    ("onesh", (128, 128)), ("sumw1024", (128, 128)),
                    ("sumw256", (128, 128)), ("blk64", (128, 128)),
                    ("maskL", (128, 64)), ("maskU", (128, 64)),
                    ("eye1", (128, 64)), ("eye2", (128, 64)),
                    ("prot", (128, 128)), ("epsv", (128, 8)),
                    ("csq", (128, T)), ("ssq", (128, T)),
                    ("csk", (128, T)), ("ssk", (128, T)), ("dmask", (128, 64))):
        DI(nm, shp)
    din["sel4"] = nc.dram_tensor("sel4", [4, 512], F32,
                                 kind="ExternalInput").ap()
    for nm in ("mqk", "mt"):
        din[nm] = nc.dram_tensor(nm, [128, 4096], BF16,
                                 kind="ExternalInput").ap()

    scr = {}
    for nm, shape, dt in (("qn_s", [128, FT, T], BF16),
                          ("kn_s", [128, FT, T], BF16),
                          ("v_s", [128, FT, T], BF16),
                          ("x1_s", [128, FT, T], BF16),
                          ("w1a_bf", [2, D, HID2], BF16),
                          ("w1g_bf", [2, D, HID2], BF16),
                          ("betar_s", [4, T], F32),
                          ("dbg", [128, 16384], BF16)):
        scr[nm] = nc.dram_tensor(nm, shape, dt, kind="Internal").ap()
    out_d = nc.dram_tensor("out_d", [D, T], F32, kind="ExternalOutput").ap()

    p.sb("stream", [128, FT * COLS], BF16)
    for nm, shp, dt in (("c_id_bf", [128, 128], BF16),
                        ("c_id_f", [128, 128], F32),
                        ("c_ones1", [1, 128], BF16),
                        ("c_ones", [128, 128], BF16),
                        ("c_sw1024", [128, 128], BF16),
                        ("c_sw256", [128, 128], BF16),
                        ("c_blk64", [128, 128], BF16),
                        ("c_maskL", [128, 64], F32),
                        ("c_maskU", [128, 64], F32),
                        ("c_eye1", [128, 64], F32),
                        ("c_eye2", [128, 64], F32),
                        ("c_prot", [128, 128], BF16),
                        ("c_epsv", [128, 8], F32),
                        ("c_sel4", [4, 512], F32),
                        ("c_dmask", [128, 64], F32),
                        ("sq", [128, 4096], BF16),
                        ("gs1", [128, 512], F32), ("gs2", [128, 512], F32),
                        ("gs3", [128, 512], F32), ("gs4", [128, 512], F32),

                        ("sdt", [128, FT * 5], F32),
                        ("halo", [128, 3 * FT * 3], F32),
                        ("g_pp", [128, 2], F32),
                        ("cv_pp", [128, 3 * FT * 4], F32),
                        ("wb_bf", [128, FT * 128], BF16),
                        ("lng_pp", [128, 2 * FT], F32),
                        ("lnb_pp", [128, 2 * FT], F32),
                        ("lnb_bf", [128, 2 * FT], BF16),
                        ("b2_pp", [128, 2 * FT], F32),
                        ("qs_pp", [128, 2], F32), ("qsr_pp", [128, 2], F32),
                        ("ks_pp", [128, 2], F32), ("ksr_pp", [128, 2], F32),
                        ("b1ae", [128, NS1], F32), ("b1ge", [128, NS1], F32),
                        ("b1ac", [128, 2 * NS1], F32),
                        ("b1gc", [128, 2 * NS1], F32),
                        ("rowtmp", [1, 128], BF16),
                        ("dumm", [1, 8], F32)):
        p.sb(nm, shp, dt)
    p.ps("pw0", [128, 1024], F32)
    p.ps("pw1", [128, 1024], F32)
    p.ps("pp0", [128, 512], F32)
    p.ps("pp1", [128, 512], F32)
    p.ps("pp2", [128, 512], F32)
    p.ps("pp3", [128, 512], F32)
    for s in ("sLD", "sLG", "sWT", "sSP", "sST", "sPE", "sVE", "sAE", "sPO", "sDB"):
        p.sem(s)
    nc.const_aps.register(F32, LEPS, p.c_epsv[:, 0:1])

    def stile(ft, t0, n):
        base = ft * COLS + 1 + t0
        return p.stream[:, base:base + n]

    with nc.Block() as block:
        emit_all(p, BlockProxy(block), din, scr, out_d, stile)
    p.es.close()
    return p


def ln_stats_sb(p, block, rhs_tiles, n, mu_sb, rstd_sb, psA, psB,
               sq_off=0, eps=LEPS):
    L = p.sems

    @block.gpsimd
    def _(g):
        p.w(g, "sPE")
        p.w(g, "sVE")
        for ft in range(FT):
            g.tensor_mul(p.sq[:, sq_off + ft * n:sq_off + (ft + 1) * n],
                         rhs_tiles(ft), rhs_tiles(ft))
        g.last.then_inc(L["sPO"], 1)
    p.bump("sPO")

    @block.tensor
    def _(t):
        p.w(t, "sPO")
        p.w(t, "sVE")
        p.w(t, "sAE")
        for ft in range(FT):
            t.matmul(psA[:, 0:n], p.c_sw1024[:], rhs_tiles(ft),
                     start=(ft == 0), stop=(ft == FT - 1))
        for ft in range(FT):
            t.matmul(psB[:, 0:n], p.c_sw1024[:],
                     p.sq[:, sq_off + ft * n:sq_off + (ft + 1) * n],
                     start=(ft == 0), stop=(ft == FT - 1))
        t.last.then_inc(L["sPE"], 1)
    p.bump("sPE")

    @block.scalar
    def _(s):
        p.w(s, "sPE")
        s.copy(mu_sb, psA[:, 0:n])
        s.last.then_inc(L["sAE"], 1)
    p.bump("sAE")

    @block.gpsimd
    def _(g):
        p.w(g, "sAE")
        g.tensor_mul(rstd_sb, mu_sb, mu_sb)
        g.last.then_inc(L["sPO"], 1)
    p.bump("sPO")

    @block.vector
    def _(v):
        p.w(v, "sPO")
        v.tensor_sub(rstd_sb, psB[:, 0:n], rstd_sb)
        v.last.then_inc(L["sVE"], 1)
    p.bump("sVE")

    @block.scalar
    def _(s):
        p.w(s, "sVE")
        s.activation(rstd_sb, rstd_sb, AF.Ln, bias=eps)
        s.activation(rstd_sb, rstd_sb, AF.Exp, scale=-0.5)
        s.last.then_inc(L["sAE"], 1)
    p.bump("sAE")


def ln_stats(p, block, rhs_tiles, n, out_rstd, eps=LEPS):
    L = p.sems

    @block.gpsimd
    def _(g):
        p.w(g, "sPE")
        p.w(g, "sVE")
        for ft in range(FT):
            g.tensor_mul(p.sq[:, ft * n:(ft + 1) * n], rhs_tiles(ft),
                         rhs_tiles(ft))
        g.last.then_inc(L["sPO"], 1)
    p.bump("sPO")

    @block.tensor
    def _(t):
        p.w(t, "sPO")
        p.w(t, "sVE")
        p.w(t, "sAE")
        for ft in range(FT):
            t.matmul(p.pp0[:, 0:n], p.c_sw1024[:], rhs_tiles(ft),
                     start=(ft == 0), stop=(ft == FT - 1))
        for ft in range(FT):
            t.matmul(p.pp1[:, 0:n], p.c_sw1024[:], p.sq[:, ft * n:(ft + 1) * n],
                     start=(ft == 0), stop=(ft == FT - 1))
        t.last.then_inc(L["sPE"], 1)
    p.bump("sPE")

    @block.scalar
    def _(s):
        p.w(s, "sPE")
        s.copy(p.gs2[:, 0:n], p.pp0[:, 0:n])
        s.last.then_inc(L["sAE"], 1)
    p.bump("sAE")

    @block.gpsimd
    def _(g):
        p.w(g, "sAE")
        g.tensor_mul(p.gs1[:, 0:n], p.gs2[:, 0:n], p.gs2[:, 0:n])
        g.last.then_inc(L["sPO"], 1)
    p.bump("sPO")

    @block.vector
    def _(v):
        p.w(v, "sPO")
        v.tensor_sub(p.gs1[:, 0:n], p.pp1[:, 0:n], p.gs1[:, 0:n])
        v.last.then_inc(L["sVE"], 1)
    p.bump("sVE")

    @block.scalar
    def _(s):
        p.w(s, "sVE")
        s.activation(p.gs1[:, 0:n], p.gs1[:, 0:n], AF.Ln, bias=eps)
        s.activation(out_rstd, p.gs1[:, 0:n], AF.Exp, scale=-0.5)
        s.last.then_inc(L["sAE"], 1)
    p.bump("sAE")


def emit_all(p, block, din, scr, out_d, stile):
    L = p.sems

    @block.gpsimd
    def _(g):
        for dst, nm in ((p.c_id_bf, "ident"), (p.c_ones1, "ones1"),
                        (p.c_ones, "onesh"), (p.c_sw1024, "sumw1024"),
                        (p.c_sw256, "sumw256"), (p.c_blk64, "blk64"),
                        (p.c_prot, "prot")):
            g.dma_start(out=dst[:], in_=din[nm]).then_inc(L["sLG"], 16)
            p.cnt["sLG"] += 16
        g.dma_start(out=p.wb_bf[:].rearrange("p (ft m) -> p ft m", ft=FT),
                    in_=din["wb"]).then_inc(L["sLG"], 16)
        g.dma_start(out=p.lnb_bf[:], in_=din["lnb"]).then_inc(L["sLG"], 16)
        g.dma_start(out=p.stream[:].rearrange("p (ft c) -> p ft c", ft=FT)
                    [:, :, 1:1 + T],
                    in_=din["x_in"].rearrange("(ft p) t -> p ft t", p=128)
                    ).then_inc(L["sLG"], 16)
        p.cnt["sLG"] += 48

    @block.sync
    def _(sy):
        for dst, nm in ((p.c_epsv, "epsv"), (p.c_id_f, "ident"),
                        (p.c_sel4, "sel4"), (p.c_maskL, "maskL"),
                        (p.c_maskU, "maskU"), (p.c_eye1, "eye1"),
                        (p.c_eye2, "eye2"), (p.c_dmask, "dmask"),
                        (p.g_pp, "g")):
            sy.dma_start(out=dst[:], in_=din[nm]).then_inc(L["sLD"], 16)
            p.cnt["sLD"] += 16
        cv = p.cv_pp[:].rearrange("p (pr ft j) -> p pr ft j", pr=3, ft=FT)
        for i, nm in enumerate(("cvq", "cvk", "cvv")):
            sy.dma_start(out=cv[:, i, :, :], in_=din[nm]).then_inc(L["sLD"], 16)
            p.cnt["sLD"] += 16
        for dst, nm in ((p.lng_pp, "lng"), (p.lnb_pp, "lnb"), (p.b2_pp, "b2"),
                        (p.qs_pp, "fqs"), (p.qsr_pp, "fqsr"),
                        (p.ks_pp, "fks"), (p.ksr_pp, "fksr"),
                        (p.b1ac, "b1a"), (p.b1gc, "b1g")):
            sy.dma_start(out=dst[:], in_=din[nm]).then_inc(L["sLD"], 16)
            p.cnt["sLD"] += 16

    ld0 = p.cnt["sLD"]
    lg0 = p.cnt["sLG"]

    @block.vector
    def _(v):
        v.wait_ge(L["sLG"], lg0)
        v.wait_ge(L["sLD"], ld0)
        v.memset(p.sdt[:], 0.0)
        v.memset(p.dumm[:], 0.0)
        v.memset(p.halo[:], 0.0)
        v.memset(p.stream[:].rearrange("p (ft c) -> p ft c", ft=FT)[:, :, 0:1],
                 0.0)
        v.last.then_inc(L["sVE"], 1)
    p.bump("sVE")
    ve0 = p.cnt["sVE"]

    @block.tensor
    def _(t):
        t.wait_ge(L["sVE"], ve0)

    @block.scalar
    def _(s):
        s.wait_ge(L["sVE"], ve0)

    emit_dn_A(p, block, din, scr, stile)
    if DEV_STOP == "A":
        emit_dev_out(p, block, out_d, stile)
        return
    emit_dn_B(p, block, din, scr, stile)
    if DEV_STOP == "B":
        emit_dev_out(p, block, out_d, stile)
        return
    for layer in range(2):
        emit_ffn_attn(p, block, din, scr, layer, stile)
        if DEV_STOP == f"L{layer}a":
            emit_dev_out(p, block, out_d, stile)
            return
        emit_ffn_mlp(p, block, din, scr, layer, stile)
        if DEV_STOP == f"L{layer}":
            emit_dev_out(p, block, out_d, stile)
            return
    emit_final(p, block, scr, out_d, stile)


def emit_dev_out(p, block, out_d, stile):
    L = p.sems
    fence(p, block)

    @block.gpsimd
    def _(g):
        p.w(g, "sVE")
        g.dma_start(out=out_d.rearrange("(ft p) t -> p ft t", p=128),
                    in_=p.stream[:].rearrange("p (ft c) -> p ft c", ft=FT)
                    [:, :, 1:1 + T]).then_inc(L["sST"], 16)
    p.bump("sST", 16)

    @block.sync
    def _(sy):
        p.w(sy, "sST")
        for nm in ("sSP", "sLD", "sLG", "sWT"):
            p.w(sy, nm)


def emit_dn_A(p, block, din, scr, stile):
    nc = p.nc
    L = p.sems
    es = ExitStack()
    wq_sb = es.enter_context(nc.sbuf_tensor("wq_sb", [128, 8192], BF16))
    wk_sb = es.enter_context(nc.sbuf_tensor("wk_sb", [128, 8192], BF16))
    wv_sb = es.enter_context(nc.sbuf_tensor("wv_sb", [128, 8192], BF16))
    xps = [es.enter_context(nc.sbuf_tensor(f"xpd{i}", [128, FT * 515], F32))
           for i in range(2)]
    lnxs = [es.enter_context(nc.sbuf_tensor("lnxd0", [128, FT * 512], BF16))]
    sts = [es.enter_context(nc.sbuf_tensor(f"std{i}", [128, FT * 512], BF16))
           for i in range(2)]
    stf = es.enter_context(nc.sbuf_tensor("stf", [128, 4 * 512], F32))
    lnmu = es.enter_context(nc.sbuf_tensor("lnmu", [128, 512], F32))
    lnt = es.enter_context(nc.sbuf_tensor("lnt", [128, 512], F32))
    lnrs = es.enter_context(nc.sbuf_tensor("lnrs", [128, 512], F32))
    bet_a = lnt

    @block.gpsimd
    def _(g):
        for wsb, nm in ((wq_sb, "wq"), (wk_sb, "wk"), (wv_sb, "wv")):
            g.dma_start(out=wsb[:], in_=din[nm]).then_inc(L["sWT"], 16)
    p.bump("sWT", 48)
    wt_w = p.cnt["sWT"]

    def wtile(wsb, fi, fo):
        return wsb[:].rearrange("p (a b q) -> p a b q", a=FT, b=FT)[:, fi,
                                                                    fo, :]

    ve_gate = p.cnt["sVE"]

    PLN = {}

    def lnx_t(sn, ft):
        return lnxs[0][:, ft * 512:(ft + 1) * 512]

    def st_t(j, ft):
        return sts[j % 2][:, ft * 512:(ft + 1) * 512]

    def stf_t(ft):
        return stf[:, (ft % 4) * 512:(ft % 4 + 1) * 512]

    def xp_t(j, ft):
        return xps[j % 2][:, ft * 515:(ft + 1) * 515]

    g0 = p.cnt["sPO"]
    pe0 = p.cnt["sPE"]
    a0 = p.cnt["sAE"]
    v0 = p.cnt["sVE"]
    s0 = p.cnt["sSP"]
    pro = dict(lnsq=g0 + 1, mu2=g0 + 2, lnmm=pe0 + 1, bmm=pe0 + 2,
               mu=a0 + 1, lnexp=a0 + 2, sig=a0 + 3, vesub=v0 + 1,
               lnap=v0 + 2, bst=s0 + 64)

    @block.gpsimd
    def _(g):
        g.wait_ge(L["sVE"], ve_gate)
        for ft in range(FT):
            g.tensor_mul(p.sq[:, ft * 512:(ft + 1) * 512],
                         stile(ft, 0, SLA), stile(ft, 0, SLA))
        g.last.then_inc(L["sPO"], 1)
        g.wait_ge(L["sAE"], pro["mu"])
        g.tensor_mul(lnrs[:], lnmu[:], lnmu[:])
        g.last.then_inc(L["sPO"], 1)
    p.cnt["sPO"] += 2

    @block.tensor
    def _(t):
        t.wait_ge(L["sPO"], pro["lnsq"])
        for ft in range(FT):
            t.matmul(p.pp2[:, 0:512], p.c_sw1024[:], stile(ft, 0, SLA),
                     start=(ft == 0), stop=(ft == FT - 1))
        for ft in range(FT):
            t.matmul(p.pp3[:, 0:512], p.c_sw1024[:],
                     p.sq[:, ft * 512:(ft + 1) * 512],
                     start=(ft == 0), stop=(ft == FT - 1))
        t.last.then_inc(L["sPE"], 1)
        t.wait_ge(L["sVE"], pro["lnap"])
        t.wait_ge(L["sWT"], wt_w)
        for fi in range(FT):
            t.matmul(p.pw0[:, 0:512],
                     p.wb_bf[:].rearrange("p (ft m) -> p ft m", ft=FT)[:, fi, :],
                     lnx_t(0, fi), start=(fi == 0), stop=(fi == FT - 1))
        t.last.then_inc(L["sPE"], 1)
    p.cnt["sPE"] += 2

    @block.scalar
    def _(sc):
        sc.wait_ge(L["sPE"], pro["lnmm"])
        sc.copy(lnmu[:], p.pp2[:, 0:512])
        sc.last.then_inc(L["sAE"], 1)
        sc.wait_ge(L["sVE"], pro["vesub"])
        sc.activation(lnrs[:], lnrs[:], AF.Ln, bias=LEPS)
        sc.activation(lnrs[:], lnrs[:], AF.Exp, scale=-0.5)
        sc.last.then_inc(L["sAE"], 1)
        sc.wait_ge(L["sPE"], pro["bmm"])
        for h in range(4):
            sc.activation(bet_a[h * 32:h * 32 + 1, :],
                          p.pw0[h * 32:h * 32 + 1, 0:512], AF.Sigmoid)
        sc.last.then_inc(L["sAE"], 1)
    p.cnt["sAE"] += 3

    @block.vector
    def _(v):
        v.wait_ge(L["sPO"], pro["mu2"])
        v.tensor_sub(lnrs[:], p.pp3[:, 0:512], lnrs[:])
        v.last.then_inc(L["sVE"], 1)
        v.wait_ge(L["sAE"], pro["lnexp"])
        for ft in range(FT):
            v.tensor_sub(lnt[:], stile(ft, 0, SLA), lnmu[:])
            v.tensor_mul(lnx_t(0, ft), lnt[:], lnrs[:])
        v.last.then_inc(L["sVE"], 1)
    p.cnt["sVE"] += 2

    @block.sync
    def _(sy):
        sy.wait_ge(L["sAE"], pro["sig"])
        for h in range(4):
            sy.dma_start(out=scr["betar_s"][h:h + 1, 0:SLA],
                         in_=bet_a[h * 32:h * 32 + 1, :]
                         ).then_inc(L["sSP"], 16)
    p.cnt["sSP"] += 64

    prev = dict(lnap=pro["lnap"], zipB_end=0, rsqB=0, rskB=0,
                vesub=pro["vesub"], convk_end=0, convv_end=0, l2kmul=0,
                sig=pro["sig"], sto_k=0, sto_v=0, bst=pro["bst"],
                siv_l=None, l2kB=pro["lnmm"], sie_v=0, a_end=p.cnt["sAE"])
    outs_nm = ("qn_s", "kn_s", "v_s")

    for s in range(NSA):
        t0 = s * SLA
        last = (s + 1 == NSA)
        jq, jk, jv = 3 * s, 3 * s + 1, 3 * s + 2
        g0 = p.cnt["sPO"]
        pe0 = p.cnt["sPE"]
        a0 = p.cnt["sAE"]
        v0 = p.cnt["sVE"]
        s0 = p.cnt["sSP"]
        if not last:
            gP = dict(lnsq=g0 + 1, mu2=g0 + 2, sqq=g0 + 3, sqk=g0 + 4)
            peP = dict(tq=[pe0 + 1 + i for i in range(8)], lnmm=pe0 + 9,
                       tk=[pe0 + 10 + i for i in range(8)],
                       l2qA=pe0 + 18, l2qB=pe0 + 19,
                       tv=[pe0 + 20 + i for i in range(8)],
                       l2kA=pe0 + 28, l2kB=pe0 + 29, bmm=pe0 + 30)
            aP = dict(evq=[a0 + 1 + i for i in range(8)], mu=a0 + 9,
                      evk=[a0 + 10 + 2 * i for i in range(8)],
                      siq=[a0 + 11 + 2 * i for i in range(8)],
                      rsqA=a0 + 26, rsqB=a0 + 27,
                      evv=[a0 + 28 + 2 * i for i in range(8)],
                      sik=[a0 + 29 + 2 * i for i in range(8)],
                      rskA=a0 + 44, rskB=a0 + 45,
                      lnexp=a0 + 46, siv=[a0 + 47 + i for i in range(8)],
                      sig=a0 + 55)
            vP = dict(convq=[v0 + 1 + i for i in range(8)], vesub=v0 + 9,
                      convk=[v0 + 10 + i for i in range(8)], l2qmul=v0 + 18,
                      lnap=v0 + 19, convv=[v0 + 20 + i for i in range(8)],
                      l2kmul=v0 + 28)
            sP = dict(sto_q=s0 + 16, sto_k=s0 + 32, sto_v=s0 + 48,
                      bst=s0 + 112)
        else:
            gP = dict(sqq=g0 + 1, sqk=g0 + 2)
            peP = dict(tq=[pe0 + 1 + i for i in range(8)],
                       tk=[pe0 + 9 + i for i in range(8)],
                       l2qA=pe0 + 17, l2qB=pe0 + 18,
                       tv=[pe0 + 19 + i for i in range(8)],
                       l2kA=pe0 + 27, l2kB=pe0 + 28)
            aP = dict(evq=[a0 + 1 + i for i in range(8)],
                      evk=[a0 + 9 + 2 * i for i in range(8)],
                      siq=[a0 + 10 + 2 * i for i in range(8)],
                      rsqA=a0 + 25, rsqB=a0 + 26,
                      evv=[a0 + 27 + 2 * i for i in range(8)],
                      sik=[a0 + 28 + 2 * i for i in range(8)],
                      rskA=a0 + 43, rskB=a0 + 44,
                      siv=[a0 + 45 + i for i in range(8)])
            vP = dict(convq=[v0 + 1 + i for i in range(8)],
                      convk=[v0 + 9 + i for i in range(8)], l2qmul=v0 + 17,
                      convv=[v0 + 18 + i for i in range(8)], l2kmul=v0 + 26)
            sP = dict(sto_q=s0 + 16, sto_k=s0 + 32, sto_v=s0 + 48)

        @block.gpsimd
        def _(g, s=s, last=last, gP=gP, aP=aP, peP=peP, prev=dict(prev),
              jq=jq, jk=jk):
            if not last:
                if prev["l2kB"]:
                    g.wait_ge(L["sPE"], prev["l2kB"])
                for ft in range(FT):
                    g.tensor_mul(p.sq[:, ft * 512:(ft + 1) * 512],
                                 stile(ft, (s + 1) * SLA, SLA),
                                 stile(ft, (s + 1) * SLA, SLA))
                g.last.then_inc(L["sPO"], 1)
                g.wait_ge(L["sAE"], aP["mu"])
                g.tensor_mul(lnrs[:], lnmu[:], lnmu[:])
                g.last.then_inc(L["sPO"], 1)
            g.wait_ge(L["sAE"], aP["siq"][7])
            g.wait_ge(L["sPE"], peP["lnmm"] if not last else peP["tq"][7])
            for ft in range(FT):
                g.tensor_mul(p.sq[:, ft * 512:(ft + 1) * 512],
                             st_t(jq, ft), st_t(jq, ft))
            g.last.then_inc(L["sPO"], 1)
            g.wait_ge(L["sAE"], aP["sik"][7])
            g.wait_ge(L["sPE"], peP["l2qB"])
            for ft in range(FT):
                g.tensor_mul(p.sq[:, ft * 512:(ft + 1) * 512],
                             st_t(jk, ft), st_t(jk, ft))
            g.last.then_inc(L["sPO"], 1)
        p.cnt["sPO"] += 2 if last else 4

        @block.tensor
        def _(t, s=s, t0=t0, last=last, gP=gP, peP=peP, aP=aP, vP=vP,
              prev=dict(prev), jq=jq, jk=jk, jv=jv):
            t.wait_ge(L["sVE"], prev["lnap"])
            if prev["zipB_end"]:
                t.wait_ge(L["sAE"], prev["zipB_end"])
            if s == 0:
                t.wait_ge(L["sWT"], wt_w)
            slotq = [None] * 8

            def projmm(j, wsb, evl, prior):
                for i in range(8):
                    if i >= 2:
                        t.wait_ge(L["sAE"], evl[i - 2])
                    elif prior[i]:
                        t.wait_ge(L["sAE"], prior[i])
                    ps = p.pw1[:, 0:512] if i % 2 == 0 else p.pw1[:, 512:1024]
                    for fi in range(FT):
                        t.matmul(ps, wtile(wsb, fi, i), lnx_t(s, fi),
                                 start=(fi == 0), stop=(fi == FT - 1))
                    t.last.then_inc(L["sPE"], 1)

            projmm(jq, wq_sb, aP["evq"],
                   [prev["zipB_end"] and (prev["zipB_end"] - 1),
                    prev["zipB_end"]])
            if not last:
                t.wait_ge(L["sPO"], gP["lnsq"])
                t.wait_ge(L["sAE"], prev["rskB"] if prev["rskB"]
                          else aP["mu"] - 1)
                t.wait_ge(L["sVE"], prev["vesub"])
                for ft in range(FT):
                    t.matmul(p.pp2[:, 0:512], p.c_sw1024[:],
                             stile(ft, (s + 1) * SLA, SLA),
                             start=(ft == 0), stop=(ft == FT - 1))
                for ft in range(FT):
                    t.matmul(p.pp3[:, 0:512], p.c_sw1024[:],
                             p.sq[:, ft * 512:(ft + 1) * 512],
                             start=(ft == 0), stop=(ft == FT - 1))
                t.last.then_inc(L["sPE"], 1)
            projmm(jk, wk_sb, aP["evk"], [aP["evq"][6], aP["evq"][7]])
            t.wait_ge(L["sPO"], gP["sqq"])
            if prev["rsqB"]:
                t.wait_ge(L["sAE"], prev["rsqB"])
            for rnd in range(2):
                if rnd == 1:
                    t.wait_ge(L["sAE"], aP["rsqA"])
                for hh in range(2):
                    h = rnd * 2 + hh
                    ps = p.pp0 if hh == 0 else p.pp1
                    for k in range(2):
                        t.matmul(ps[:, 0:512], p.c_ones[:],
                                 p.sq[:, (2 * h + k) * 512:
                                      (2 * h + k + 1) * 512],
                                 start=(k == 0), stop=(k == 1))
                t.last.then_inc(L["sPE"], 1)
            projmm(jv, wv_sb, aP["evv"], [aP["evk"][6], aP["evk"][7]])
            t.wait_ge(L["sPO"], gP["sqk"])
            if not last:
                t.wait_ge(L["sVE"], vP["vesub"])
                t.wait_ge(L["sAE"], aP["mu"])
            for rnd in range(2):
                if rnd == 1:
                    t.wait_ge(L["sAE"], aP["rskA"])
                for hh in range(2):
                    h = rnd * 2 + hh
                    ps = p.pp2 if hh == 0 else p.pp3
                    for k in range(2):
                        t.matmul(ps[:, 0:512], p.c_ones[:],
                                 p.sq[:, (2 * h + k) * 512:
                                      (2 * h + k + 1) * 512],
                                 start=(k == 0), stop=(k == 1))
                t.last.then_inc(L["sPE"], 1)
            if not last:
                t.wait_ge(L["sVE"], vP["lnap"])
                t.wait_ge(L["sAE"], prev["sig"])
                for fi in range(FT):
                    t.matmul(p.pw0[:, 0:512],
                             p.wb_bf[:].rearrange("p (ft m) -> p ft m",
                                                  ft=FT)[:, fi, :],
                             lnx_t(s + 1, fi), start=(fi == 0),
                             stop=(fi == FT - 1))
                t.last.then_inc(L["sPE"], 1)
        p.cnt["sPE"] += 30 if not last else 28

        @block.scalar
        def _(sc, s=s, last=last, peP=peP, aP=aP, vP=vP, prev=dict(prev),
              jq=jq, jk=jk, jv=jv):
            hap = p.halo[:].rearrange("p (pr ft j) -> p pr ft j", pr=3, ft=FT)
            for ft in range(8):
                sc.wait_ge(L["sPE"], peP["tq"][ft])
                if ft == 0 and prev["convk_end"]:
                    sc.wait_ge(L["sVE"], prev["convk_end"])
                ps = p.pw1[:, 0:512] if ft % 2 == 0 else p.pw1[:, 512:1024]
                sc.copy(xps[jq % 2][:, ft * 515 + 3:ft * 515 + 515],
                        ps).then_inc(L["sAE"], 1)
            if not last:
                sc.wait_ge(L["sPE"], peP["lnmm"])
                sc.copy(lnmu[:], p.pp2[:, 0:512]).then_inc(L["sAE"], 1)
            for ft in range(8):
                sc.wait_ge(L["sPE"], peP["tk"][ft])
                if ft == 0 and prev["convv_end"]:
                    sc.wait_ge(L["sVE"], prev["convv_end"])
                ps = p.pw1[:, 0:512] if ft % 2 == 0 else p.pw1[:, 512:1024]
                sc.copy(xps[jk % 2][:, ft * 515 + 3:ft * 515 + 515],
                        ps).then_inc(L["sAE"], 1)
                sc.wait_ge(L["sVE"], vP["convq"][ft])
                if ft == 0 and prev["sto_k"]:
                    sc.wait_ge(L["sSP"], prev["sto_k"])
                sc.activation(st_t(jq, ft), stf_t(ft),
                              AF.Silu).then_inc(L["sAE"], 1)
            sc.wait_ge(L["sPE"], peP["l2qA"])
            if prev["l2kmul"]:
                sc.wait_ge(L["sVE"], prev["l2kmul"])
            for hh in range(2):
                ps = p.pp0 if hh == 0 else p.pp1
                dst = [p.gs1, p.gs2][hh]
                sc.activation(dst[:, 0:512], ps[:, 0:512], AF.Ln)
                sc.activation(dst[:, 0:512], dst[:, 0:512], AF.Exp, scale=-0.5)
            sc.last.then_inc(L["sAE"], 1)
            sc.wait_ge(L["sPE"], peP["l2qB"])
            for hh in range(2):
                ps = p.pp0 if hh == 0 else p.pp1
                dst = [p.gs3, p.gs4][hh]
                sc.activation(dst[:, 0:512], ps[:, 0:512], AF.Ln)
                sc.activation(dst[:, 0:512], dst[:, 0:512], AF.Exp, scale=-0.5)
            sc.last.then_inc(L["sAE"], 1)
            for ft in range(8):
                sc.wait_ge(L["sPE"], peP["tv"][ft])
                sc.wait_ge(L["sVE"], vP["convq"][7])
                ps = p.pw1[:, 0:512] if ft % 2 == 0 else p.pw1[:, 512:1024]
                sc.copy(xps[jv % 2][:, ft * 515 + 3:ft * 515 + 515],
                        ps).then_inc(L["sAE"], 1)
                sc.wait_ge(L["sVE"], vP["convk"][ft])
                if ft == 0 and prev["sto_v"]:
                    sc.wait_ge(L["sSP"], prev["sto_v"])
                sc.activation(st_t(jk, ft), stf_t(ft),
                              AF.Silu).then_inc(L["sAE"], 1)
            sc.wait_ge(L["sPE"], peP["l2kA"])
            sc.wait_ge(L["sVE"], vP["l2qmul"])
            for hh in range(2):
                ps = p.pp2 if hh == 0 else p.pp3
                dst = [p.gs1, p.gs2][hh]
                sc.activation(dst[:, 0:512], ps[:, 0:512], AF.Ln)
                sc.activation(dst[:, 0:512], dst[:, 0:512], AF.Exp, scale=-0.5)
            sc.last.then_inc(L["sAE"], 1)
            sc.wait_ge(L["sPE"], peP["l2kB"])
            for hh in range(2):
                ps = p.pp2 if hh == 0 else p.pp3
                dst = [p.gs3, p.gs4][hh]
                sc.activation(dst[:, 0:512], ps[:, 0:512], AF.Ln)
                sc.activation(dst[:, 0:512], dst[:, 0:512], AF.Exp, scale=-0.5)
            sc.last.then_inc(L["sAE"], 1)
            if not last:
                sc.wait_ge(L["sVE"], vP["vesub"])
                sc.activation(lnrs[:], lnrs[:], AF.Ln, bias=LEPS)
                sc.activation(lnrs[:], lnrs[:], AF.Exp, scale=-0.5)
                sc.last.then_inc(L["sAE"], 1)
            for ft in range(8):
                sc.wait_ge(L["sVE"], vP["convv"][ft])
                if ft == 0:
                    sc.wait_ge(L["sSP"], sP["sto_q"])
                sc.activation(st_t(jv, ft), stf_t(ft),
                              AF.Silu).then_inc(L["sAE"], 1)
            if not last:
                sc.wait_ge(L["sPE"], peP["bmm"])
                sc.wait_ge(L["sSP"], prev["bst"])
                for h in range(4):
                    sc.activation(bet_a[h * 32:h * 32 + 1, :],
                                  p.pw0[h * 32:h * 32 + 1, 0:512], AF.Sigmoid)
                sc.last.then_inc(L["sAE"], 1)
        p.cnt["sAE"] += 55 if not last else 52

        @block.vector
        def _(v, s=s, last=last, aP=aP, vP=vP, gP=gP, peP=peP,
              prev=dict(prev), jq=jq, jk=jk, jv=jv):
            hap = p.halo[:].rearrange("p (pr ft j) -> p pr ft j", pr=3, ft=FT)
            cv = p.cv_pp[:].rearrange("p (pr ft j) -> p pr ft j", pr=3, ft=FT)

            def conv(j, pi, evl, s_own, s_prev):
                for ft in range(FT):
                    w_ = evl[ft]
                    if ft >= 4:
                        w_ = max(w_, s_own[ft - 4])
                    elif s_prev:
                        w_ = max(w_, s_prev[ft + 4])
                    v.wait_ge(L["sAE"], w_)
                    xpj = xps[j % 2]
                    v.tensor_copy(xpj[:, ft * 515:ft * 515 + 3],
                                  hap[:, pi, ft, :])
                    v.tensor_scalar_mul(stf_t(ft),
                                        xpj[:, ft * 515:ft * 515 + 512],
                                        cv[:, pi, ft, 0:1])
                    for jj in range(1, 4):
                        v.scalar_tensor_tensor(
                            stf_t(ft),
                            xpj[:, ft * 515 + jj:ft * 515 + jj + 512],
                            cv[:, pi, ft, jj:jj + 1], stf_t(ft),
                            ALU.mult, ALU.add)
                    v.tensor_copy(hap[:, pi, ft, :],
                                  xpj[:, ft * 515 + 512:ft * 515 + 515])
                    v.last.then_inc(L["sVE"], 1)

            def l2mul(j):
                for ft in range(FT):
                    rs = [p.gs1, p.gs2, p.gs3, p.gs4][ft // 2]
                    v.tensor_mul(st_t(j, ft), st_t(j, ft), rs[:, 0:512])
                v.last.then_inc(L["sVE"], 1)

            conv(jq, 0, aP["evq"], aP["siq"], prev["siv_l"])
            if not last:
                v.wait_ge(L["sPO"], gP["mu2"])
                v.tensor_sub(lnrs[:], p.pp3[:, 0:512], lnrs[:])
                v.last.then_inc(L["sVE"], 1)
            conv(jk, 1, aP["siq"], aP["sik"], aP["siq"])
            v.wait_ge(L["sAE"], aP["rsqB"])
            l2mul(jq)
            if not last:
                v.wait_ge(L["sAE"], aP["lnexp"])
                v.wait_ge(L["sPE"], peP["tv"][7])
                v.wait_ge(L["sSP"], prev["bst"])
                for ft in range(FT):
                    v.tensor_sub(lnt[:], stile(ft, (s + 1) * SLA, SLA),
                                 lnmu[:])
                    v.tensor_mul(lnx_t(s + 1, ft), lnt[:], lnrs[:])
                v.last.then_inc(L["sVE"], 1)
            conv(jv, 2, aP["sik"], aP["siv"], aP["sik"])
            v.wait_ge(L["sAE"], aP["rskB"])
            l2mul(jk)
        p.cnt["sVE"] += 28 if not last else 26

        @block.sync
        def _(sy, s=s, last=last, aP=aP, vP=vP, jq=jq, jk=jk, jv=jv):
            sy.wait_ge(L["sVE"], vP["l2qmul"])
            sy.dma_start(out=scr["qn_s"][:, :, s * SLA:(s + 1) * SLA],
                         in_=sts[jq % 2][:].rearrange("p (ft t) -> p ft t",
                                                      ft=FT)
                         ).then_inc(L["sSP"], 16)
            sy.wait_ge(L["sVE"], vP["l2kmul"])
            sy.dma_start(out=scr["kn_s"][:, :, s * SLA:(s + 1) * SLA],
                         in_=sts[jk % 2][:].rearrange("p (ft t) -> p ft t",
                                                      ft=FT)
                         ).then_inc(L["sSP"], 16)
            sy.wait_ge(L["sAE"], aP["siv"][7])
            sy.dma_start(out=scr["v_s"][:, :, s * SLA:(s + 1) * SLA],
                         in_=sts[jv % 2][:].rearrange("p (ft t) -> p ft t",
                                                      ft=FT)
                         ).then_inc(L["sSP"], 16)
            if not last:
                sy.wait_ge(L["sAE"], aP["sig"])
                for h in range(4):
                    sy.dma_start(out=scr["betar_s"][h:h + 1, (s + 1) * SLA:
                                                    (s + 2) * SLA],
                                 in_=bet_a[h * 32:h * 32 + 1, :]
                                 ).then_inc(L["sSP"], 16)
        p.cnt["sSP"] += 112 if not last else 48

        prev = dict(lnap=vP.get("lnap", prev["lnap"]),
                    zipB_end=aP["sik"][7], rsqB=aP["rsqB"], rskB=aP["rskB"],
                    vesub=vP.get("vesub", prev["vesub"]),
                    convk_end=vP["convk"][7], convv_end=vP["convv"][7],
                    l2kmul=vP["l2kmul"], sig=aP.get("sig", prev["sig"]),
                    sto_k=sP["sto_k"], sto_v=sP["sto_v"],
                    bst=sP.get("bst", prev["bst"]), siv_l=aP["siv"],
                    l2kB=peP["l2kB"], sie_v=0, a_end=p.cnt["sAE"])

    fence(p, block)
    es.close()


def emit_dn_B(p, block, din, scr, stile):
    nc = p.nc
    L = p.sems
    es = ExitStack()
    wo_sb = es.enter_context(nc.sbuf_tensor("wo_sb", [128, 8192], BF16))
    qn_b = es.enter_context(nc.sbuf_tensor("qn_b", [128, 2 * FT * SLB], BF16))
    kn_b = es.enter_context(nc.sbuf_tensor("kn_b", [128, 2 * FT * SLB], BF16))
    v_b = es.enter_context(nc.sbuf_tensor("v_b", [128, 2 * FT * SLB], BF16))
    w_b = es.enter_context(nc.sbuf_tensor("w_b", [128, 2 * FT * SLB], BF16))
    u_b = es.enter_context(nc.sbuf_tensor("u_b", [128, 2 * FT * SLB], BF16))
    kb_b = es.enter_context(nc.sbuf_tensor("kb_b", [128, FT * SLB], BF16))
    vb_b = es.enter_context(nc.sbuf_tensor("vb_b", [128, FT * SLB], BF16))
    o_b = es.enter_context(nc.sbuf_tensor("o_b", [128, FT * SLB], BF16))
    ao_b = es.enter_context(nc.sbuf_tensor("ao_b", [128, FT * SLB], BF16))
    kpm = es.enter_context(nc.sbuf_tensor("kpm", [128, 2 * FT * 128], BF16))
    vpm = es.enter_context(nc.sbuf_tensor("vpm", [128, 2 * FT * 128], BF16))
    dm_b = es.enter_context(nc.sbuf_tensor("dm_b", [128, SLB], F32))
    bet_b = es.enter_context(nc.sbuf_tensor("bet_b", [4, 2 * SLB], F32))
    aa_b = es.enter_context(nc.sbuf_tensor("aa_b", [128, FT * CPS], F32))
    bb_b = es.enter_context(nc.sbuf_tensor("bb_b", [128, FT * CPS], F32))
    ch = {}
    for nm in ("chL", "chLt", "chP", "chPt", "chQ", "chB"):
        ch[nm] = es.enter_context(nc.sbuf_tensor(nm, [128, CHW], F32))
    tt_bf = es.enter_context(nc.sbuf_tensor("tt_bf", [128, CHW], BF16))
    mu_a = es.enter_context(nc.sbuf_tensor("mu_a", [128, SLB], F32))
    rs_a = es.enter_context(nc.sbuf_tensor("rs_a", [128, SLB], F32))
    mu_c = es.enter_context(nc.sbuf_tensor("mu_c", [128, SLB], F32))
    rs_c = es.enter_context(nc.sbuf_tensor("rs_c", [128, SLB], F32))
    lt_b = es.enter_context(nc.sbuf_tensor("lt_b", [128, SLB], F32))

    @block.gpsimd
    def _(g):
        p.w(g, "sPE")
        p.w(g, "sVE")
        p.w(g, "sAE")
        g.dma_start(out=wo_sb[:], in_=din["wod"]).then_inc(L["sWT"], 16)
    p.bump("sWT", 16)
    wo_w = p.cnt["sWT"]

    def wotile(fi, fo):
        return wo_sb[:].rearrange("p (a b q) -> p a b q", a=FT, b=FT)[:, fi,
                                                                      fo, :]

    def slab(buf, c, ft, lo=0, n=SLB):
        base = ((c % 2) * FT + ft) * SLB
        return buf[:, base + lo:base + lo + n]

    b3 = lambda x: x.rearrange("p (n c) -> p n c", c=64)
    PL = {}

    for c in range(NSB + 2):
        c1, c2 = c - 1, c - 2
        hasB1 = c < NSB
        hasA = 0 <= c1 < NSB
        hasB = 0 <= c2 < NSB
        pl = {}
        pe = p.cnt["sPE"]
        ve = p.cnt["sVE"]
        ac = p.cnt["sAE"]
        po = p.cnt["sPO"]
        if hasB:
            pl["subln1"] = ve = ve + 1
            pl["resid"] = ve = ve + 1
            pl["ln2mm"] = pe = pe + 1
        if hasA:
            pl["rowsum"] = pe = pe + 1
            pl["dm"] = ve = ve + 1
        if hasB1:
            pl["beta"] = pe = pe + 1
            pl["kbvb"] = ve = ve + 1
        if hasB:
            pl["subln2"] = ve = ve + 1
        if hasB1:
            pl["tr"] = [pe + 1 + i for i in range(8)]
            pe += 8
            pl["aat"] = pe = pe + 1
            pl["mask"] = ve = ve + 1
            pl["pair"] = []
            pl["qupd"] = []
            for lvl in range(5):
                pe += 1
                pl["pair"].append(pe)
                pe += 1
                pl["qupd"].append(pe)
            pl["wu"] = [pe + 1 + i for i in range(CPS)]
            pe += CPS
        if hasA:
            pl["scan"] = ve = ve + 1
        if hasB1:
            pl["tt"] = ve = ve + 1
            pl["wuev"] = [ve + 1 + i for i in range(CPS)]
            ve += CPS
        if hasA:
            pl["rmsA"] = pe = pe + 1
            pl["rmsB"] = pe = pe + 1
            pl["scale"] = ve = ve + 1
            pl["wo"] = [pe + 1 + i for i in range(8)]
            pe += 8
            pl["ln1mm"] = pe = pe + 1
        if hasB:
            pl["apply"] = ve = ve + 1
        if hasB:
            pl["muln2"] = ac = ac + 1
            pl["lnexp2"] = ac = ac + 1
        if hasB1:
            pl["trev"] = [ac + 1 + i for i in range(8)]
            ac += 8
            pl["chcp"] = []
            for lvl in range(5):
                ac += 1
                pl["chcp"].append(ac)
                if lvl < 4:
                    ac += 1
                    pl["chcp"].append(ac)
        if hasA:
            pl["rstdA"] = ac = ac + 1
            pl["rstdB"] = ac = ac + 1
            pl["woev"] = [ac + 1 + i for i in range(8)]
            ac += 8
            pl["muln1"] = ac = ac + 1
            pl["lnexp1"] = ac = ac + 1
        if hasB:
            pl["ln2sq"] = po = po + 1
            pl["mu2ln2"] = po = po + 1
        if hasA:
            pl["qk"] = po = po + 1
            pl["sqrms"] = po = po + 1
            pl["ln1sq"] = po = po + 1
            pl["mu2ln1"] = po = po + 1
        pl["loads"] = p.cnt["sLD"] + (64 if hasB1 else 0)
        pl["x1st"] = p.cnt["sSP"] + (16 if hasB else 0)
        PL[c] = pl
        q1 = PL.get(c1, {})
        q2 = PL.get(c2, {})

        if DEV_DNB and c == 1:
            @block.sync
            def _(sy, q=PL[0]):
                sy.wait_ge(L["sVE"], q["kbvb"])
                sy.dma_start(out=scr["dbg"][:, 0:2048],
                             in_=kb_b[:]).then_inc(L["sDB"], 16)
                sy.dma_start(out=scr["dbg"][:, 2048:4096],
                             in_=vb_b[:]).then_inc(L["sDB"], 16)
                sy.wait_ge(L["sVE"], q["wuev"][-1])
                sy.dma_start(out=scr["dbg"][:, 4096:6144],
                             in_=w_b[:, 0:2048]).then_inc(L["sDB"], 16)
                sy.dma_start(out=scr["dbg"][:, 6144:8192],
                             in_=u_b[:, 0:2048]).then_inc(L["sDB"], 16)
                sy.wait_ge(L["sVE"], q["tt"])
                sy.dma_start(out=scr["dbg"][:, 8192:8320],
                             in_=tt_bf[:]).then_inc(L["sDB"], 16)
            p.cnt["sDB"] += 80
        if DEV_DNB and c == 2:
            @block.sync
            def _(sy, q=PL[1]):
                sy.wait_ge(L["sVE"], q["scale"])
                sy.dma_start(out=scr["dbg"][:, 8704:10752],
                             in_=o_b[:]).then_inc(L["sDB"], 16)
                sy.wait_ge(L["sAE"], q["woev"][7])
                sy.dma_start(out=scr["dbg"][:, 10752:12800],
                             in_=ao_b[:]).then_inc(L["sDB"], 16)
            p.cnt["sDB"] += 32

        @block.sync
        def _(sy, c=c, c2=c2, pl=pl, q1=q1, q2=q2, hasB1=hasB1, hasB=hasB):
            if hasB1:
                if q1 and "scan" in q1:
                    sy.wait_ge(L["sVE"], q1["scan"])
                    sy.wait_ge(L["sPO"], q1["qk"])
                if q2 and "beta" in q2:
                    sy.wait_ge(L["sPE"], q2["beta"])
                for src_nm, dst in (("qn_s", qn_b), ("kn_s", kn_b),
                                    ("v_s", v_b)):
                    sy.dma_start(
                        out=dst[:].rearrange("p (sl ft t) -> p sl ft t",
                                             sl=2, ft=FT)[:, c % 2, :, :],
                        in_=scr[src_nm][:, :, c * SLB:(c + 1) * SLB]
                        ).then_inc(L["sLD"], 16)
                sy.dma_start(out=bet_b[0:4, (c % 2) * SLB:(c % 2 + 1) * SLB],
                             in_=scr["betar_s"][:, c * SLB:(c + 1) * SLB]
                             ).then_inc(L["sLD"], 16)
            if hasB:
                sy.wait_ge(L["sVE"], pl["resid"])
                sy.dma_start(out=scr["x1_s"][:, :, c2 * SLB:(c2 + 1) * SLB],
                             in_=p.stream[:].rearrange("p (ft cc) -> p ft cc",
                                                       ft=FT)
                             [:, :, 1 + c2 * SLB:1 + (c2 + 1) * SLB]
                             ).then_inc(L["sSP"], 16)
        p.cnt["sLD"] = pl["loads"]
        p.cnt["sSP"] = pl["x1st"]

        @block.gpsimd
        def _(g, c=c, c1=c1, c2=c2, pl=pl, q1=q1, hasB1=hasB1, hasA=hasA,
              hasB=hasB):
            if hasB:
                g.wait_ge(L["sVE"], pl["resid"])
                if q1 and "ln1mm" in q1:
                    g.wait_ge(L["sPE"], q1["ln1mm"])
                for ft in range(FT):
                    g.tensor_mul(p.sq[:, ft * SLB:(ft + 1) * SLB],
                                 stile(ft, c2 * SLB, SLB),
                                 stile(ft, c2 * SLB, SLB))
                g.last.then_inc(L["sPO"], 1)
                g.wait_ge(L["sAE"], pl["muln2"])
                g.tensor_mul(rs_c[:], mu_c[:], mu_c[:])
                g.last.then_inc(L["sPO"], 1)
            if hasA:
                if "ln2mm" in pl:
                    g.wait_ge(L["sPE"], pl["ln2mm"])
                g.wait_ge(L["sLD"], q1["loads"])
                for ft in range(FT):
                    g.tensor_mul(p.sq[:, ft * SLB:(ft + 1) * SLB],
                                 slab(qn_b, c1, ft), slab(kn_b, c1, ft))
                g.last.then_inc(L["sPO"], 1)
                g.wait_ge(L["sVE"], pl["scan"])
                g.wait_ge(L["sPE"], pl["rowsum"])
                for ft in range(FT):
                    g.tensor_mul(p.sq[:, ft * SLB:(ft + 1) * SLB],
                                 o_b[:, ft * SLB:(ft + 1) * SLB],
                                 o_b[:, ft * SLB:(ft + 1) * SLB])
                g.last.then_inc(L["sPO"], 1)
                g.wait_ge(L["sAE"], pl["woev"][7])
                g.wait_ge(L["sPE"], pl["rmsB"])
                for ft in range(FT):
                    g.tensor_mul(p.sq[:, ft * SLB:(ft + 1) * SLB],
                                 ao_b[:, ft * SLB:(ft + 1) * SLB],
                                 ao_b[:, ft * SLB:(ft + 1) * SLB])
                g.last.then_inc(L["sPO"], 1)
                g.wait_ge(L["sAE"], pl["muln1"])
                g.tensor_mul(rs_a[:], mu_a[:], mu_a[:])
                g.last.then_inc(L["sPO"], 1)
        p.cnt["sPO"] = po

        @block.tensor
        def _(t, c=c, c1=c1, c2=c2, pl=pl, q1=q1, hasB1=hasB1, hasA=hasA,
              hasB=hasB):
            if hasB:
                t.wait_ge(L["sPO"], pl["ln2sq"])
                if q1 and "muln1" in q1:
                    t.wait_ge(L["sAE"], q1["muln1"])
                t.wait_ge(L["sVE"], pl["subln1"])
                for ft in range(FT):
                    t.matmul(p.pp0[:, 0:SLB], p.c_sw1024[:],
                             stile(ft, c2 * SLB, SLB),
                             start=(ft == 0), stop=(ft == FT - 1))
                for ft in range(FT):
                    t.matmul(p.pp1[:, 0:SLB], p.c_sw1024[:],
                             p.sq[:, ft * SLB:(ft + 1) * SLB],
                             start=(ft == 0), stop=(ft == FT - 1))
                t.last.then_inc(L["sPE"], 1)
            if hasA:
                t.wait_ge(L["sPO"], pl["qk"])
                if hasB:
                    t.wait_ge(L["sAE"], pl["muln2"])
                for ft in range(FT):
                    t.matmul(p.pp0[:, 0:SLB], p.c_ones[:],
                             p.sq[:, ft * SLB:(ft + 1) * SLB],
                             start=(ft == 0), stop=(ft == FT - 1))
                t.last.then_inc(L["sPE"], 1)
            if hasB1:
                t.wait_ge(L["sLD"], pl["loads"])
                if q1 and "wuev" in q1:
                    t.wait_ge(L["sVE"], q1["wuev"][CPS - 1])
                for h in range(4):
                    t.matmul(p.pw1[:, h * SLB:(h + 1) * SLB],
                             p.c_sel4[0:4, h * 128:(h + 1) * 128],
                             bet_b[0:4,
                                   (c % 2) * SLB:(c % 2) * SLB + SLB],
                             start=(h % 2 == 0), stop=(h % 2 == 1),
                             tile_position=(0, 0))
                t.last.then_inc(L["sPE"], 1)
                t.wait_ge(L["sVE"], pl["kbvb"])
                if q1 and "trev" in q1:
                    t.wait_ge(L["sAE"], q1["trev"][7])
                for i in range(8):
                    if i > 0:
                        t.wait_ge(L["sAE"], pl["trev"][i - 1])
                    sb = kb_b if i < 4 else vb_b
                    bb = (i % 4) // 2
                    fh = i % 2
                    for ft in range(fh * 4, fh * 4 + 4):
                        t.matmul(p.pp3[:, (ft % 4) * 128:(ft % 4 + 1) * 128],
                                 sb[:, ft * SLB + bb * 128:
                                    ft * SLB + (bb + 1) * 128],
                                 p.c_id_bf[:], start=(ft % 4 == 0),
                                 stop=(ft % 4 == 3))
                    t.last.then_inc(L["sPE"], 1)
                for rb in (0, 64):
                    order = [(0, 0), (64, 0), (0, 256), (64, 256)]
                    for gi, (cb, base) in enumerate(order):
                        cc = (cb // 64) * 2 + (rb // 64)
                        for fi in range(FT):
                            if base == 0:
                                lhs = kb_b[:, fi * SLB + cc * 64:
                                           fi * SLB + (cc + 1) * 64]
                                rhs = slab(kn_b, c, fi, cc * 64, 64)
                            else:
                                lhs = slab(kn_b, c, fi, cc * 64, 64)
                                rhs = kb_b[:, fi * SLB + cc * 64:
                                           fi * SLB + (cc + 1) * 64]
                            t.matmul(p.pw0[rb:rb + 64,
                                           base + cb:base + cb + 64],
                                     lhs, rhs,
                                     start=(gi == 0 and fi == 0),
                                     stop=(gi == 3 and fi == FT - 1),
                                     tile_position=(0, rb))
                t.last.then_inc(L["sPE"], 1)
                curP, curPt = ch["chL"], ch["chLt"]
                for lvl in range(5):
                    if lvl == 0:
                        t.wait_ge(L["sVE"], pl["mask"])
                    else:
                        t.wait_ge(L["sAE"], pl["chcp"][2 * lvl - 1])
                    for rb in (0, 64):
                        groups = [(0, 0, curPt, curP), (64, 0, curPt, curP)]
                        if lvl < 4:
                            groups += [(0, 256, curP, curPt),
                                       (64, 256, curP, curPt)]
                        for gi, (cb, base, lh, rh) in enumerate(groups):
                            t.matmul(p.pw1[rb:rb + 64,
                                           base + cb:base + cb + 64],
                                     lh[rb:rb + 64, cb:cb + 64],
                                     rh[rb:rb + 64, cb:cb + 64],
                                     start=(gi == 0),
                                     stop=(gi == len(groups) - 1),
                                     tile_position=(rb, rb))
                    t.last.then_inc(L["sPE"], 1)
                    t.wait_ge(L["sAE"],
                              pl["chcp"][2 * lvl] if lvl < 4
                              else pl["chcp"][8])
                    for rb in (0, 64):
                        for gi, cb in enumerate((0, 64)):
                            t.matmul(p.pw0[rb:rb + 64, cb:cb + 64],
                                     p.c_id_f[rb:rb + 64, rb:rb + 64],
                                     ch["chQ"][rb:rb + 64, cb:cb + 64],
                                     start=(gi == 0), stop=False,
                                     tile_position=(rb, rb))
                            t.matmul(p.pw0[rb:rb + 64, cb:cb + 64],
                                     ch["chP"][rb:rb + 64, cb:cb + 64],
                                     ch["chQ"][rb:rb + 64, cb:cb + 64],
                                     start=False, stop=(gi == 1),
                                     tile_position=(rb, rb))
                    t.last.then_inc(L["sPE"], 1)
                    curP, curPt = ch["chP"], ch["chPt"]
                t.wait_ge(L["sVE"], pl["tt"])
                for cc in range(CPS):
                    if cc >= 2:
                        t.wait_ge(L["sVE"], pl["wuev"][cc - 2])
                    rb, cb = (cc % 2) * 64, (cc // 2) * 64
                    bb = cc // 2
                    wps = (p.pw1[:, 0:512] if cc % 2 == 0
                           else p.pw1[:, 512:1024])
                    ups = (p.pw0[:, 512:1024] if cc % 2 == 0
                           else p.pp2[:, 0:512])
                    for ft in range(FT):
                        t.matmul(wps[:, ft * 64:(ft + 1) * 64],
                                 kpm[rb:rb + 64, bb * FT * 128 + ft * 128:
                                     bb * FT * 128 + (ft + 1) * 128],
                                 tt_bf[rb:rb + 64, cb:cb + 64],
                                 start=(ft == 0), stop=(ft == FT - 1),
                                 tile_position=(rb, 0))
                    for ft in range(FT):
                        t.matmul(ups[:, ft * 64:(ft + 1) * 64],
                                 vpm[rb:rb + 64, bb * FT * 128 + ft * 128:
                                     bb * FT * 128 + (ft + 1) * 128],
                                 tt_bf[rb:rb + 64, cb:cb + 64],
                                 start=(ft == 0), stop=(ft == FT - 1),
                                 tile_position=(rb, 0))
                    t.last.then_inc(L["sPE"], 1)
            if hasA:
                t.wait_ge(L["sPO"], pl["sqrms"])
                t.wait_ge(L["sVE"], pl["dm"])
                if hasB:
                    t.wait_ge(L["sVE"], pl["subln2"])
                for rnd in range(2):
                    if rnd == 1:
                        t.wait_ge(L["sAE"], pl["rstdA"])
                    for hh in range(2):
                        h = rnd * 2 + hh
                        ps = p.pp0 if hh == 0 else p.pp1
                        for k in range(2):
                            t.matmul(ps[:, 0:SLB], p.c_sw256[:],
                                     p.sq[:, (2 * h + k) * SLB:
                                          (2 * h + k + 1) * SLB],
                                     start=(k == 0), stop=(k == 1))
                    t.last.then_inc(L["sPE"], 1)
                if c == 1:
                    t.wait_ge(L["sWT"], wo_w)
                t.wait_ge(L["sVE"], pl["scale"])
                for fo in range(FT):
                    if fo >= 2:
                        t.wait_ge(L["sAE"], pl["woev"][fo - 2])
                    ps = p.pp0 if fo % 2 == 0 else p.pp1
                    for fi in range(FT):
                        t.matmul(ps[:, 0:SLB], wotile(fi, fo),
                                 o_b[:, fi * SLB:(fi + 1) * SLB],
                                 start=(fi == 0), stop=(fi == FT - 1))
                    t.last.then_inc(L["sPE"], 1)
                t.wait_ge(L["sPO"], pl["ln1sq"])
                for ft in range(FT):
                    t.matmul(p.pp0[:, 0:SLB], p.c_sw1024[:],
                             ao_b[:, ft * SLB:(ft + 1) * SLB],
                             start=(ft == 0), stop=(ft == FT - 1))
                for ft in range(FT):
                    t.matmul(p.pp1[:, 0:SLB], p.c_sw1024[:],
                             p.sq[:, ft * SLB:(ft + 1) * SLB],
                             start=(ft == 0), stop=(ft == FT - 1))
                t.last.then_inc(L["sPE"], 1)
        p.cnt["sPE"] = pe

        @block.scalar
        def _(sc, c=c, c1=c1, c2=c2, pl=pl, q1=q1, hasB1=hasB1, hasA=hasA,
              hasB=hasB, ve_end=ve):
            if hasB:
                sc.wait_ge(L["sPE"], pl["ln2mm"])
                sc.copy(mu_c[:], p.pp0[:, 0:SLB]).then_inc(L["sAE"], 1)
                sc.wait_ge(L["sVE"], pl["subln2"])
                sc.activation(rs_c[:], rs_c[:], AF.Ln, bias=LEPS)
                sc.activation(rs_c[:], rs_c[:], AF.Exp, scale=-0.5)
                sc.last.then_inc(L["sAE"], 1)
            if hasB1:
                for i in range(8):
                    sc.wait_ge(L["sPE"], pl["tr"][i])
                    dst = kpm if i < 4 else vpm
                    bb = (i % 4) // 2
                    fh = i % 2
                    sc.copy(dst[:, bb * FT * 128 + fh * 512:
                                bb * FT * 128 + fh * 512 + 512],
                            p.pp3[:, 0:512]).then_inc(L["sAE"], 1)
                for lvl in range(5):
                    sc.wait_ge(L["sPE"], pl["pair"][lvl])
                    sc.copy(ch["chP"][:], p.pw1[:, 0:CHW])
                    if lvl < 4:
                        sc.copy(ch["chPt"][:], p.pw1[:, 256:256 + CHW])
                    sc.last.then_inc(L["sAE"], 1)
                    if lvl < 4:
                        sc.wait_ge(L["sPE"], pl["qupd"][lvl])
                        sc.copy(ch["chQ"][:],
                                p.pw0[:, 0:CHW]).then_inc(L["sAE"], 1)
            if hasA:
                sc.wait_ge(L["sPE"], pl["rmsA"])
                if q1 and "scale" in q1:
                    sc.wait_ge(L["sVE"], q1["scale"])
                for hh in range(2):
                    ps = p.pp0 if hh == 0 else p.pp1
                    dst = [p.gs1, p.gs2][hh]
                    sc.activation(dst[:, 0:SLB], ps[:, 0:SLB], AF.Ln,
                                  bias=LEPS)
                    sc.activation(dst[:, 0:SLB], dst[:, 0:SLB], AF.Exp,
                                  scale=-0.5)
                sc.last.then_inc(L["sAE"], 1)
                sc.wait_ge(L["sPE"], pl["rmsB"])
                for hh in range(2):
                    ps = p.pp0 if hh == 0 else p.pp1
                    dst = [p.gs3, p.gs4][hh]
                    sc.activation(dst[:, 0:SLB], ps[:, 0:SLB], AF.Ln,
                                  bias=LEPS)
                    sc.activation(dst[:, 0:SLB], dst[:, 0:SLB], AF.Exp,
                                  scale=-0.5)
                sc.last.then_inc(L["sAE"], 1)
                for fo in range(FT):
                    sc.wait_ge(L["sPE"], pl["wo"][fo])
                    if fo == 0 and DEV_DNB and c == 2:
                        sc.wait_ge(L["sDB"], p.cnt["sDB"])
                    if fo == 0 and hasB:
                        sc.wait_ge(L["sVE"], pl["resid"])
                    ps = p.pp0 if fo % 2 == 0 else p.pp1
                    sc.copy(ao_b[:, fo * SLB:(fo + 1) * SLB],
                            ps[:, 0:SLB]).then_inc(L["sAE"], 1)
                sc.wait_ge(L["sPE"], pl["ln1mm"])
                if hasB:
                    sc.wait_ge(L["sVE"], pl["resid"])
                sc.copy(mu_a[:], p.pp0[:, 0:SLB]).then_inc(L["sAE"], 1)
                sc.wait_ge(L["sVE"], ve_end + 1)
                sc.activation(rs_a[:], rs_a[:], AF.Ln, bias=LEPS)
                sc.activation(rs_a[:], rs_a[:], AF.Exp, scale=-0.5)
                sc.last.then_inc(L["sAE"], 1)
        p.cnt["sAE"] = ac

        @block.vector
        def _(v, c=c, c1=c1, c2=c2, pl=pl, q1=q1, q2=q2, hasB1=hasB1,
              hasA=hasA, hasB=hasB):
            if hasB:
                v.wait_ge(L["sPO"], q1["mu2ln1"])
                v.wait_ge(L["sPE"], q1["ln1mm"])
                v.tensor_sub(rs_a[:], p.pp1[:, 0:SLB], rs_a[:])
                v.last.then_inc(L["sVE"], 1)
                v.wait_ge(L["sAE"], q1["lnexp1"])
                for ft in range(FT):
                    v.tensor_sub(lt_b[:], ao_b[:, ft * SLB:(ft + 1) * SLB],
                                 mu_a[:])
                    v.tensor_mul(lt_b[:], lt_b[:], rs_a[:])
                    v.tensor_add(stile(ft, c2 * SLB, SLB), lt_b[:],
                                 stile(ft, c2 * SLB, SLB))
                v.last.then_inc(L["sVE"], 1)
            if hasA:
                v.wait_ge(L["sPE"], pl["rowsum"])
                dm = p.c_dmask[:].unsqueeze(1).broadcast_to((128, CPS, 64))
                v.tensor_mul(b3(dm_b[:]), b3(p.pp0[:, 0:SLB]), dm)
                v.last.then_inc(L["sVE"], 1)
            if hasB1:
                v.wait_ge(L["sPE"], pl["beta"])
                if DEV_DNB and c == 1:
                    v.wait_ge(L["sDB"], p.cnt["sDB"])
                for ft in range(FT):
                    ps = p.pw1[:, (ft // 2) * SLB:(ft // 2 + 1) * SLB]
                    v.tensor_mul(kb_b[:, ft * SLB:(ft + 1) * SLB],
                                 slab(kn_b, c, ft), ps)
                    v.tensor_mul(vb_b[:, ft * SLB:(ft + 1) * SLB],
                                 slab(v_b, c, ft), ps)
                v.last.then_inc(L["sVE"], 1)
            if hasB:
                v.wait_ge(L["sPO"], pl["mu2ln2"])
                v.tensor_sub(rs_c[:], p.pp1[:, 0:SLB], rs_c[:])
                v.last.then_inc(L["sVE"], 1)
            if hasB1:
                v.wait_ge(L["sPE"], pl["aat"])
                bc = lambda x: x[:].unsqueeze(1).broadcast_to(
                    (128, NPAIR, 64))
                v.tensor_mul(b3(ch["chL"][:]), b3(p.pw0[:, 0:CHW]),
                             bc(p.c_maskL))
                v.tensor_mul(b3(ch["chLt"][:]), b3(p.pw0[:, 256:256 + CHW]),
                             bc(p.c_maskU))
                v.scalar_tensor_tensor(b3(ch["chB"][:]), b3(ch["chLt"][:]),
                                       -1.0, bc(p.c_eye1), ALU.mult, ALU.add)
                v.tensor_copy(ch["chQ"][:], ch["chB"][:])
                v.last.then_inc(L["sVE"], 1)
            if hasA:
                if DEV_DNB and c == 2:
                    v.wait_ge(L["sDB"], p.cnt["sDB"])
                if q1 and "wo" in q1:
                    v.wait_ge(L["sPE"], q1["wo"][7])
                    v.wait_ge(L["sPO"], q1["sqrms"])
                for ft in range(FT):
                    v.tensor_mul(p.gs3[:, 0:SLB], slab(kn_b, c1, ft),
                                 slab(w_b, c1, ft))
                    v.reduce_sum(aa_b[:, ft * CPS:(ft + 1) * CPS],
                                 b3(p.gs3[:, 0:SLB]), axis=AX.X)
                    v.tensor_mul(p.gs3[:, 0:SLB], slab(kn_b, c1, ft),
                                 slab(u_b, c1, ft))
                    v.reduce_sum(bb_b[:, ft * CPS:(ft + 1) * CPS],
                                 b3(p.gs3[:, 0:SLB]), axis=AX.X)
                v.tensor_scalar(aa_b[:], aa_b[:], -1.0, 1.0, ALU.mult,
                                ALU.add)
                sd = p.sdt[:].rearrange("p (ft cc) -> p ft cc", ft=FT)
                for ft in range(FT):
                    v.tensor_tensor_scan(sd[:, ft, 1:5],
                                         aa_b[:, ft * CPS:(ft + 1) * CPS],
                                         bb_b[:, ft * CPS:(ft + 1) * CPS],
                                         sd[:, ft, 0:1], ALU.mult, ALU.add)
                for ft in range(FT):
                    sdexp = sd[:, ft, 0:4].unsqueeze(2).broadcast_to(
                        (128, CPS, 64))
                    v.tensor_mul(b3(p.gs3[:, 0:SLB]),
                                 b3(slab(w_b, c1, ft)), sdexp)
                    v.tensor_sub(p.gs4[:, 0:SLB], slab(u_b, c1, ft),
                                 p.gs3[:, 0:SLB])
                    v.tensor_mul(p.gs4[:, 0:SLB], p.gs4[:, 0:SLB], dm_b[:])
                    v.tensor_mul(b3(p.gs3[:, 0:SLB]),
                                 b3(slab(qn_b, c1, ft)), sdexp)
                    v.tensor_add(o_b[:, ft * SLB:(ft + 1) * SLB],
                                 p.gs3[:, 0:SLB], p.gs4[:, 0:SLB])
                for ft in range(FT):
                    v.tensor_copy(sd[:, ft, 0:1], sd[:, ft, 4:5])
                v.last.then_inc(L["sVE"], 1)
            if hasB1:
                v.wait_ge(L["sPE"], pl["qupd"][4])
                e2 = p.c_eye2[:].unsqueeze(1).broadcast_to((128, NPAIR, 64))
                v.scalar_tensor_tensor(b3(tt_bf[:]), b3(p.pw0[:, 0:CHW]), 3.0,
                                       e2, ALU.mult, ALU.subtract)
                v.last.then_inc(L["sVE"], 1)
                for cc in range(CPS):
                    v.wait_ge(L["sPE"], pl["wu"][cc])
                    wps = (p.pw1[:, 0:512] if cc % 2 == 0
                           else p.pw1[:, 512:1024])
                    ups = (p.pw0[:, 512:1024] if cc % 2 == 0
                           else p.pp2[:, 0:512])
                    v.tensor_copy(
                        w_b[:].rearrange("p (sl ft t) -> p sl ft t", sl=2,
                                         ft=FT)[:, c % 2, :,
                                                cc * 64:(cc + 1) * 64],
                        wps.rearrange("p (ft k) -> p ft k", ft=FT))
                    v.tensor_copy(
                        u_b[:].rearrange("p (sl ft t) -> p sl ft t", sl=2,
                                         ft=FT)[:, c % 2, :,
                                                cc * 64:(cc + 1) * 64],
                        ups.rearrange("p (ft k) -> p ft k", ft=FT))
                    v.last.then_inc(L["sVE"], 1)
            if hasA:
                v.wait_ge(L["sAE"], pl["rstdB"])
                for ft in range(FT):
                    rs = [p.gs1, p.gs2, p.gs3, p.gs4][ft // 2]
                    v.scalar_tensor_tensor(o_b[:, ft * SLB:(ft + 1) * SLB],
                                           o_b[:, ft * SLB:(ft + 1) * SLB],
                                           p.g_pp[:, (ft % 2):(ft % 2) + 1],
                                           rs[:, 0:SLB], ALU.mult, ALU.mult)
                v.last.then_inc(L["sVE"], 1)
            if hasB:
                v.wait_ge(L["sAE"], pl["lnexp2"])
                v.wait_ge(L["sSP"], pl["x1st"])
                v.tensor_mul(lt_b[:], mu_c[:], rs_c[:])
                for ft in range(FT):
                    v.tensor_mul(stile(ft, c2 * SLB, SLB),
                                 stile(ft, c2 * SLB, SLB), rs_c[:])
                    v.tensor_sub(stile(ft, c2 * SLB, SLB),
                                 stile(ft, c2 * SLB, SLB), lt_b[:])
                v.last.then_inc(L["sVE"], 1)
        p.cnt["sVE"] = ve

    fence(p, block)
    es.close()


def emit_ffn_attn(p, block, din, scr, layer, stile):
    nc = p.nc
    L = p.sems
    es = ExitStack()
    wqkv_sb = es.enter_context(nc.sbuf_tensor(f"wqkv_sb{layer}", [128, FT * 12 * 128],
                                              BF16))
    fwo_sb = es.enter_context(nc.sbuf_tensor(f"fwo_sb{layer}", [128, 4 * FT * 128],
                                             BF16))
    tbl = es.enter_context(nc.sbuf_tensor(f"tbl{layer}", [128, 4 * 512], BF16))
    qw = es.enter_context(nc.sbuf_tensor(f"qw{layer}", [128, 4 * 512], BF16))
    qraw = es.enter_context(nc.sbuf_tensor(f"qraw{layer}", [128, 2 * 4 * 512], BF16))
    kr = es.enter_context(nc.sbuf_tensor(f"kr{layer}", [128, 3 * 4 * 512], BF16))
    vtm = es.enter_context(nc.sbuf_tensor(f"vtm{layer}", [128, 3 * 4 * 512], BF16))
    prT = es.enter_context(nc.sbuf_tensor(f"prT{layer}", [128, 8 * 512], BF16))
    o_n = es.enter_context(nc.sbuf_tensor(f"o_n{layer}", [128, 4 * 512], BF16))
    wo_buf = es.enter_context(nc.sbuf_tensor(f"wo_buf{layer}", [128, FT * 512], BF16))
    negmax = es.enter_context(nc.sbuf_tensor(f"negmax{layer}", [128, 8], F32))
    negrow = es.enter_context(nc.sbuf_tensor(f"negrow{layer}", [1, 512], BF16))
    p.c_mqk = es.enter_context(nc.sbuf_tensor(f"mqk{layer}", [128, 4096],
                                               BF16))
    p.c_mt = es.enter_context(nc.sbuf_tensor(f"mt{layer}", [128, 4096], BF16))
    p.gs5 = es.enter_context(nc.sbuf_tensor(f"gs5{layer}", [128, 512], F32))
    p.gs6 = es.enter_context(nc.sbuf_tensor(f"gs6{layer}", [128, 512], F32))
    p.gs7 = es.enter_context(nc.sbuf_tensor(f"gs7{layer}", [128, 512], F32))

    def krs(w3, ft):
        return kr[:, (w3 * 4 + ft) * 512:(w3 * 4 + ft + 1) * 512]

    def vts(w3, tb):
        return vtm[:, (w3 * 4 + tb) * 512:(w3 * 4 + tb + 1) * 512]

    def qrs(w2, ft):
        return qraw[:, (w2 * 4 + ft) * 512:(w2 * 4 + ft + 1) * 512]

    def qws(w2, ft):
        return qw[:, ft * 512:(ft + 1) * 512]

    @block.gpsimd
    def _(g):
        p.w(g, "sPE")
        p.w(g, "sVE")
        p.w(g, "sAE")
        g.dma_start(out=wqkv_sb[:],
                    in_=din["fwqkv"][layer]).then_inc(L["sWT"], 16)
        g.dma_start(out=fwo_sb[:],
                    in_=din["fwo"][layer]).then_inc(L["sWT"], 16)
    p.bump("sWT", 32)
    wt_w = p.cnt["sWT"]

    @block.sync
    def _(sy):
        p.w(sy, "sPE")
        p.w(sy, "sVE")
        p.w(sy, "sAE")
        sy.dma_start(out=p.c_mqk[:], in_=din["mqk"]).then_inc(L["sLD"], 16)
        sy.dma_start(out=p.c_mt[:], in_=din["mt"]).then_inc(L["sLD"], 16)
    p.bump("sLD", 32)
    msk_w = p.cnt["sLD"]

    def qkvtile(fi, fo):
        return wqkv_sb[:].rearrange("p (a b q) -> p a b q", a=FT,
                                    b=12)[:, fi, fo, :]

    def wotile(fi, fo):
        return fwo_sb[:].rearrange("p (a b q) -> p a b q", a=4,
                                   b=FT)[:, fi, fo, :]

    NWE = DEV_NW if DEV_NW else NW

    def emit_proj(w, pre_counts=None):
        t0 = w * WIN
        w2, w3 = w % 2, w % 3
        if pre_counts is None:
            pre_counts = [(nm, p.cnt[nm]) for nm in ("sVE", "sAE")]

        def slot(i):
            return p.pw0[:, 0:512] if i % 2 == 0 else p.pw0[:, 512:1024]

        def hsl(fi, tt, n):
            off = 0 if fi < 4 else -1
            return stile(fi, tt + off, n)

        def mm_qk(t, i):
            j, fo = divmod(i, 4)
            for fi in range(FT):
                t.matmul(slot(i), qkvtile(fi, 4 * j + fo),
                         hsl(fi, t0, WIN), start=(fi == 0), stop=(fi == FT - 1))

        def ev_qk(e, i):
            j, fo = divmod(i, 4)
            dst = qrs(w2, fo) if j == 0 else krs(w3, fo)
            return e.copy(dst, slot(i))

        if w == 0:
            @block.tensor
            def _(t):
                t.wait_ge(L["sWT"], wt_w)
        pipe(p, block, 8, mm_qk, ev_qk, "scalar", depth=2,
             pre_counts=pre_counts)

        def mm_v(t, tb):
            for fi in range(FT):
                off = 0 if fi < 4 else -1
                t.matmul(slot(tb), stile(fi, t0 + tb * 128 + off, 128),
                         wqkv_sb[:].rearrange("p (a bq) -> p a bq", a=FT)
                         [:, fi, 8 * 128:12 * 128],
                         start=(fi == 0), stop=(fi == FT - 1))

        def ev_v(e, tb):
            return e.copy(vts(w3, tb), slot(tb))

        pipe(p, block, 4, mm_v, ev_v, "scalar", depth=2, pre_wait=("sAE",))

        @block.gpsimd
        def _(g):
            p.w(g, "sAE")
            for i in range(8):
                j, fo = divmod(i, 4)
                srcb = qrs(w2, fo) if j == 0 else krs(w3, fo)
                g.tensor_mul(p.sq[:, i * 512:(i + 1) * 512], srcb, srcb)
            g.last.then_inc(L["sPO"], 1)
        p.bump("sPO")

    for w in range(NWE):
        t0 = w * WIN
        w2, w3 = w % 2, w % 3
        w3p = (w - 1) % 3
        first = (w == 0)

        @block.gpsimd
        def _(g, t0=t0):
            p.w(g, "sVE")
            for i, nm in enumerate(("csq", "ssq", "csk", "ssk")):
                g.dma_start(out=tbl[:, i * 512:(i + 1) * 512],
                            in_=din[nm][:, t0:t0 + WIN]).then_inc(L["sLG"], 16)
        p.bump("sLG", 64)
        tbl_w = p.cnt["sLG"]

        if w == 0:
            emit_proj(0)

        peb_r = p.cnt["sPE"]
        aeb_r = p.cnt["sAE"]
        veb_r = p.cnt["sVE"]
        pob_r = p.cnt["sPO"]

        @block.tensor
        def _(t, w2=w2, w3=w3):
            p.w(t, "sPO", pob_r)
            p.w(t, "sAE", aeb_r)
            p.w(t, "sVE", veb_r)
            for j in range(2):
                for ft in range(4):
                    ps = [p.pp0, p.pp1, p.pp2, p.pp3][ft]
                    t.matmul(ps[:, 0:512], p.c_blk64[:],
                             p.sq[:, (4 * j + ft) * 512:(4 * j + ft + 1) * 512],
                             start=True, stop=True)
                t.last.then_inc(L["sPE"], 1)
                for ft in range(4):
                    src = qrs(w2, ft) if j == 0 else krs(w3, ft)
                    half = (j * 4 + ft) % 2
                    if j * 4 + ft >= 2:
                        t.wait_ge(L["sVE"], veb_r + (j * 4 + ft) - 1)
                    t.matmul(p.pw1[:, half * 512:half * 512 + 512],
                             p.c_prot[:], src, start=True, stop=True)
                    t.last.then_inc(L["sPE"], 1)
        p.cnt["sPE"] += 2 * 5

        @block.scalar
        def _(sc):
            for j in range(2):
                sc.wait_ge(L["sPE"], peb_r + j * 5 + 1)
                for ft in range(4):
                    ps = [p.pp0, p.pp1, p.pp2, p.pp3][ft]
                    dst = [p.gs1, p.gs2, p.gs3, p.gs4][ft]
                    sc.activation(dst[:, 0:512], ps[:, 0:512], AF.Ln)
                    sc.activation(dst[:, 0:512], dst[:, 0:512], AF.Exp,
                                  scale=-0.5)
                sc.last.then_inc(L["sAE"], 1)
        p.cnt["sAE"] += 2

        @block.vector
        def _(v, w2=w2, w3=w3, tbl_w=tbl_w):
            v.wait_ge(L["sLG"], tbl_w)
            for j in range(2):
                spp = p.qs_pp if j == 0 else p.ks_pp
                srpp = p.qsr_pp if j == 0 else p.ksr_pp
                csoff = 0 if j == 0 else 1024
                ssoff = 512 if j == 0 else 1536
                v.wait_ge(L["sAE"], aeb_r + j + 1)
                for ft in range(4):
                    rs = [p.gs1, p.gs2, p.gs3, p.gs4][ft]
                    v.tensor_scalar_min(rs[:, 0:512], rs[:, 0:512], 1e12)
                    src = qrs(w2, ft) if j == 0 else krs(w3, ft)
                    dst = qws(w2, ft) if j == 0 else src
                    half = (j * 4 + ft) % 2
                    v.wait_ge(L["sPE"], peb_r + j * 5 + 1 + ft + 1)
                    v.scalar_tensor_tensor(p.gs5[:, 0:512], rs[:, 0:512],
                                           spp[:, layer:layer + 1],
                                           tbl[:, csoff:csoff + 512],
                                           ALU.mult, ALU.mult)
                    v.scalar_tensor_tensor(p.gs6[:, 0:512], rs[:, 0:512],
                                           srpp[:, layer:layer + 1],
                                           tbl[:, ssoff:ssoff + 512],
                                           ALU.mult, ALU.mult)
                    v.tensor_mul(p.gs7[:, 0:512], src, p.gs5[:, 0:512])
                    v.tensor_mul(p.gs5[:, 0:512],
                                 p.pw1[:, half * 512:half * 512 + 512],
                                 p.gs6[:, 0:512])
                    v.tensor_add(dst, p.gs7[:, 0:512],
                                 p.gs5[:, 0:512]).then_inc(L["sVE"], 1)
        p.cnt["sVE"] += 8

        if w + 1 < NWE:
            emit_proj(w + 1, pre_counts=[("sVE", veb_r), ("sAE", aeb_r)])

        if w > 0:
            @block.vector
            def _(v, t0=t0):
                p.w(v, "sPE")
                p.w(v, "sAE")
                for ft in range(FT):
                    v.tensor_add(stile(ft, t0 - WIN, WIN),
                                 stile(ft, t0 - WIN, WIN),
                                 wo_buf[:, ft * 512:(ft + 1) * 512])
                v.last.then_inc(L["sVE"], 1)
            p.bump("sVE")

        if w == 1 or (NWE == 1 and w == 0):
            for side, (srcn, dstn) in enumerate((("w1a", "w1a_bf"),
                                                 ("w1g", "w1g_bf"))):
                for piece in range(8):
                    @block.gpsimd
                    def _(g, srcn=srcn, piece=piece, side=side):
                        if piece == 0 and side == 0:
                            p.w(g, "sPE")
                            p.w(g, "sAE")
                            p.w(g, "sVE")
                        g.wait_ge(L["sSP"], p.cnt["sSP"])
                        g.dma_start(out=prT[:, 0:HID2],
                                    in_=din[srcn][layer].rearrange(
                                        "(a p) h -> p a h", p=128)[:, piece, :]
                                    ).then_inc(L["sLG"], 16)
                    p.bump("sLG", 16)

                    @block.sync
                    def _(sy, dstn=dstn, piece=piece):
                        p.w(sy, "sLG")
                        sy.dma_start(out=scr[dstn][layer].rearrange(
                            "(a p) h -> p a h", p=128)[:, piece, :],
                            in_=prT[:, 0:HID2]).then_inc(L["sSP"], 16)
                    p.bump("sSP", 16)
            p.w1cast_done = p.cnt["sSP"]

        heads = [] if DEV_SKIP_HEADS else list(range(H2))
        kbs = list(range(8)) if not first else list(range(4, 8))
        lenk = len(kbs)
        peb = p.cnt["sPE"]
        veb = p.cnt["sVE"]
        aeb = p.cnt["sAE"]
        pe_per = 4 + 1 + lenk + 1 + 1
        ae_per = lenk + 1

        def ve_cnt(h):
            full = 0
            for hh in range(h + 1):
                full += 5 + (1 if hh % 2 == 1 else 0)
            return veb + full

        if heads:
            @block.tensor
            def _(t, w2=w2, w3=w3, w3p=w3p, par_first=first, kbs=kbs,
                  lenk=lenk, peb=peb, veb=veb, aeb=aeb, pe_per=pe_per,
                  ae_per=ae_per, ve_cnt=ve_cnt, w=w):
                p.w(t, "sVE", veb)
                p.w(t, "sAE", aeb)
                if w == 0:
                    t.wait_ge(L["sLD"], msk_w)
                for h in range(H2):
                    ftq = h // 2
                    rb = (h % 2) * 64
                    hb = peb + h * pe_per
                    vh = ve_cnt(h - 1) if h > 0 else veb
                    for qt in range(4):
                        g = h * 4 + qt
                        if g >= 2:
                            h2, qt2 = divmod(g - 2, 4)
                            t.wait_ge(L["sVE"],
                                      (ve_cnt(h2 - 1) if h2 > 0 else veb)
                                      + qt2 + 1)
                        if qt % 2 == 1 and h > 0:
                            t.wait_ge(L["sAE"], aeb + h * ae_per)
                        if par_first:
                            ranges = ((512, w3),)
                        else:
                            ranges = ((0, w3p), (512, w3))
                        for koff, kp in ranges:
                            if qt % 2 == 0:
                                dst = p.pw0[:, koff:koff + 512]
                            elif par_first or koff == 512:
                                dst = p.pp3[:, 0:512]
                            else:
                                dst = p.pp2[:, 0:512]
                            t.matmul(dst, p.c_id_bf[:],
                                     p.c_mqk[:, qt * 1024 + koff:
                                             qt * 1024 + koff + 512],
                                     start=True, stop=False)
                            t.matmul(dst,
                                     qw[rb:rb + 64,
                                        ftq * 512 + qt * 128:
                                        ftq * 512 + (qt + 1) * 128],
                                     krs(kp, ftq)[rb:rb + 64, :],
                                     start=False, stop=True,
                                     tile_position=(rb, 0))
                        t.last.then_inc(L["sPE"], 1)
                    t.wait_ge(L["sVE"], vh + 4)
                    if h > 0:
                        t.wait_ge(L["sAE"], aeb + h * ae_per)
                    for qt in range(4):
                        t.matmul(p.pp2[0:1, qt * 128:(qt + 1) * 128],
                                 negmax[:, qt:qt + 1], p.c_id_f[:],
                                 start=(qt == 0), stop=(qt == 3))
                    t.last.then_inc(L["sPE"], 1)
                    for i, kb in enumerate(kbs):
                        gi = h * lenk + i
                        if gi >= 4:
                            t.wait_ge(L["sAE"], aeb + (gi - 4) + 1 +
                                      ((gi - 4) // lenk))
                        if i == 0:
                            t.wait_ge(L["sVE"], vh + 5)
                        ps = [p.pp0[:, 0:512], p.pp1[:, 0:512],
                              p.pw0[:, 0:512], p.pw0[:, 512:1024]][gi % 4]
                        t.matmul(ps, p.c_id_bf[:],
                                 p.c_mt[:, kb * 512:(kb + 1) * 512],
                                 start=True, stop=False)
                        t.matmul(ps, p.c_ones1[:], negrow[0:1, :],
                                 start=False, stop=False)
                        kp = w3p if kb < 4 else w3
                        ko = (kb % 4) * 128
                        t.matmul(ps,
                                 krs(kp, ftq)[rb:rb + 64, ko:ko + 128],
                                 qw[rb:rb + 64,
                                    ftq * 512:(ftq + 1) * 512],
                                 start=False, stop=True, tile_position=(rb, 0))
                        t.last.then_inc(L["sPE"], 1)
                    t.wait_ge(L["sAE"], aeb + h * ae_per + lenk)
                    for i, kb in enumerate(kbs):
                        t.matmul(p.pp2[:, 0:512], p.c_ones[:],
                                 prT[:, kb * 512:(kb + 1) * 512],
                                 start=(i == 0), stop=(i == lenk - 1))
                    t.last.then_inc(L["sPE"], 1)
                    if h % 2 == 0 and h > 0:
                        t.wait_ge(L["sVE"], ve_cnt(h - 1))
                    for i, kb in enumerate(kbs):
                        vp = w3p if kb < 4 else w3
                        vb = kb % 4
                        t.matmul(p.pw1[rb:rb + 64, 0:512],
                                 vts(vp, vb)[:, h * 64:(h + 1) * 64],
                                 prT[:, kb * 512:(kb + 1) * 512],
                                 start=(i == 0), stop=(i == lenk - 1),
                                 tile_position=(0, rb))
                    t.last.then_inc(L["sPE"], 1)
            p.cnt["sPE"] += H2 * pe_per

            @block.vector
            def _(v, par_first=first, lenk=lenk, peb=peb, pe_per=pe_per,
                  aeb=aeb, ae_per=ae_per):
                for h in range(H2):
                    hb = peb + h * pe_per
                    for qt in range(4):
                        v.wait_ge(L["sPE"], hb + qt + 1)
                        if qt % 2 == 0:
                            src_ = (p.pw0[:, 512:1024] if par_first else
                                    p.pw0[:, 0:1024])
                            v.reduce_max(negmax[:, qt:qt + 1], src_, axis=AX.X,
                                         negate=True).then_inc(L["sVE"], 1)
                        elif par_first:
                            v.reduce_max(negmax[:, qt:qt + 1], p.pp3[:, 0:512],
                                         axis=AX.X,
                                         negate=True).then_inc(L["sVE"], 1)
                        else:
                            v.reduce_max(negmax[:, 4:5], p.pp2[:, 0:512],
                                         axis=AX.X, negate=True)
                            v.reduce_max(negmax[:, 5:6], p.pp3[:, 0:512],
                                         axis=AX.X, negate=True)
                            v.tensor_tensor(negmax[:, qt:qt + 1],
                                            negmax[:, 4:5], negmax[:, 5:6],
                                            ALU.min).then_inc(L["sVE"], 1)
                    v.wait_ge(L["sPE"], hb + 5)
                    v.tensor_copy(negrow[0:1, :],
                                  p.pp2[0:1, 0:512]).then_inc(L["sVE"], 1)
                    if h % 2 == 1:
                        ftq = h // 2
                        v.wait_ge(L["sPE"], hb + pe_per)
                        v.wait_ge(L["sAE"], aeb + (h + 1) * ae_per)
                        v.tensor_mul(o_n[:, ftq * 512:(ftq + 1) * 512],
                                     p.pw1[:, 0:512],
                                     p.gs7[:, 0:512]).then_inc(L["sVE"], 1)
            p.cnt["sVE"] = ve_cnt(H2 - 1)

            @block.scalar
            def _(sc, w=w, kbs=kbs, lenk=lenk, peb=peb, pe_per=pe_per,
                  ve_cnt=ve_cnt):
                if w == 1:
                    sc.wait_ge(L["sSP"], p.w1cast_done)
                for h in range(H2):
                    hb = peb + h * pe_per
                    rb = (h % 2) * 64
                    for i, kb in enumerate(kbs):
                        gi = h * lenk + i
                        sc.wait_ge(L["sPE"], hb + 5 + i + 1)
                        srcp = [p.pp0[:, 0:512], p.pp1[:, 0:512],
                                p.pw0[:, 0:512], p.pw0[:, 512:1024]][gi % 4]
                        sc.activation(prT[:, kb * 512:(kb + 1) * 512],
                                      srcp, AF.Exp).then_inc(L["sAE"], 1)
                    sc.wait_ge(L["sPE"], hb + 5 + lenk + 1)
                    if h >= 2:
                        sc.wait_ge(L["sVE"], ve_cnt(h - 1))
                    sc.activation(p.gs7[rb:rb + 64, 0:512],
                                  p.pp2[rb:rb + 64, 0:512], AF.Ln)
                    sc.activation(p.gs7[rb:rb + 64, 0:512],
                                  p.gs7[rb:rb + 64, 0:512],
                                  AF.Exp, scale=-1.0).then_inc(L["sAE"], 1)
            p.cnt["sAE"] += H2 * ae_per

        def mm_wo2(t, fo):
            ps = p.pw0 if fo % 2 == 0 else p.pp0
            for fi in range(4):
                t.matmul(ps[:, 0:512], wotile(fi, fo),
                         o_n[:, fi * 512:(fi + 1) * 512],
                         start=(fi == 0), stop=(fi == 3))

        def ev_wo2(e, fo):
            ps = p.pw0 if fo % 2 == 0 else p.pp0
            return e.copy(wo_buf[:, fo * 512:(fo + 1) * 512], ps[:, 0:512])

        if not DEV_SKIP_HEADS:
            pipe(p, block, FT, mm_wo2, ev_wo2, "scalar", depth=2,
                 pre_wait=("sVE", "sAE"))

    if not DEV_SKIP_HEADS:
        @block.vector
        def _(v):
            p.w(v, "sAE")
            for ft in range(FT):
                v.tensor_add(stile(ft, (NWE - 1) * WIN, WIN),
                             stile(ft, (NWE - 1) * WIN, WIN),
                             wo_buf[:, ft * 512:(ft + 1) * 512])
            v.last.then_inc(L["sVE"], 1)
        p.bump("sVE")
    fence(p, block)
    es.close()


def emit_ffn_mlp(p, block, din, scr, layer, stile):
    nc = p.nc
    L = p.sems
    es = ExitStack()
    h2n = es.enter_context(nc.sbuf_tensor(f"h2n{layer}", [128, FT * TB], BF16))
    mbuf = es.enter_context(nc.sbuf_tensor(f"mbuf{layer}", [128, NS1 * TB], BF16))
    w2res = es.enter_context(nc.sbuf_tensor(f"w2res{layer}", [128, NS1 * FT * 128],
                                            BF16))
    w2buf = es.enter_context(nc.sbuf_tensor(f"w2buf{layer}", [128, FT * TB], BF16))
    p.stat_mu = es.enter_context(nc.sbuf_tensor(f"smu{layer}", [128, 2 * TB], F32))
    p.stat_rs = es.enter_context(nc.sbuf_tensor(f"srs{layer}", [128, 2 * TB], F32))
    slab = es.enter_context(nc.sbuf_tensor(f"slab{layer}", [128, 2 * 2 * FT * 128],
                                           BF16))
    gelb = es.enter_context(nc.sbuf_tensor(f"gelb{layer}", [128, 2 * TB], BF16))

    @block.gpsimd
    def _(g):
        p.w(g, "sPE")
        p.w(g, "sVE")
        p.w(g, "sAE")
        g.dma_start(out=w2res[:].rearrange("p (s f q) -> p s f q", s=NS1, f=FT),
                    in_=din["w2p"][layer].rearrange("(s p) (f q) -> p s f q",
                                                    p=128, q=128)
                    ).then_inc(L["sWT"], 16)
    p.bump("sWT", 16)
    w2_w = p.cnt["sWT"]

    def w2tile(s, fo):
        return w2res[:].rearrange("p (s f q) -> p s f q", s=NS1,
                                  f=FT)[:, s, fo, :]

    def slab_tile(par, side, fi):
        return slab[:].rearrange("p (r d f q) -> p r d f q", r=2, d=2,
                                 f=FT)[:, par, side, fi, :]

    def load_slab(sy, s, par):
        for side, src in ((0, "w1a_bf"), (1, "w1g_bf")):
            sy.dma_start(
                out=slab[:].rearrange("p (r d f q) -> p r d f q", r=2, d=2,
                                      f=FT)[:, par, side, :, :],
                in_=scr[src][layer].rearrange("(f p) (s q) -> p f s q",
                                              p=128, q=128)[:, :, s, :]
                ).then_inc(L["sSP"], 16)
            p.bump("sSP", 16)

    mu2 = [p.stat_mu[:, 0:TB], p.stat_mu[:, TB:2 * TB]]
    rs2 = [p.stat_rs[:, 0:TB], p.stat_rs[:, TB:2 * TB]]

    def rhs_sh_tb(tb):
        def f(ft, t0=tb * TB):
            off = 0 if ft < 4 else -1
            return stile(ft, t0 + off, TB)
        return f

    for tb in range(NTB):
        t0 = tb * TB
        first_tb = (tb == 0)
        sslot = tb % 2

        if tb == 0:
            ln_stats_sb(p, block, rhs_sh_tb(0), TB, mu2[0], rs2[0],
                        p.pw1[:, 0:512], p.pw1[:, 512:1024], sq_off=0)

        @block.vector
        def _(v, t0=t0, sslot=sslot, h2c=h2n):
            p.w(v, "sAE")
            for ft in range(FT):
                off = 0 if ft < 4 else -1
                v.tensor_sub(p.gs3[:, 0:TB], stile(ft, t0 + off, TB),
                             mu2[sslot])
                v.scalar_tensor_tensor(h2c[:, ft * TB:(ft + 1) * TB],
                                       p.gs3[:, 0:TB],
                                       p.lng_pp[:, layer * FT + ft:
                                                layer * FT + ft + 1],
                                       rs2[sslot], ALU.mult, ALU.mult)
            v.last.then_inc(L["sVE"], 1)
        p.bump("sVE")

        if tb > 0:
            @block.vector
            def _(v, t0=t0):
                for ft in range(FT):
                    v.tensor_add(stile(ft, t0 - TB, TB),
                                 stile(ft, t0 - TB, TB),
                                 w2buf[:, ft * TB:(ft + 1) * TB])
                v.last.then_inc(L["sVE"], 1)
            p.bump("sVE")

        sp_targets = []

        @block.sync
        def _(sy, tb=tb):
            sy.wait_ge(L["sSP"], p.w1cast_done)
            p.w(sy, "sPE")
            p.w(sy, "sVE")
            p.w(sy, "sAE")
            load_slab(sy, 0, 0)
        sp_targets.append(p.cnt["sSP"])

        pe0 = p.cnt["sPE"]
        ae0 = p.cnt["sAE"]
        ve0 = p.cnt["sVE"]
        if first_tb:
            h2c = h2n
            for s in range(NS1):
                par = s % 2
                aps = p.pp0 if par == 0 else p.pp2
                gps = p.pp1 if par == 0 else p.pw0
                gel = gelb[:, par * TB:(par + 1) * TB]

                if s + 1 < NS1:
                    @block.sync
                    def _(sy, s=s, par=par):
                        p.w(sy, "sPE")
                        load_slab(sy, s + 1, par ^ 1)
                    sp_targets.append(p.cnt["sSP"])

                @block.tensor
                def _(t, s=s, par=par, aps=aps, gps=gps, h2c=h2c):
                    t.wait_ge(L["sSP"], sp_targets[s])
                    if s == 0:
                        t.wait_ge(L["sWT"], w2_w)
                    p.w(t, "sVE")
                    p.w(t, "sAE")
                    for fi in range(FT):
                        t.matmul(aps[:, 0:TB], slab_tile(par, 0, fi),
                                 h2c[:, fi * TB:(fi + 1) * TB],
                                 start=(fi == 0), stop=(fi == FT - 1))
                    for fi in range(FT):
                        t.matmul(gps[:, 0:TB], slab_tile(par, 1, fi),
                                 h2c[:, fi * TB:(fi + 1) * TB],
                                 start=(fi == 0), stop=(fi == FT - 1))
                    for side, ps in ((0, p.pp1 if par else p.pp2),
                                     (1, p.pw1)):
                        for fi in range(FT):
                            t.matmul(ps[0:1, 0:128],
                                     p.lnb_bf[:, layer * FT + fi:
                                              layer * FT + fi + 1],
                                     slab_tile(par, side, fi),
                                     start=(fi == 0), stop=(fi == FT - 1))
                    t.last.then_inc(L["sPE"], 1)
                p.bump("sPE")

                @block.vector
                def _(v, par=par):
                    p.w(v, "sPE")
                    v.tensor_copy(p.rowtmp[0:1, :],
                                  (p.pp1 if par else p.pp2)[0:1, 0:128])
                    v.last.then_inc(L["sVE"], 1)
                p.bump("sVE")

                @block.tensor
                def _(t):
                    p.w(t, "sVE")
                    t.matmul(p.pw1[:, 512:513], p.rowtmp[0:1, :],
                             p.c_ones1[0:1, 0:1], start=True, stop=True)
                    t.last.then_inc(L["sPE"], 1)
                p.bump("sPE")

                @block.vector
                def _(v, s=s):
                    p.w(v, "sPE")
                    v.tensor_add(p.b1ae[:, s:s + 1], p.pw1[:, 512:513],
                                 p.b1ac[:, layer * NS1 + s:layer * NS1 + s + 1])
                    v.tensor_copy(p.rowtmp[0:1, :], p.pw1[0:1, 0:128])
                    v.last.then_inc(L["sVE"], 1)
                p.bump("sVE")

                @block.tensor
                def _(t):
                    p.w(t, "sVE")
                    t.matmul(p.pw1[:, 512:513], p.rowtmp[0:1, :],
                             p.c_ones1[0:1, 0:1], start=True, stop=True)
                    t.last.then_inc(L["sPE"], 1)
                p.bump("sPE")

                @block.vector
                def _(v, s=s):
                    p.w(v, "sPE")
                    v.tensor_add(p.b1ge[:, s:s + 1], p.pw1[:, 512:513],
                                 p.b1gc[:, layer * NS1 + s:layer * NS1 + s + 1])
                    v.last.then_inc(L["sVE"], 1)
                p.bump("sVE")

                @block.scalar
                def _(sc, s=s, gps=gps, gel=gel):
                    p.w(sc, "sPE")
                    p.w(sc, "sVE")
                    sc.activation(gel, gps[:, 0:TB], AF.Gelu,
                                  bias=p.b1ge[:, s:s + 1])
                    sc.last.then_inc(L["sAE"], 1)
                p.bump("sAE")

                @block.vector
                def _(v, s=s, aps=aps, gel=gel):
                    p.w(v, "sAE")
                    v.scalar_tensor_tensor(mbuf[:, s * TB:(s + 1) * TB],
                                           aps[:, 0:TB], p.b1ae[:, s:s + 1],
                                           gel, ALU.add, ALU.mult)
                    v.last.then_inc(L["sVE"], 1)
                p.bump("sVE")
        else:
            @block.sync
            def _(sy):
                sy.wait_ge(L["sSP"], p.w1cast_done)
                for s in range(NS1 - 1):
                    sy.wait_ge(L["sPE"], pe0 + s)
                    load_slab(sy, s + 1, (s + 1) % 2)
            for s in range(NS1 - 1):
                sp_targets.append(sp_targets[0] + 32 * (s + 1))

            h2c = h2n

            @block.tensor
            def _(t, h2c=h2c):
                for s in range(NS1):
                    par = s % 2
                    aps = p.pp0 if par == 0 else p.pp2
                    gps = p.pp1 if par == 0 else p.pw0
                    t.wait_ge(L["sSP"], sp_targets[s])
                    if s == 0:
                        t.wait_ge(L["sWT"], w2_w)
                        p.w(t, "sVE", ve0)
                        p.w(t, "sAE", ae0)
                    else:
                        t.wait_ge(L["sAE"], ae0 + max(0, s - 1))
                        t.wait_ge(L["sVE"], ve0 + max(0, s - 1))
                    for fi in range(FT):
                        t.matmul(aps[:, 0:TB], slab_tile(par, 0, fi),
                                 h2c[:, fi * TB:(fi + 1) * TB],
                                 start=(fi == 0), stop=(fi == FT - 1))
                    for fi in range(FT):
                        t.matmul(gps[:, 0:TB], slab_tile(par, 1, fi),
                                 h2c[:, fi * TB:(fi + 1) * TB],
                                 start=(fi == 0), stop=(fi == FT - 1))
                    t.last.then_inc(L["sPE"], 1)
            p.cnt["sPE"] += NS1

            @block.scalar
            def _(sc):
                for s in range(NS1):
                    gps = p.pp1 if s % 2 == 0 else p.pw0
                    gel = gelb[:, (s % 2) * TB:(s % 2 + 1) * TB]
                    sc.wait_ge(L["sPE"], pe0 + s + 1)
                    sc.activation(gel, gps[:, 0:TB], AF.Gelu,
                                  bias=p.b1ge[:, s:s + 1]).then_inc(L["sAE"], 1)
            p.cnt["sAE"] += NS1

            @block.vector
            def _(v):
                for s in range(NS1):
                    aps = p.pp0 if s % 2 == 0 else p.pp2
                    gel = gelb[:, (s % 2) * TB:(s % 2 + 1) * TB]
                    v.wait_ge(L["sAE"], ae0 + s + 1)
                    v.scalar_tensor_tensor(mbuf[:, s * TB:(s + 1) * TB],
                                           aps[:, 0:TB], p.b1ae[:, s:s + 1],
                                           gel, ALU.add,
                                           ALU.mult).then_inc(L["sVE"], 1)
            p.cnt["sVE"] += NS1

        if tb + 1 < NTB:
            ln_stats_sb(p, block, rhs_sh_tb(tb + 1), TB,
                        mu2[(tb + 1) % 2], rs2[(tb + 1) % 2],
                        p.pw1[:, 0:512], p.pw1[:, 512:1024], sq_off=0)

        def mm_w2(t, fo):
            ps = p.pp0 if fo % 2 == 0 else p.pp1
            for s in range(NS1):
                t.matmul(ps[:, 0:TB], w2tile(s, fo),
                         mbuf[:, s * TB:(s + 1) * TB],
                         start=(s == 0), stop=(s == NS1 - 1))

        def ev_w2(e, fo):
            ps = p.pp0 if fo % 2 == 0 else p.pp1
            return e.tensor_scalar_add(w2buf[:, fo * TB:(fo + 1) * TB],
                                       ps[:, 0:TB],
                                       p.b2_pp[:, layer * FT + fo:
                                               layer * FT + fo + 1])

        pipe(p, block, FT, mm_w2, ev_w2, "vector", depth=2, pre_wait=("sVE",))

        fence(p, block)

    @block.vector
    def _(v):
        for ft in range(FT):
            v.tensor_add(stile(ft, (NTB - 1) * TB, TB),
                         stile(ft, (NTB - 1) * TB, TB),
                         w2buf[:, ft * TB:(ft + 1) * TB])
        v.last.then_inc(L["sVE"], 1)
    p.bump("sVE")
    fence(p, block)
    es.close()


def emit_final(p, block, scr, out_d, stile):
    nc = p.nc
    L = p.sems
    es = ExitStack()
    x1b = es.enter_context(nc.sbuf_tensor("x1b", [128, FT * 512], BF16))
    outf = es.enter_context(nc.sbuf_tensor("outf", [128, FT * 512], F32))

    for s in range(8):
        t0 = s * 512

        @block.sync
        def _(sy, s=s):
            p.w(sy, "sVE")
            p.w(sy, "sPE")
            p.w(sy, "sAE")
            sy.dma_start(out=x1b[:].rearrange("p (ft t) -> p ft t", ft=FT),
                         in_=scr["x1_s"][:, :, s * 512:(s + 1) * 512]
                         ).then_inc(L["sLD"], 16)
        p.bump("sLD", 16)

        ln_stats(p, block, lambda ft, t0=t0: stile(ft, t0, 512), 512,
                 p.gs2[:, 0:512])

        @block.vector
        def _(v, t0=t0):
            p.w(v, "sAE")
            p.w(v, "sLD")
            v.tensor_mul(p.gs1[:, 0:512], p.pp0[:, 0:512], p.gs2[:, 0:512])
            for ft in range(FT):
                v.tensor_mul(p.gs3[:, 0:512], stile(ft, t0, 512),
                             p.gs2[:, 0:512])
                v.tensor_sub(p.gs3[:, 0:512], p.gs3[:, 0:512], p.gs1[:, 0:512])
                v.tensor_add(outf[:, ft * 512:(ft + 1) * 512], p.gs3[:, 0:512],
                             x1b[:, ft * 512:(ft + 1) * 512])
            v.last.then_inc(L["sVE"], 1)
        p.bump("sVE")

        @block.sync
        def _(sy, s=s):
            p.w(sy, "sVE")
            sy.dma_start(out=out_d.rearrange("(ft p) t -> p ft t", p=128)
                         [:, :, s * 512:(s + 1) * 512],
                         in_=outf[:].rearrange("p (ft t) -> p ft t", ft=FT)
                         ).then_inc(L["sST"], 16)
        p.bump("sST", 16)
        fence(p, block)

    @block.sync
    def _(sy):
        p.w(sy, "sST")
    es.close()


def kernel(**inputs):
    consts = _host_consts()
    w = _prep_weights(inputs)
    x = inputs["L0_attn_x"]
    p = build_program()
    in_maps = []
    for core in range(8):
        m = {"x_in": np.ascontiguousarray(x[core].T)}
        m.update(w)
        m.update(consts)
        dm = np.zeros((128, 64), np.float32)
        dm[:, :core + 1] = 1.0
        m["dmask"] = dm
        in_maps.append(m)
    res = run_bass_kernel_spmd(p.nc, in_maps, core_ids=list(range(8)))
    out = np.stack([np.ascontiguousarray(r["out_d"].T) for r in res.results])
    return out
